# version 03 (final)
# speedup vs baseline: 1.2680x; 1.2680x over previous
import math
from contextlib import ExitStack
from functools import partial

import numpy as np
import ml_dtypes

import concourse.bass as bass
import concourse.mybir as mybir
from concourse.bass_utils import run_bass_kernel_spmd

F32 = mybir.dt.float32
BF16 = mybir.dt.bfloat16
AF = mybir.ActivationFunctionType
ALU = mybir.AluOpType
AX = mybir.AxisListType
BF = ml_dtypes.bfloat16

D = 2048
TC = 2048
NCH = TC // 128
B, S = 4, 4096
NCORES = 8

ENGINES = ("sync", "scalar", "vector", "tensor", "gpsimd")


class Ctr:
    __slots__ = ("h", "n", "name")

    def __init__(self, h, name):
        self.h = h
        self.n = 0
        self.name = name


class Prog:

    def __init__(self, nc, ctx):
        self.nc = nc
        self.ctx = ctx
        self.ops = {e: [] for e in ENGINES}
        self._nsem = 0
        self.auto = {}
        self.all_sems = []

    def sem(self, name=None):
        name = f"{name or 's'}_{self._nsem}"
        self._nsem += 1
        h = self.ctx.enter_context(self.nc.semaphore(name))
        c = Ctr(h, name)
        self.all_sems.append(c)
        return c

    def op(self, eng, fn, waits=(), inc=None, inc_amt=1):
        waits = list(waits)
        auto = self.auto.get(eng)
        if auto is not None and inc is auto and auto.n > 0:
            waits.append((auto, auto.n))
        post = None
        if inc is not None:
            inc.n += inc_amt
            post = inc.n
        self.ops[eng].append((tuple(waits), fn, inc, inc_amt))
        return post

    def dma(self, eng, out, in_, waits=(), inc=None, **kw):
        return self.op(eng, lambda e: e.dma_start(out=out, in_=in_, **kw),
                       waits=waits, inc=inc, inc_amt=16)

    def emit(self):
        nc = self.nc
        with nc.Block() as block:
            def replay(eng_name):
                def run(eng):
                    for waits, fn, inc, amt in self.ops[eng_name]:
                        for c, v in waits:
                            eng.wait_ge(c.h, v)
                        inst = fn(eng)
                        if inc is not None:
                            inst.then_inc(inc.h, amt)
                return run
            block.sync(replay("sync"))
            block.scalar(replay("scalar"))
            block.vector(replay("vector"))
            block.tensor(replay("tensor"))
            block.gpsimd(replay("gpsimd"))

    def n_ops(self):
        return {e: len(v) for e, v in self.ops.items()}


def _consts_np():
    ident = np.eye(128, dtype=BF)
    ones1 = np.ones((1, 128), dtype=BF)
    ones_tile = np.ones((128, 128), dtype=BF)
    ut01 = np.triu(np.ones((128, 128), dtype=np.float32)).T.copy()
    ut01 = np.triu(np.ones((128, 128), dtype=np.float32))
    constf = np.zeros((128, 8), dtype=np.float32)
    constf[:, 0] = 1e-5
    constf[:, 1] = 1.0
    return dict(ident=ident, ones1=ones1, ones_tile=ones_tile, ut01=ut01.astype(BF),
                constf=constf)



def build_phase_a():
    nc = bass.Bass("TRN2", target_bir_lowering=False)
    ctx = ExitStack()
    p = Prog(nc, ctx)

    x = nc.dram_tensor("x", [TC, D], F32, kind="ExternalInput")
    wq = nc.dram_tensor("wq", [D, 1024], F32, kind="ExternalInput")
    wk = nc.dram_tensor("wk", [D, 1024], F32, kind="ExternalInput")
    wkg1 = nc.dram_tensor("wkg1", [D, 16], F32, kind="ExternalInput")
    wkg2 = nc.dram_tensor("wkg2", [16, 1024], F32, kind="ExternalInput")
    bkg2c = nc.dram_tensor("bkg2c", [128, 8], F32, kind="ExternalInput")
    wv = nc.dram_tensor("wv", [D, D], F32, kind="ExternalInput")
    wgg = nc.dram_tensor("wgg", [D, D], F32, kind="ExternalInput")
    bg_row = nc.dram_tensor("bg_row", [1, D], F32, kind="ExternalInput")
    gate0 = nc.dram_tensor("gate0", [D, 16], F32, kind="ExternalInput")
    swg = nc.dram_tensor("swg", [D, D], F32, kind="ExternalInput")
    swu = nc.dram_tensor("swu", [D, D], F32, kind="ExternalInput")
    swd = nc.dram_tensor("swd", [D, D], F32, kind="ExternalInput")
    ewg = nc.dram_tensor("ewg", [16, D, 1024], F32, kind="ExternalInput")
    ewu = nc.dram_tensor("ewu", [16, D, 1024], F32, kind="ExternalInput")
    ewd = nc.dram_tensor("ewd", [16, 1024, D], F32, kind="ExternalInput")
    ident_d = nc.dram_tensor("ident", [128, 128], BF16, kind="ExternalInput")
    ones1_d = nc.dram_tensor("ones1", [1, 128], BF16, kind="ExternalInput")
    onest_d = nc.dram_tensor("ones_tile", [128, 128], BF16, kind="ExternalInput")
    constf_d = nc.dram_tensor("constf", [128, 8], F32, kind="ExternalInput")

    qT_o = nc.dram_tensor("qT", [1024, TC], BF16, kind="ExternalOutput")
    kT_o = nc.dram_tensor("kT", [1024, TC], BF16, kind="ExternalOutput")
    gkT_o = nc.dram_tensor("gkT", [1024, TC], F32, kind="ExternalOutput")
    v_o = nc.dram_tensor("v_tok", [TC, D], BF16, kind="ExternalOutput")
    g_o = nc.dram_tensor("g_tok", [TC, D], BF16, kind="ExternalOutput")
    f0_o = nc.dram_tensor("f0", [TC, D], F32, kind="ExternalOutput")
    xnT_d = nc.dram_tensor("xnT", [D, TC], BF16, kind="ExternalOutput")
    dbg_o = nc.dram_tensor("dbg", [128, 512], F32, kind="ExternalOutput")
    dbg2_o = nc.dram_tensor("dbg2", [128, 48], F32, kind="ExternalOutput")

    with ctx:
        e = ctx.enter_context
        ident = e(nc.sbuf_tensor("identsb", [128, 128], BF16))
        ones1 = e(nc.sbuf_tensor("ones1sb", [1, 128], BF16))
        onest = e(nc.sbuf_tensor("onestsb", [128, 128], BF16))
        bkg2_sb = e(nc.sbuf_tensor("bkg2sb", [128, 8], F32))
        bg_sb = e(nc.sbuf_tensor("bgsb", [1, D], BF16))
        wkg1_sb = e(nc.sbuf_tensor("wkg1sb", [128, 16 * 16], BF16))
        wkg2_sb = e(nc.sbuf_tensor("wkg2sb", [16, 1024], BF16))
        gate0_sb = e(nc.sbuf_tensor("gate0sb", [128, 16 * 16], BF16))
        constf = e(nc.sbuf_tensor("constfsb", [128, 8], F32))

        stat = e(nc.sbuf_tensor("stat", [128, 16], F32))

        big = e(nc.sbuf_tensor("bigscr", [128, 16384], BF16))
        xc = [big[:, 0:4096].bitcast(F32), big[:, 4096:8192].bitcast(F32)]
        xn_sb = [big[:, 8192:10240], big[:, 10240:12288]]
        xnT_st = [big[:, 12288:14336], big[:, 14336:16384]]
        m_sb = big

        xnw = e(nc.sbuf_tensor("xnw", [128, 16 * 1024], BF16))
        wbuf = [e(nc.sbuf_tensor(f"wbuf{i}", [128, 4096], BF16)) for i in range(4)]
        facc = e(nc.sbuf_tensor("facc", [128, 8 * D], F32))
        comb_sb = e(nc.sbuf_tensor("comb", [128, 8 * 16], F32))
        rw_sb = e(nc.sbuf_tensor("rw", [128, 16], F32))
        t8_sb = e(nc.sbuf_tensor("t8", [128, 8], F32))
        mask_sb = e(nc.sbuf_tensor("maskw", [128, 16], F32))
        t1_sb = e(nc.sbuf_tensor("t1", [16, 1024], BF16))
        silu_sb = [e(nc.sbuf_tensor(f"silu{i}", [128, 512], BF16)) for i in range(2)]
        bsc_sb = e(nc.sbuf_tensor("bsc", [128, 128], BF16))
        be_sb = e(nc.sbuf_tensor("be", [128, 8 * 128], BF16))
        stg = [e(nc.sbuf_tensor(f"stg{i}", [128, 512], BF16)) for i in range(4)]
        stgf = [e(nc.sbuf_tensor(f"stgf{i}", [128, 512], F32)) for i in range(2)]
        scr = [e(nc.sbuf_tensor(f"scr{i}", [128, 512], F32)) for i in range(3)]

        pb = [e(nc.psum_tensor(f"pb{i}", [128, 512], F32)) for i in range(8)]

        s_const = p.sem("const")
        s_constg = p.sem("constg")
        s_xs = [p.sem(f"x{i}") for i in range(2)]
        s_wb = [p.sem(f"wb{i}") for i in range(4)]
        s_xw = p.sem("xw")
        s_mm = p.sem("mm")
        s_v = p.sem("v")
        s_sc = p.sem("sc")
        s_g = p.sem("g")
        s_stgd = [p.sem(f"stgd{i}") for i in range(4)]
        s_stgfd = [p.sem(f"stgfd{i}") for i in range(2)]
        s_xnt = [p.sem(f"xnt{i}") for i in range(2)]
        s_f0 = p.sem("f0")
        p.auto = {"vector": s_v, "scalar": s_sc, "gpsimd": s_g}

        bank_writer = [0] * 8
        bank_reader = [0] * 8
        bank_rsem = [None] * 8

        def mm_group(bank_i, mms, waits_first=()):
            w = list(waits_first)
            if bank_rsem[bank_i] is not None:
                w.append((bank_rsem[bank_i], bank_reader[bank_i]))
            for i, fn in enumerate(mms):
                inc = s_mm if i == len(mms) - 1 else None
                p.op("tensor", fn, waits=w if i == 0 else (), inc=inc)
            bank_writer[bank_i] = s_mm.n
            return s_mm.n

        def bank_consumed(bank_i, sem, val):
            bank_rsem[bank_i] = sem
            bank_reader[bank_i] = val

        stg_tok = [ (None, 0) ] * 4
        stgf_tok = [ (None, 0) ] * 4

        def use_stg(pool_tok, idx):
            semval = pool_tok[idx]
            return [] if semval[0] is None else [semval]

        p.dma("sync", ident[:], ident_d[:, :], inc=s_const)
        p.dma("sync", ones1[:], ones1_d[:, :], inc=s_const)
        p.dma("sync", onest[:], onest_d[:, :], inc=s_const)
        p.dma("sync", bkg2_sb[:], bkg2c[:, :], inc=s_const)
        p.dma("sync", constf[:], constf_d[:, :], inc=s_const)
        p.dma("gpsimd", bg_sb[:], bg_row[:, :], inc=s_constg)
        p.dma("gpsimd", wkg1_sb[:].rearrange("p (k j) -> p k j", k=16),
              wkg1[:, :].rearrange("(k p) j -> p k j", p=128), inc=s_constg)
        p.dma("gpsimd", wkg2_sb[:], wkg2[:, :], inc=s_constg)
        p.dma("gpsimd", gate0_sb[:].rearrange("p (k j) -> p k j", k=16),
              gate0[:, :].rearrange("(k p) j -> p k j", p=128), inc=s_constg)
        CONSTS = s_const.n
        CONSTG = s_constg.n

        ln_ch_done = []
        tr_last = [0, 0]
        xnT_dma_tok = [0, 0]
        sq_read = 0
        for c in range(NCH):
            sl = c % 2
            xbuf = xc[sl]
            w = []
            if c >= 2:
                w = [(s_v, ln_ch_done[c - 2][0]), (s_sc, ln_ch_done[c - 2][1])]
            t_x = p.dma("sync", xbuf[:], x[c * 128:(c + 1) * 128, :], waits=w, inc=s_xs[sl])
            p.op("scalar", partial(lambda s, xb, xnb_: s.activation(xnb_, xb[:], AF.Square,
                                                                    accum_out=stat[:, 1:2]),
                                   xb=xbuf, xnb_=xn_sb[sl]),
                 waits=[(s_xs[sl], t_x), (s_v, sq_read), (s_mm, tr_last[sl])], inc=s_sc)
            sq_done = s_sc.n
            p.op("vector", partial(lambda v, xb: v.reduce_sum(stat[:, 0:1], xb[:], AX.X), xb=xbuf),
                 waits=[(s_xs[sl], t_x)], inc=s_v)
            p.op("vector", lambda v: v.drain(), inc=s_v)
            sx_done = s_v.n
            p.op("scalar", lambda s: s.mul(stat[:, 2:3], stat[:, 0:1], 1.0 / D),
                 waits=[(s_v, sx_done)], inc=s_sc)
            mu_done = s_sc.n
            p.op("vector", lambda v: v.tensor_scalar_mul(stat[:, 3:4], stat[:, 1:2], 1.0 / D),
                 waits=[(s_sc, sq_done)], inc=s_v)
            sq_read = s_v.n
            p.op("vector", lambda v: v.tensor_tensor(stat[:, 4:5], stat[:, 2:3], stat[:, 2:3], ALU.mult),
                 waits=[(s_sc, mu_done)], inc=s_v)
            p.op("vector", lambda v: v.tensor_tensor(stat[:, 5:6], stat[:, 3:4], stat[:, 4:5], ALU.subtract), inc=s_v)
            p.op("vector", lambda v: v.drain(), inc=s_v)
            var_done = s_v.n
            p.op("scalar", lambda s: s.activation(stat[:, 6:7], stat[:, 5:6], AF.Ln, bias=constf[:, 0:1]),
                 waits=[(s_v, var_done), (s_const, CONSTS)], inc=s_sc)
            p.op("scalar", lambda s: s.activation(stat[:, 7:8], stat[:, 6:7], AF.Exp, scale=-0.5), inc=s_sc)
            rstd_done = s_sc.n
            xnb = xn_sb[sl]
            p.op("vector", partial(lambda v, xb, xnb_: v.tensor_scalar(
                    xnb_[:], xb[:], stat[:, 2:3], stat[:, 7:8], ALU.subtract, ALU.mult),
                    xb=xbuf, xnb_=xnb),
                 waits=[(s_sc, rstd_done), (s_mm, tr_last[sl])], inc=s_v)
            p.op("vector", lambda v: v.drain(), inc=s_v)
            xn_done = s_v.n
            ln_ch_done.append((xn_done, sq_done))
            stbuf = xnT_st[sl]
            for k in range(16):
                bi = k % 2
                w0 = [(s_const, CONSTS), (s_constg, CONSTG), (s_v, xn_done)]
                tok = mm_group(bi, [partial(
                    lambda t, k_, bi_, xnb_: t.transpose(
                        pb[bi_][:, 0:64].bitcast(BF16), xnb_[:, k_ * 128:(k_ + 1) * 128], ident[:]),
                    k_=k, bi_=bi, xnb_=xnb)], waits_first=w0)
                eng = "vector" if k % 2 == 0 else "scalar"
                sem = s_v if eng == "vector" else s_sc
                fn = (partial(lambda v, k_, bi_, sb: v.tensor_copy(
                        sb[:, k_ * 128:(k_ + 1) * 128], pb[bi_][:, 0:64].bitcast(BF16)), k_=k, bi_=bi, sb=stbuf)
                      if eng == "vector" else
                      partial(lambda s, k_, bi_, sb: s.copy(
                        sb[:, k_ * 128:(k_ + 1) * 128], pb[bi_][:, 0:64].bitcast(BF16)), k_=k, bi_=bi, sb=stbuf))
                wv_ = [(s_mm, tok)]
                if k < 2:
                    wv_.append((s_xnt[sl], xnT_dma_tok[sl]))
                ev = p.op(eng, fn, waits=wv_, inc=sem)
                bank_consumed(bi, sem, ev)
            tr_last[sl] = s_mm.n
            xnT_dma_tok[sl] = p.dma(
                "sync", xnT_d[:, :].rearrange("(k p) t -> p k t", p=128)[:, :, c * 128:(c + 1) * 128],
                stbuf[:].rearrange("p (k j) -> p k j", k=16),
                waits=[(s_v, s_v.n), (s_sc, s_sc.n)], inc=s_xnt[sl])

        f0_dma_tok = 0
        for tsl in range(2):
            t0 = tsl * 1024
            t_xw = p.dma("sync", xnw[:].rearrange("p (k t) -> p k t", k=16),
                         xnT_d[:, t0:t0 + 1024].rearrange("(k p) t -> p k t", p=128),
                         waits=[(s_mm, s_mm.n), (s_xnt[0], s_xnt[0].n), (s_xnt[1], s_xnt[1].n)],
                         inc=s_xw)

            comb_read = 0
            dbg_guard = s_f0.n
            for tc in range(8):
                bi = tc % 2
                mms = []
                for k in range(16):
                    mms.append(partial(
                        lambda t, k_, tc_, bi_: t.matmul(
                            pb[bi_][:, 0:16], xnw[:, k_ * 1024 + tc_ * 128: k_ * 1024 + (tc_ + 1) * 128],
                            gate0_sb[:, k_ * 16:(k_ + 1) * 16], start=(k_ == 0), stop=(k_ == 15)),
                        k_=k, tc_=tc, bi_=bi))
                tok = mm_group(bi, mms, waits_first=[(s_xw, t_xw), (s_const, CONSTS), (s_constg, CONSTG)])
                p.op("scalar", partial(lambda s, bi_: s.activation(rw_sb[:], pb[bi_][:, 0:16], AF.Sigmoid), bi_=bi),
                     waits=[(s_mm, tok), (s_v, comb_read), (s_f0, dbg_guard)], inc=s_sc)
                sig_done = s_sc.n
                bank_consumed(bi, s_sc, sig_done)
                p.op("vector", lambda v: v.max(t8_sb[:], rw_sb[:]), waits=[(s_sc, sig_done)], inc=s_v)
                p.op("vector", lambda v: v.drain(), inc=s_v)
                max_done = s_v.n
                p.op("scalar", lambda s: s.activation(stat[:, 0:8], t8_sb[:], AF.Copy,
                                                      accum_out=stat[:, 8:9]),
                     waits=[(s_v, max_done)], inc=s_sc)
                sum_done = s_sc.n
                p.op("scalar", lambda s: s.copy(stat[:, 12:13], t8_sb[:, 7:8]), inc=s_sc)
                thr_done = s_sc.n
                p.op("vector", lambda v: v.tensor_scalar(stat[:, 9:10], stat[:, 8:9], 1e-6, 0.4,
                                                         ALU.add, ALU.mult),
                     waits=[(s_sc, sum_done)], inc=s_v)
                p.op("vector", lambda v: v.reciprocal(stat[:, 10:11], stat[:, 9:10]), inc=s_v)
                p.op("vector", lambda v: v.drain(), inc=s_v)
                recip_done = s_v.n
                p.op("scalar", lambda s: s.copy(stat[:, 13:14], stat[:, 10:11]),
                     waits=[(s_v, recip_done)], inc=s_sc)
                sc2_done = s_sc.n
                p.op("vector", lambda v: v.tensor_scalar(mask_sb[:], rw_sb[:], stat[:, 12:13], None, ALU.is_ge),
                     waits=[(s_sc, thr_done)], inc=s_v)
                p.op("vector", partial(lambda v, tc_: v.scalar_tensor_tensor(
                        comb_sb[:, tc_ * 16:(tc_ + 1) * 16], rw_sb[:], stat[:, 13:14], mask_sb[:],
                        ALU.mult, ALU.mult), tc_=tc),
                     waits=[(s_sc, sc2_done), (s_f0, dbg_guard)], inc=s_v)
                p.op("vector", lambda v: v.drain(), inc=s_v)
                comb_read = s_v.n
            comb_done = s_v.n
            p.dma("sync", dbg_o[:, tsl * 128:(tsl + 1) * 128], comb_sb[:],
                  waits=[(s_v, comb_done)], inc=s_f0)

            def proj_1024(wt_dram, out_dram, name):
                for eh in range(2):
                    w_war = [(s_mm, s_mm.n)]
                    tw0 = p.dma("gpsimd", wbuf[0][:].rearrange("p (k e) -> p k e", k=8),
                          wt_dram[0:1024, eh * 512:(eh + 1) * 512].rearrange("(k p) e -> p k e", p=128),
                          waits=w_war, inc=s_wb[0])
                    tw1 = p.dma("gpsimd", wbuf[1][:].rearrange("p (k e) -> p k e", k=8),
                          wt_dram[1024:2048, eh * 512:(eh + 1) * 512].rearrange("(k p) e -> p k e", p=128),
                          inc=s_wb[1])
                    for ec in range(4):
                        for ts in range(2):
                            bi = 2 + (ec * 2 + ts) % 2
                            mms = []
                            for k in range(16):
                                mms.append(partial(
                                    lambda t, k_, ec_, ts_, bi_: t.matmul(
                                        pb[bi_][:, :],
                                        wbuf[k_ // 8][:, (k_ % 8) * 512 + ec_ * 128:
                                                      (k_ % 8) * 512 + (ec_ + 1) * 128],
                                        xnw[:, k_ * 1024 + ts_ * 512: k_ * 1024 + (ts_ + 1) * 512],
                                        start=(k_ == 0), stop=(k_ == 15)),
                                    k_=k, ec_=ec, ts_=ts, bi_=bi))
                            tok = mm_group(bi, mms, waits_first=[(s_wb[0], tw0), (s_wb[1], tw1),
                                                                  (s_xw, t_xw)])
                            si = (ec * 2 + ts) % 4
                            w_stg = use_stg(stg_tok, si) + [(s_mm, tok)]
                            ev = p.op("vector", partial(
                                lambda v, bi_, si_: v.tensor_copy(stg[si_][:], pb[bi_][:, :]), bi_=bi, si_=si),
                                waits=w_stg, inc=s_v)
                            bank_consumed(bi, s_v, ev)
                            dt = p.dma("sync",
                                       out_dram[eh * 512 + ec * 128: eh * 512 + (ec + 1) * 128,
                                                t0 + ts * 512: t0 + (ts + 1) * 512],
                                       stg[si][:], waits=[(s_v, ev)], inc=s_stgd[si])
                            stg_tok[si] = (s_stgd[si], dt)

            proj_1024(wq, qT_o, "q")
            proj_1024(wk, kT_o, "k")

            for ts in range(2):
                bi = 2 + ts % 2
                mms = []
                for k in range(16):
                    mms.append(partial(
                        lambda t, k_, ts_, bi_: t.matmul(
                            pb[bi_][0:16, :],
                            wkg1_sb[:, k_ * 16:(k_ + 1) * 16],
                            xnw[:, k_ * 1024 + ts_ * 512: k_ * 1024 + (ts_ + 1) * 512],
                            start=(k_ == 0), stop=(k_ == 15)),
                        k_=k, ts_=ts, bi_=bi))
                tok = mm_group(bi, mms, waits_first=[(s_xw, t_xw), (s_const, CONSTS), (s_constg, CONSTG)])
                ev = p.op("vector", partial(
                    lambda v, bi_, ts_: v.tensor_copy(t1_sb[:, ts_ * 512:(ts_ + 1) * 512], pb[bi_][0:16, :]),
                    bi_=bi, ts_=ts), waits=[(s_mm, tok)], inc=s_v)
                bank_consumed(bi, s_v, ev)
            t1_done = s_v.n

            for ec in range(8):
                for ts in range(2):
                    bi = 2 + (ec * 2 + ts) % 2
                    mms = [partial(
                        lambda t, ec_, ts_, bi_: t.matmul(
                            pb[bi_][:, :], wkg2_sb[:, ec_ * 128:(ec_ + 1) * 128],
                            t1_sb[:, ts_ * 512:(ts_ + 1) * 512], start=True, stop=True),
                        ec_=ec, ts_=ts, bi_=bi)]
                    tok = mm_group(bi, mms, waits_first=[(s_v, t1_done)])
                    bq = ec % 8
                    p.op("vector", partial(lambda v, bi_, ec_: v.tensor_scalar(
                            scr[0][:], pb[bi_][:, :], bkg2_sb[:, ec_:ec_ + 1], None, ALU.add),
                            bi_=bi, ec_=ec), waits=[(s_mm, tok), (s_const, CONSTS), (s_constg, CONSTG)], inc=s_v)
                    pre_done = s_v.n
                    bank_consumed(bi, s_v, pre_done)
                    abs_done = p.op("scalar", lambda s: s.activation(scr[1][:], scr[0][:], AF.Abs),
                         waits=[(s_v, pre_done)], inc=s_sc)
                    p.op("scalar", lambda s: s.activation(scr[1][:], scr[1][:], AF.Exp, scale=-1.0), inc=s_sc)
                    p.op("scalar", lambda s: s.activation(scr[1][:], scr[1][:], AF.Ln, bias=constf[:, 1:2]),
                         waits=[(s_const, CONSTS)], inc=s_sc)
                    ln_done = s_sc.n
                    p.op("vector", lambda v: v.tensor_scalar_min(scr[0][:], scr[0][:], 0.0),
                         waits=[(s_sc, abs_done)], inc=s_v)
                    si = (ec * 2 + ts) % 2
                    w_stg = use_stg(stgf_tok, si) + [(s_sc, ln_done)]
                    ev = p.op("vector", partial(
                        lambda v, si_: v.tensor_tensor(stgf[si_][:], scr[0][:], scr[1][:], ALU.subtract), si_=si),
                        waits=w_stg, inc=s_v)
                    dt = p.dma("sync", gkT_o[ec * 128:(ec + 1) * 128, t0 + ts * 512: t0 + (ts + 1) * 512],
                               stgf[si][:], waits=[(s_v, ev)], inc=s_stgfd[si])
                    stgf_tok[si] = (s_stgfd[si], dt)

            def proj_tok(wt_dram, out_dram, bias_row, do_silu):
                for dq in range(4):
                    w_war = [(s_mm, s_mm.n)]
                    tw0 = p.dma("gpsimd", wbuf[0][:].rearrange("p (k e) -> p k e", k=8),
                          wt_dram[0:1024, dq * 512:(dq + 1) * 512].rearrange("(k p) e -> p k e", p=128),
                          waits=w_war, inc=s_wb[0])
                    tw1 = p.dma("gpsimd", wbuf[1][:].rearrange("p (k e) -> p k e", k=8),
                          wt_dram[1024:2048, dq * 512:(dq + 1) * 512].rearrange("(k p) e -> p k e", p=128),
                          inc=s_wb[1])
                    for tc in range(8):
                        bi = 2 + tc % 2
                        last_k = 15 if bias_row is None else None
                        mms = []
                        for k in range(16):
                            mms.append(partial(
                                lambda t, k_, tc_, bi_: t.matmul(
                                    pb[bi_][:, :],
                                    xnw[:, k_ * 1024 + tc_ * 128: k_ * 1024 + (tc_ + 1) * 128],
                                    wbuf[k_ // 8][:, (k_ % 8) * 512:(k_ % 8 + 1) * 512],
                                    start=(k_ == 0), stop=(k_ == last_k)),
                                k_=k, tc_=tc, bi_=bi))
                        if bias_row is not None:
                            mms.append(partial(
                                lambda t, tc_, bi_, dq_: t.matmul(
                                    pb[bi_][:, :], ones1[:],
                                    bg_sb[0:1, dq_ * 512:(dq_ + 1) * 512], start=False, stop=True),
                                tc_=tc, bi_=bi, dq_=dq))
                        tok = mm_group(bi, mms, waits_first=[(s_wb[0], tw0), (s_wb[1], tw1),
                                                              (s_xw, t_xw), (s_const, CONSTS), (s_constg, CONSTG)])
                        si = tc % 4
                        w_stg = use_stg(stg_tok, si) + [(s_mm, tok)]
                        if do_silu:
                            ev = p.op("scalar", partial(
                                lambda s, bi_, si_: s.activation(stg[si_][:], pb[bi_][:, :], AF.Silu),
                                bi_=bi, si_=si), waits=w_stg, inc=s_sc)
                            sem = s_sc
                        else:
                            ev = p.op("vector", partial(
                                lambda v, bi_, si_: v.tensor_copy(stg[si_][:], pb[bi_][:, :]),
                                bi_=bi, si_=si), waits=w_stg, inc=s_v)
                            sem = s_v
                        bank_consumed(bi, sem, ev)
                        dt = p.dma("sync", out_dram[t0 + tc * 128: t0 + (tc + 1) * 128,
                                                    dq * 512:(dq + 1) * 512],
                                   stg[si][:], waits=[(sem, ev)], inc=s_stgd[si])
                        stg_tok[si] = (s_stgd[si], dt)

            proj_tok(wv, v_o, None, False)
            proj_tok(wgg, g_o, bg_sb, True)

            wd_prev = 0
            silu_tok = [0, 0]
            bsc_tok = 0

            for grp in range(9):
                shared_grp = (grp == 8)
                m_war = wd_prev
                for ei in range(2):
                    eg = grp * 2 + ei
                    for imh in range(2):
                        if shared_grp:
                            col0 = ei * 1024
                            g_src = swg[:, col0 + imh * 512: col0 + (imh + 1) * 512]
                            u_src = swu[:, col0 + imh * 512: col0 + (imh + 1) * 512]
                        else:
                            g_src = ewg[eg, :, imh * 512:(imh + 1) * 512]
                            u_src = ewu[eg, :, imh * 512:(imh + 1) * 512]
                        w_war = [(s_mm, s_mm.n)]
                        twg0 = p.dma("gpsimd", wbuf[0][:].rearrange("p (k e) -> p k e", k=8),
                              g_src[0:1024, :].rearrange("(k p) e -> p k e", p=128),
                              waits=w_war, inc=s_wb[0])
                        twg1 = p.dma("gpsimd", wbuf[1][:].rearrange("p (k e) -> p k e", k=8),
                              g_src[1024:2048, :].rearrange("(k p) e -> p k e", p=128), inc=s_wb[1])
                        twu0 = p.dma("gpsimd", wbuf[2][:].rearrange("p (k e) -> p k e", k=8),
                              u_src[0:1024, :].rearrange("(k p) e -> p k e", p=128), inc=s_wb[2])
                        twu1 = p.dma("gpsimd", wbuf[3][:].rearrange("p (k e) -> p k e", k=8),
                              u_src[1024:2048, :].rearrange("(k p) e -> p k e", p=128), inc=s_wb[3])
                        for imc in range(4):
                            imabs = imh * 4 + imc
                            for tw in range(2):
                                mmsg = [partial(
                                    lambda t, k_, imc_, tw_: t.matmul(
                                        pb[0][:, :],
                                        wbuf[k_ // 8][:, (k_ % 8) * 512 + imc_ * 128:
                                                      (k_ % 8) * 512 + (imc_ + 1) * 128],
                                        xnw[:, k_ * 1024 + tw_ * 512: k_ * 1024 + (tw_ + 1) * 512],
                                        start=(k_ == 0), stop=(k_ == 15)),
                                    k_=k, imc_=imc, tw_=tw) for k in range(16)]
                                tokg = mm_group(0, mmsg, waits_first=[(s_wb[0], twg0), (s_wb[1], twg1),
                                                                      (s_xw, t_xw)])
                                mmsu = [partial(
                                    lambda t, k_, imc_, tw_: t.matmul(
                                        pb[1][:, :],
                                        wbuf[2 + k_ // 8][:, (k_ % 8) * 512 + imc_ * 128:
                                                          (k_ % 8) * 512 + (imc_ + 1) * 128],
                                        xnw[:, k_ * 1024 + tw_ * 512: k_ * 1024 + (tw_ + 1) * 512],
                                        start=(k_ == 0), stop=(k_ == 15)),
                                    k_=k, imc_=imc, tw_=tw) for k in range(16)]
                                toku = mm_group(1, mmsu, waits_first=[(s_wb[2], twu0), (s_wb[3], twu1),
                                                                      (s_xw, t_xw)])
                                sl = tw % 2
                                ev_s = p.op("scalar", partial(
                                    lambda s, sl_: s.activation(silu_sb[sl_][:], pb[0][:, :], AF.Silu), sl_=sl),
                                    waits=[(s_mm, tokg), (s_v, silu_tok[sl])], inc=s_sc)
                                bank_consumed(0, s_sc, ev_s)
                                moff = (ei * 8 + imabs) * 1024 + tw * 512
                                ev_m = p.op("vector", partial(
                                    lambda v, sl_, mo_: v.tensor_tensor(
                                        m_sb[:, mo_:mo_ + 512], silu_sb[sl_][:], pb[1][:, :], ALU.mult),
                                    sl_=sl, mo_=moff),
                                    waits=[(s_mm, toku), (s_sc, ev_s), (s_mm, m_war)], inc=s_v)
                                silu_tok[sl] = ev_m
                                bank_consumed(1, s_v, ev_m)
                p.op("vector", lambda v: v.drain(), inc=s_v)
                m_done = s_v.n

                if not shared_grp:
                    for ei in range(2):
                        eg = grp * 2 + ei
                        for tc8 in range(8):
                            bi = 2 + tc8 % 2
                            ev_b = p.op("gpsimd", partial(
                                lambda g, eg_, tc_: g.tensor_scalar_mul(
                                    bsc_sb[:], onest[:], comb_sb[:, tc_ * 16 + eg_: tc_ * 16 + eg_ + 1]),
                                eg_=eg, tc_=tc8),
                                waits=[(s_v, comb_done), (s_mm, bsc_tok), (s_const, CONSTS), (s_constg, CONSTG)],
                                inc=s_g)
                            tok = mm_group(bi, [partial(
                                lambda t, bi_: t.transpose(pb[bi_][:, 0:64].bitcast(BF16), bsc_sb[:], ident[:]),
                                bi_=bi)], waits_first=[(s_g, ev_b)])
                            bsc_tok = tok
                            ev = p.op("scalar", partial(
                                lambda s, bi_, tc_: s.copy(be_sb[:, tc_ * 128:(tc_ + 1) * 128],
                                                           pb[bi_][:, 0:64].bitcast(BF16)), bi_=bi, tc_=tc8),
                                waits=[(s_mm, tok), (s_g, s_g.n)], inc=s_sc)
                            bank_consumed(bi, s_sc, ev)
                        p.op("scalar", lambda s: s.drain(), inc=s_sc)
                        p.op("gpsimd", partial(
                            lambda g, ei_: g.tensor_tensor(
                                m_sb[:, ei_ * 8192:(ei_ + 1) * 8192].rearrange("p (i t) -> p i t", i=8),
                                m_sb[:, ei_ * 8192:(ei_ + 1) * 8192].rearrange("p (i t) -> p i t", i=8),
                                be_sb[:].unsqueeze(1).broadcast_to([128, 8, 1024]), ALU.mult),
                            ei_=ei),
                            waits=[(s_v, m_done), (s_sc, s_sc.n)], inc=s_g)
                if not shared_grp:
                    p.op("gpsimd", lambda g: g.drain(), inc=s_g)
                mscale_done = s_g.n

                for dq in range(4):
                    for ei in range(2):
                        eg = grp * 2 + ei
                        if shared_grp:
                            wd_src = swd[ei * 1024:(ei + 1) * 1024, dq * 512:(dq + 1) * 512]
                        else:
                            wd_src = ewd[eg, :, dq * 512:(dq + 1) * 512]
                        p.dma("gpsimd", wbuf[ei][:].rearrange("p (i e) -> p i e", i=8),
                              wd_src.rearrange("(i p) e -> p i e", p=128),
                              waits=[(s_mm, s_mm.n)], inc=s_wb[ei])
                    t_wd = [(s_wb[0], s_wb[0].n), (s_wb[1], s_wb[1].n)]
                    for tc8 in range(8):
                        bi = 4 + tc8 % 4
                        mms = []
                        for ei in range(2):
                            for imc in range(8):
                                mms.append(partial(
                                    lambda t, ei_, imc_, tc_, bi_: t.matmul(
                                        pb[bi_][:, :],
                                        m_sb[:, (ei_ * 8 + imc_) * 1024 + tc_ * 128:
                                                (ei_ * 8 + imc_) * 1024 + (tc_ + 1) * 128],
                                        wbuf[ei_][:, imc_ * 512:(imc_ + 1) * 512],
                                        start=(ei_ == 0 and imc_ == 0), stop=(ei_ == 1 and imc_ == 7)),
                                    ei_=ei, imc_=imc, tc_=tc8, bi_=bi))
                        tok = mm_group(bi, mms, waits_first=t_wd + [(s_g, mscale_done), (s_v, m_done)])
                        fo = tc8 * D + dq * 512
                        if grp == 0:
                            ev = p.op("vector", partial(
                                lambda v, bi_, fo_: v.tensor_copy(facc[:, fo_:fo_ + 512], pb[bi_][:, :]),
                                bi_=bi, fo_=fo), waits=[(s_mm, tok), (s_f0, f0_dma_tok)], inc=s_v)
                        else:
                            ev = p.op("vector", partial(
                                lambda v, bi_, fo_: v.tensor_tensor(
                                    facc[:, fo_:fo_ + 512], facc[:, fo_:fo_ + 512], pb[bi_][:, :], ALU.add),
                                bi_=bi, fo_=fo), waits=[(s_mm, tok)], inc=s_v)
                        bank_consumed(bi, s_v, ev)
                wd_prev = s_mm.n

            p.op("vector", lambda v: v.drain(), inc=s_v)
            f0_dma_tok = p.dma("sync", f0_o[t0:t0 + 1024, :].rearrange("(c p) e -> p c e", p=128),
                               facc[:].rearrange("p (c e) -> p c e", c=8),
                               waits=[(s_v, s_v.n)], inc=s_f0)

        fin = ([(s, s.n) for s in s_stgd] + [(s, s.n) for s in s_stgfd]
               + [(s, s.n) for s in s_xnt] + [(s_f0, s_f0.n)])
        p.op("sync", lambda sy: sy.nop(), waits=fin)

    p.emit()
    return nc, p.n_ops()



def build_phase_b():
    nc = bass.Bass("TRN2", target_bir_lowering=False)
    ctx = ExitStack()
    p = Prog(nc, ctx)

    NU = 4
    NCHU = 32
    DV = 256

    qT = nc.dram_tensor("qTu", [NU * 128, 4096], BF16, kind="ExternalInput")
    kT = nc.dram_tensor("kTu", [NU * 128, 4096], BF16, kind="ExternalInput")
    gkT = nc.dram_tensor("gkTu", [NU * 128, 4096], F32, kind="ExternalInput")
    vt = nc.dram_tensor("vtu", [4096, NU * DV], BF16, kind="ExternalInput")
    ident_d = nc.dram_tensor("ident", [128, 128], BF16, kind="ExternalInput")
    ut01_d = nc.dram_tensor("ut01", [128, 128], BF16, kind="ExternalInput")
    constf_d = nc.dram_tensor("constf", [128, 8], F32, kind="ExternalInput")
    o_o = nc.dram_tensor("o_tok", [4096, NU * DV], BF16, kind="ExternalOutput")

    with ctx:
        e = ctx.enter_context
        ident = e(nc.sbuf_tensor("identsb", [128, 128], BF16))
        ut01 = e(nc.sbuf_tensor("ut01sb", [128, 128], BF16))
        constf = e(nc.sbuf_tensor("constfsb", [128, 8], F32))
        qs = [e(nc.sbuf_tensor(f"qs{i}", [128, 4096], BF16)) for i in range(2)]
        ks = [e(nc.sbuf_tensor(f"ks{i}", [128, 4096], BF16)) for i in range(2)]
        gs = [e(nc.sbuf_tensor(f"gs{i}", [128, 4096], F32)) for i in range(2)]
        vs = [e(nc.sbuf_tensor(f"vs{i}", [128, 32 * DV], BF16)) for i in range(2)]
        gkc = e(nc.sbuf_tensor("gkc", [128, 128], F32))
        eq = e(nc.sbuf_tensor("eq", [128, 128], F32))
        eki = e(nc.sbuf_tensor("eki", [128, 128], F32))
        qd = e(nc.sbuf_tensor("qd", [128, 128], BF16))
        kexp = e(nc.sbuf_tensor("kexp", [128, 128], BF16))
        amask = e(nc.sbuf_tensor("amask", [128, 128], BF16))
        kexpp = e(nc.sbuf_tensor("kexpp", [128, 128], BF16))
        S_f = e(nc.sbuf_tensor("S_f", [128, DV], F32))
        S_b = e(nc.sbuf_tensor("S_b", [128, DV], BF16))
        ostat = e(nc.sbuf_tensor("ostat", [128, 16], F32))
        sigc = e(nc.sbuf_tensor("sigc", [128, 1], F32))
        sqscr = e(nc.sbuf_tensor("sqscr", [128, DV], BF16))
        ost = [e(nc.sbuf_tensor(f"ost{i}", [128, 4 * DV], BF16)) for i in range(2)]

        pb = [e(nc.psum_tensor(f"pb{i}", [128, 512], F32)) for i in range(8)]

        s_const = p.sem("const")
        s_u = [p.sem(f"u{i}") for i in range(2)]
        s_mm = p.sem("mm")
        s_v = p.sem("v")
        s_sc = p.sem("sc")
        s_g = p.sem("g")
        s_od = [p.sem(f"od{i}") for i in range(2)]
        p.auto = {"vector": s_v, "scalar": s_sc, "gpsimd": s_g}

        bank_writer = [0] * 8
        bank_reader = [0] * 8
        bank_rsem = [None] * 8

        def mm_group(bank_i, mms, waits_first=()):
            w = list(waits_first)
            if bank_rsem[bank_i] is not None:
                w.append((bank_rsem[bank_i], bank_reader[bank_i]))
            for i, fn in enumerate(mms):
                inc = s_mm if i == len(mms) - 1 else None
                p.op("tensor", fn, waits=w if i == 0 else (), inc=inc)
            bank_writer[bank_i] = s_mm.n
            return s_mm.n

        def bank_consumed(bank_i, sem, val):
            bank_rsem[bank_i] = sem
            bank_reader[bank_i] = val

        p.dma("sync", ident[:], ident_d[:, :], inc=s_const)
        p.dma("sync", ut01[:], ut01_d[:, :], inc=s_const)
        p.dma("sync", constf[:], constf_d[:, :], inc=s_const)
        CONSTS = s_const.n

        unit_tok = [0, 0]
        unit_free = [0] * 2
        ost_tok = [0, 0]

        def load_unit(u):
            sl = u % 2
            w = [(s_mm, unit_free[sl])]
            p.dma("sync", qs[sl][:], qT[u * 128:(u + 1) * 128, :], waits=w, inc=s_u[sl])
            p.dma("sync", ks[sl][:], kT[u * 128:(u + 1) * 128, :], inc=s_u[sl])
            p.dma("sync", gs[sl][:], gkT[u * 128:(u + 1) * 128, :], inc=s_u[sl])
            tv = p.dma("sync", vs[sl][:].rearrange("p (n d) -> p n d", n=32),
                       vt[:, u * DV:(u + 1) * DV].rearrange("(n p) d -> p n d", p=128),
                       inc=s_u[sl])
            unit_tok[sl] = tv
        load_unit(0)

        for u in range(NU):
            sl = u % 2
            tq = tk = tg = tv = unit_tok[sl]
            if u + 1 < NU:
                load_unit(u + 1)
            p.op("vector", lambda v: v.memset(S_f[:], 0.0), waits=[(s_mm, s_mm.n)], inc=s_v)
            p.op("vector", lambda v: v.memset(S_b[:], 0.0), inc=s_v)
            p.op("vector", lambda v: v.drain(), inc=s_v)
            s_ready = s_v.n
            s_ready_mm = 0
            prev_chunk_mm = s_mm.n

            for n in range(NCHU):
                q_sl = partial(lambda s_, n_: s_[:, n_ * 128:(n_ + 1) * 128], qs[sl], n)
                k_sl = partial(lambda s_, n_: s_[:, n_ * 128:(n_ + 1) * 128], ks[sl], n)
                g_sl = partial(lambda s_, n_: s_[:, n_ * 128:(n_ + 1) * 128], gs[sl], n)
                v_sl = partial(lambda s_, n_: s_[:, n_ * DV:(n_ + 1) * DV], vs[sl], n)
                p.op("vector", partial(lambda v, gsl=None: v.tensor_tensor_scan(
                        gkc[:], gsl(), gsl(), 0.0, ALU.add, ALU.bypass), gsl=g_sl),
                     waits=[(s_u[sl], tg)], inc=s_v)
                p.op("vector", lambda v: v.drain(), inc=s_v)
                scan_done = s_v.n
                p.op("scalar", lambda s: s.activation(eq[:], gkc[:], AF.Exp, scale=1.0 / 16),
                     waits=[(s_v, scan_done)], inc=s_sc)
                p.op("scalar", lambda s: s.activation(eki[:], gkc[:], AF.Exp, scale=-1.0 / 16), inc=s_sc)
                p.op("scalar", lambda s: s.activation(sigc[:], gkc[:, 127:128], AF.Exp, scale=1.0 / 16), inc=s_sc)
                exps_done = s_sc.n
                p.op("vector", partial(lambda v, qsl=None: v.tensor_tensor(qd[:], qsl(), eq[:], ALU.mult), qsl=q_sl),
                     waits=[(s_sc, exps_done), (s_u[sl], tq), (s_mm, prev_chunk_mm)], inc=s_v)
                p.op("vector", partial(lambda v, ksl=None: v.scalar_tensor_tensor(
                        kexp[:], ksl(), 1.0 / 16, eki[:], ALU.mult, ALU.mult), ksl=k_sl),
                     waits=[(s_u[sl], tk), (s_mm, prev_chunk_mm)], inc=s_v)
                p.op("vector", lambda v: v.drain(), inc=s_v)
                qdkexp_done = s_v.n
                ba = n % 2
                tok_a = mm_group(ba, [partial(
                    lambda t, ba_=0: t.matmul(pb[ba_][:, 0:128], kexp[:], qd[:], start=True, stop=True), ba_=ba)],
                    waits_first=[(s_v, qdkexp_done)])
                bt = 2 + n % 2
                tok_t = mm_group(bt, [partial(
                    lambda t, bt_=0: t.transpose(pb[bt_][:, 0:64].bitcast(BF16), kexp[:], ident[:]), bt_=bt)],
                    waits_first=[(s_v, qdkexp_done), (s_const, CONSTS)])
                ev_am = p.op("vector", partial(
                    lambda v, ba_=0: v.tensor_tensor(amask[:], pb[ba_][:, 0:128], ut01[:], ALU.mult), ba_=ba),
                    waits=[(s_mm, tok_a), (s_const, CONSTS)], inc=s_v)
                bank_consumed(ba, s_v, ev_am)
                ev_kp = p.op("scalar", partial(
                    lambda s, bt_=0: s.copy(kexpp[:], pb[bt_][:, 0:64].bitcast(BF16)), bt_=bt),
                    waits=[(s_mm, tok_t)], inc=s_sc)
                bank_consumed(bt, s_sc, ev_kp)
                bu = 4 + n % 2
                tok_u = mm_group(bu, [partial(
                    lambda t, bu_=0, vsl=None: t.matmul(pb[bu_][:, 0:DV], kexpp[:], vsl(), start=True, stop=True),
                    bu_=bu, vsl=v_sl)],
                    waits_first=[(s_sc, ev_kp), (s_u[sl], tv)])
                bo = 6 + n % 2
                p.op("vector", lambda v: v.drain(), waits=[(s_v, s_v.n)], inc=s_v)
                am_done = s_v.n
                tok_o = mm_group(bo, [
                    partial(lambda t, bo_=0: t.matmul(pb[bo_][:, 0:DV], qd[:], S_b[:], start=True, stop=False), bo_=bo),
                    partial(lambda t, bo_=0, vsl=None: t.matmul(pb[bo_][:, 0:DV], amask[:], vsl(), start=False, stop=True),
                            bo_=bo, vsl=v_sl),
                ], waits_first=[(s_v, am_done), (s_v, s_ready)])
                s_ready_mm = tok_o
                p.op("vector", partial(lambda v, bu_=0: v.tensor_tensor(
                        S_f[:], S_f[:], pb[bu_][:, 0:DV], ALU.add), bu_=bu),
                     waits=[(s_mm, tok_u), (s_sc, exps_done)], inc=s_v)
                ev_su = p.op("vector", lambda v: v.tensor_scalar_mul(S_f[:], S_f[:], sigc[:, 0:1]), inc=s_v)
                bank_consumed(bu, s_v, ev_su)
                p.op("vector", lambda v: v.tensor_copy(S_b[:], S_f[:]),
                     waits=[(s_mm, s_ready_mm)], inc=s_v)
                p.op("vector", lambda v: v.drain(), inc=s_v)
                s_ready = s_v.n
                p.op("vector", partial(lambda v, bo_=0: v.reduce_sum(ostat[:, 0:1], pb[bo_][:, 0:DV], AX.X), bo_=bo),
                     waits=[(s_mm, tok_o)], inc=s_v)
                p.op("vector", lambda v: v.drain(), inc=s_v)
                osum_done = s_v.n
                p.op("scalar", partial(lambda s, bo_=0: s.activation(
                        sqscr[:], pb[bo_][:, 0:DV], AF.Square, accum_out=ostat[:, 1:2]), bo_=bo),
                     waits=[(s_mm, tok_o)], inc=s_sc)
                p.op("scalar", lambda s: s.mul(ostat[:, 2:3], ostat[:, 0:1], 1.0 / DV),
                     waits=[(s_v, osum_done)], inc=s_sc)
                p.op("scalar", lambda s: s.mul(ostat[:, 3:4], ostat[:, 1:2], 1.0 / DV), inc=s_sc)
                mu_done = s_sc.n
                p.op("vector", lambda v: v.tensor_tensor(ostat[:, 4:5], ostat[:, 2:3], ostat[:, 2:3], ALU.mult),
                     waits=[(s_sc, mu_done)], inc=s_v)
                p.op("vector", lambda v: v.tensor_tensor(ostat[:, 5:6], ostat[:, 3:4], ostat[:, 4:5], ALU.subtract), inc=s_v)
                p.op("vector", lambda v: v.drain(), inc=s_v)
                var_done = s_v.n
                p.op("scalar", lambda s: s.activation(ostat[:, 6:7], ostat[:, 5:6], AF.Ln, bias=constf[:, 0:1]),
                     waits=[(s_v, var_done), (s_const, CONSTS)], inc=s_sc)
                p.op("scalar", lambda s: s.activation(ostat[:, 7:8], ostat[:, 6:7], AF.Exp, scale=-0.5), inc=s_sc)
                rstd_done = s_sc.n
                osl = (n // 4) % 2
                wnorm = [(s_sc, rstd_done)]
                if n % 4 == 0:
                    wnorm.append((s_od[osl], ost_tok[osl]))
                ev_on = p.op("vector", partial(
                    lambda v, bo_=0, osl_=0, n_=0: v.tensor_scalar(
                        ost[osl_][:, (n_ % 4) * DV:(n_ % 4 + 1) * DV], pb[bo_][:, 0:DV],
                        ostat[:, 2:3], ostat[:, 7:8], ALU.subtract, ALU.mult),
                    bo_=bo, osl_=osl, n_=n), waits=wnorm, inc=s_v)
                bank_consumed(bo, s_v, ev_on)
                if n % 4 == 3:
                    p.op("vector", lambda v: v.drain(), inc=s_v)
                    p.dma("sync", o_o[(n - 3) * 128 + 0:(n + 1) * 128, u * DV:(u + 1) * DV]
                          .rearrange("(c p) d -> p c d", p=128),
                          ost[osl][:].rearrange("p (c d) -> p c d", c=4),
                          waits=[(s_v, s_v.n)], inc=s_od[osl])
                    ost_tok[osl] = s_od[osl].n
                prev_chunk_mm = s_mm.n
            unit_free[sl] = s_mm.n

        fin = [(s, s.n) for s in s_od]
        p.op("sync", lambda sy: sy.nop(), waits=fin)

    p.emit()
    return nc, p.n_ops()



def build_phase_c():
    nc = bass.Bass("TRN2", target_bir_lowering=False)
    ctx = ExitStack()
    p = Prog(nc, ctx)

    x = nc.dram_tensor("x", [TC, D], F32, kind="ExternalInput")
    f0 = nc.dram_tensor("f0", [TC, D], F32, kind="ExternalInput")
    o_t = nc.dram_tensor("o_tok", [TC, D], BF16, kind="ExternalInput")
    g_t = nc.dram_tensor("g_tok", [TC, D], BF16, kind="ExternalInput")
    wo = nc.dram_tensor("wo", [D, D], F32, kind="ExternalInput")
    wqkvT = nc.dram_tensor("wqkvT", [D, 3 * D], F32, kind="ExternalInput")
    bqkv_c = nc.dram_tensor("bqkv_c", [128, 32], F32, kind="ExternalInput")
    bv_row = nc.dram_tensor("bv_row", [1, D], F32, kind="ExternalInput")
    gate1 = nc.dram_tensor("gate1", [D, 8], F32, kind="ExternalInput")
    upw = nc.dram_tensor("upw", [D, 8192], F32, kind="ExternalInput")
    downw = nc.dram_tensor("downw", [8192, D], F32, kind="ExternalInput")
    expw = nc.dram_tensor("expw", [D, D], F32, kind="ExternalInput")
    gam_row = nc.dram_tensor("gam_row", [1, D], F32, kind="ExternalInput")
    bet_row = nc.dram_tensor("bet_row", [1, D], F32, kind="ExternalInput")
    ident_d = nc.dram_tensor("ident", [128, 128], BF16, kind="ExternalInput")
    ones1_d = nc.dram_tensor("ones1", [1, 128], BF16, kind="ExternalInput")
    constf_d = nc.dram_tensor("constf", [128, 8], F32, kind="ExternalInput")

    q1T_o = nc.dram_tensor("q1T", [D, TC], BF16, kind="ExternalOutput")
    k1T_o = nc.dram_tensor("k1T", [D, TC], BF16, kind="ExternalOutput")
    v1_o = nc.dram_tensor("v1_tok", [TC, D], BF16, kind="ExternalOutput")
    x1pf_o = nc.dram_tensor("x1pf", [TC, D], F32, kind="ExternalOutput")
    x1_o = nc.dram_tensor("x1", [TC, D], F32, kind="ExternalOutput")
    zT_d = nc.dram_tensor("zT_scr", [D, TC], BF16)
    x01_d = nc.dram_tensor("x01_scr", [TC, D], F32)
    xn1T_d = nc.dram_tensor("xn1T_scr", [D, TC], BF16)

    with ctx:
        e = ctx.enter_context
        ident = e(nc.sbuf_tensor("identsb", [128, 128], BF16))
        ones1 = e(nc.sbuf_tensor("ones1sb", [1, 128], BF16))
        constf = e(nc.sbuf_tensor("constfsb", [128, 8], F32))
        bqkv_sb = e(nc.sbuf_tensor("bqkvsb", [128, 32], F32))
        bv_sb = e(nc.sbuf_tensor("bvsb", [1, D], BF16))
        gate1_sb = e(nc.sbuf_tensor("gate1sb", [128, 16 * 8], BF16))
        stat = e(nc.sbuf_tensor("stat", [128, 16], F32))
        t2_sb = e(nc.sbuf_tensor("t2sb", [128, 8], F32))
        sc2_sb = e(nc.sbuf_tensor("sc2sb", [128, 2], F32))

        big = e(nc.sbuf_tensor("bigscr", [128, 16384], BF16))
        oc_sb = big[:, 0:2048]
        gc_sb = big[:, 2048:4096]
        z_sb = big[:, 4096:6144]
        trst = big[:, 6144:8192]
        xn1_sb = big[:, 0:2048]
        m1g = big[:, 0:8192]
        gb_sb = big[:, 8192:12288].bitcast(F32)
        bb_sb = big[:, 12288:16384].bitcast(F32)

        xq = [e(nc.sbuf_tensor(f"xq{i}", [128, D], F32)) for i in range(2)]
        xq2 = [e(nc.sbuf_tensor(f"xq2{i}", [128, D], F32)) for i in range(2)]
        xnw = e(nc.sbuf_tensor("xnw", [128, 16 * 1024], BF16))
        gam_bf = xnw[0:1, 0:D]
        bet_bf = xnw[0:1, D:2 * D]
        wbuf = [e(nc.sbuf_tensor(f"wbuf{i}", [128, 4096], BF16)) for i in range(4)]
        facc = e(nc.sbuf_tensor("facc", [128, 8 * D], F32))
        stg = [e(nc.sbuf_tensor(f"stg{i}", [128, 512], BF16)) for i in range(4)]
        relu_sb = [e(nc.sbuf_tensor(f"relu{i}", [128, 512], BF16)) for i in range(2)]

        pb = [e(nc.psum_tensor(f"pb{i}", [128, 512], F32)) for i in range(8)]

        s_const = p.sem("const")
        s_constg = p.sem("constg")
        s_ld = [p.sem(f"ld{i}") for i in range(4)]
        s_wb = [p.sem(f"wb{i}") for i in range(4)]
        s_xw = p.sem("xw")
        s_mm = p.sem("mm")
        s_v = p.sem("v")
        s_sc = p.sem("sc")
        s_g = p.sem("g")
        s_stgd = [p.sem(f"stgd{i}") for i in range(4)]
        s_trd = p.sem("trd")
        s_x01w = p.sem("x01w")
        s_x1s = [p.sem(f"x1s{i}") for i in range(2)]
        s_pfw = p.sem("pfw")
        p.auto = {"vector": s_v, "scalar": s_sc, "gpsimd": s_g}

        bank_writer = [0] * 8
        bank_reader = [0] * 8
        bank_rsem = [None] * 8

        def mm_group(bank_i, mms, waits_first=()):
            w = list(waits_first)
            if bank_rsem[bank_i] is not None:
                w.append((bank_rsem[bank_i], bank_reader[bank_i]))
            for i, fn in enumerate(mms):
                inc = s_mm if i == len(mms) - 1 else None
                p.op("tensor", fn, waits=w if i == 0 else (), inc=inc)
            bank_writer[bank_i] = s_mm.n
            return s_mm.n

        def bank_consumed(bank_i, sem, val):
            bank_rsem[bank_i] = sem
            bank_reader[bank_i] = val

        stg_tok = [(None, 0)] * 4

        def use_tok(pool, idx):
            semval = pool[idx]
            return [] if semval[0] is None else [semval]

        p.dma("sync", ident[:], ident_d[:, :], inc=s_const)
        p.dma("sync", constf[:], constf_d[:, :], inc=s_const)
        p.dma("sync", bqkv_sb[:], bqkv_c[:, :], inc=s_const)
        p.dma("gpsimd", ones1[:], ones1_d[:, :], inc=s_constg)
        p.dma("gpsimd", bv_sb[:], bv_row[:, :], inc=s_constg)
        p.dma("gpsimd", gate1_sb[:].rearrange("p (k j) -> p k j", k=16),
              gate1[:, :].rearrange("(k p) j -> p k j", p=128), inc=s_constg)
        p.dma("gpsimd", gam_bf, gam_row[:, :], inc=s_constg)
        p.dma("gpsimd", bet_bf, bet_row[:, :], inc=s_constg)
        CONSTS = s_const.n
        CONSTG = s_constg.n

        for half, (row, dst) in enumerate([(gam_bf, gb_sb), (bet_bf, bb_sb)]):
            for i in range(4):
                bi = half * 4 + i
                tok = mm_group(bi, [partial(
                    lambda t, bi_=0, i_=0, row_=None: t.matmul(
                        pb[bi_][:, :], ones1[:], row_[0:1, i_ * 512:(i_ + 1) * 512],
                        start=True, stop=True), bi_=bi, i_=i, row_=row)],
                    waits_first=[(s_constg, CONSTG), (s_const, CONSTS)])
                ev = p.op("vector", partial(lambda v, bi_=0, i_=0, dst_=None: v.tensor_copy(
                    dst_[:, i_ * 512:(i_ + 1) * 512], pb[bi_][:, :]), bi_=bi, i_=i, dst_=dst),
                    waits=[(s_mm, tok)], inc=s_v)
                bank_consumed(bi, s_v, ev)

        prev_z = 0
        prev_x01 = 0
        prev_tr = 0
        for c in range(NCH):
            ldw = [(s_v, s_v.n)]
            t_o = p.dma("sync", oc_sb, o_t[c * 128:(c + 1) * 128, :], waits=ldw, inc=s_ld[0])
            t_g = p.dma("sync", gc_sb, g_t[c * 128:(c + 1) * 128, :], inc=s_ld[1])
            t_x = p.dma("sync", xq[0][:], x[c * 128:(c + 1) * 128, :],
                        waits=[(s_x01w, prev_x01)], inc=s_ld[2])
            t_f = p.dma("sync", xq[1][:], f0[c * 128:(c + 1) * 128, :], inc=s_ld[3])
            p.op("vector", lambda v: v.tensor_tensor(z_sb, gc_sb, oc_sb, ALU.mult),
                 waits=[(s_ld[0], t_o), (s_ld[1], t_g), (s_mm, prev_z)], inc=s_v)
            p.op("vector", lambda v: v.drain(), inc=s_v)
            z_done = s_v.n
            p.op("vector", lambda v: v.tensor_tensor(xq2[0][:], xq[0][:], xq[1][:], ALU.add),
                 waits=[(s_ld[2], t_x), (s_ld[3], t_f), (s_x01w, prev_x01)], inc=s_v)
            p.op("vector", lambda v: v.drain(), inc=s_v)
            prev_x01 = p.dma("sync", x01_d[c * 128:(c + 1) * 128, :], xq2[0][:],
                             waits=[(s_v, s_v.n)], inc=s_x01w)
            for k in range(16):
                bi = k % 2
                tok = mm_group(bi, [partial(
                    lambda t, k_=0, bi_=0: t.transpose(
                        pb[bi_][:, 0:64].bitcast(BF16), z_sb[:, k_ * 128:(k_ + 1) * 128], ident[:]),
                    k_=k, bi_=bi)], waits_first=[(s_v, z_done), (s_const, CONSTS)])
                eng, sem = ("vector", s_v) if k % 2 == 0 else ("scalar", s_sc)
                wv_ = [(s_mm, tok)]
                if k < 2:
                    wv_.append((s_trd, prev_tr))
                fn = (partial(lambda v, k_=0, bi_=0: v.tensor_copy(
                        trst[:, k_ * 128:(k_ + 1) * 128], pb[bi_][:, 0:64].bitcast(BF16)), k_=k, bi_=bi)
                      if eng == "vector" else
                      partial(lambda s, k_=0, bi_=0: s.copy(
                        trst[:, k_ * 128:(k_ + 1) * 128], pb[bi_][:, 0:64].bitcast(BF16)), k_=k, bi_=bi))
                ev = p.op(eng, fn, waits=wv_, inc=sem)
                bank_consumed(bi, sem, ev)
            prev_z = s_mm.n
            prev_tr = p.dma("sync",
                            zT_d[:, :].rearrange("(k p) t -> p k t", p=128)[:, :, c * 128:(c + 1) * 128],
                            trst.rearrange("p (k j) -> p k j", k=16),
                            waits=[(s_v, s_v.n), (s_sc, s_sc.n)], inc=s_trd)

        X01_ALL = s_x01w.n
        prev_x1 = [0, 0]
        x01_tok = [0, 0]
        for tsl in range(2):
            t0 = tsl * 1024
            t_zw = p.dma("sync", xnw[:].rearrange("p (k t) -> p k t", k=16),
                         zT_d[:, t0:t0 + 1024].rearrange("(k p) t -> p k t", p=128),
                         waits=[(s_mm, s_mm.n), (s_trd, s_trd.n)], inc=s_xw)
            for dq in range(4):
                w_war = [(s_mm, s_mm.n)]
                tw0 = p.dma("gpsimd", wbuf[0][:].rearrange("p (k e) -> p k e", k=8),
                            wo[0:1024, dq * 512:(dq + 1) * 512].rearrange("(k p) e -> p k e", p=128),
                            waits=w_war, inc=s_wb[0])
                tw1 = p.dma("gpsimd", wbuf[1][:].rearrange("p (k e) -> p k e", k=8),
                            wo[1024:2048, dq * 512:(dq + 1) * 512].rearrange("(k p) e -> p k e", p=128),
                            inc=s_wb[1])
                for tc in range(8):
                    bi = 2 + tc % 2
                    mms = []
                    for k in range(16):
                        mms.append(partial(
                            lambda t, k_=0, tc_=0, bi_=0: t.matmul(
                                pb[bi_][:, :],
                                xnw[:, k_ * 1024 + tc_ * 128: k_ * 1024 + (tc_ + 1) * 128],
                                wbuf[k_ // 8][:, (k_ % 8) * 512:(k_ % 8 + 1) * 512],
                                start=(k_ == 0), stop=(k_ == 15)),
                            k_=k, tc_=tc, bi_=bi))
                    tok = mm_group(bi, mms, waits_first=[(s_wb[0], tw0), (s_wb[1], tw1), (s_xw, t_zw)])
                    fo = tc * D + dq * 512
                    ev = p.op("vector", partial(
                        lambda v, bi_=0, fo_=0: v.tensor_copy(facc[:, fo_:fo_ + 512], pb[bi_][:, :]),
                        bi_=bi, fo_=fo), waits=[(s_mm, tok)], inc=s_v)
                    bank_consumed(bi, s_v, ev)
            for tc in range(8):
                xsl = tc % 2
                x01_tok[xsl] = p.dma("sync", xq[xsl][:],
                                     x01_d[t0 + tc * 128: t0 + (tc + 1) * 128, :],
                                     waits=[(s_v, s_v.n), (s_x01w, X01_ALL),
                                            (s_x1s[xsl], prev_x1[xsl])], inc=s_ld[2 + xsl])
                xb = xq2[xsl]
                p.op("vector", partial(lambda v, xb_=None, xsl_=0, tc_=0: v.tensor_tensor(
                    xb_[:], xq[xsl_][:], facc[:, tc_ * D:(tc_ + 1) * D], ALU.add),
                    xb_=xb, xsl_=xsl, tc_=tc),
                    waits=[(s_ld[2 + xsl], x01_tok[xsl]), (s_x1s[xsl], prev_x1[xsl])], inc=s_v)
                p.op("vector", lambda v: v.drain(), inc=s_v)
                p.op("vector", partial(lambda v, xb_=None: v.reduce_sum(
                    stat[:, 0:1], xb_[:], AX.X), xb_=xb), inc=s_v)
                p.op("vector", lambda v: v.drain(), inc=s_v)
                sx_done = s_v.n
                p.op("scalar", partial(lambda s, xb_=None: s.activation(
                    xn1_sb, xb_[:], AF.Square, accum_out=stat[:, 1:2]), xb_=xb),
                    waits=[(s_v, sx_done), (s_mm, prev_z)], inc=s_sc)
                sq_done = s_sc.n
                p.op("scalar", lambda s: s.mul(stat[:, 2:3], stat[:, 0:1], 1.0 / D), inc=s_sc)
                mu_done = s_sc.n
                p.op("vector", lambda v: v.tensor_scalar_mul(stat[:, 3:4], stat[:, 1:2], 1.0 / D),
                     waits=[(s_sc, sq_done)], inc=s_v)
                p.op("vector", lambda v: v.tensor_tensor(stat[:, 4:5], stat[:, 2:3], stat[:, 2:3],
                                                         ALU.mult), waits=[(s_sc, mu_done)], inc=s_v)
                p.op("vector", lambda v: v.tensor_tensor(stat[:, 5:6], stat[:, 3:4], stat[:, 4:5],
                                                         ALU.subtract), inc=s_v)
                p.op("vector", lambda v: v.drain(), inc=s_v)
                var_done = s_v.n
                p.op("scalar", lambda s: s.activation(stat[:, 6:7], stat[:, 5:6], AF.Ln,
                                                      bias=constf[:, 0:1]),
                     waits=[(s_v, var_done), (s_const, CONSTS)], inc=s_sc)
                p.op("scalar", lambda s: s.activation(stat[:, 7:8], stat[:, 6:7], AF.Exp,
                                                      scale=-0.5), inc=s_sc)
                rstd_done = s_sc.n
                p.op("vector", partial(lambda v, xb_=None: v.tensor_scalar(
                    xn1_sb, xb_[:], stat[:, 2:3], stat[:, 7:8], ALU.subtract, ALU.mult), xb_=xb),
                    waits=[(s_sc, rstd_done)], inc=s_v)
                p.op("vector", lambda v: v.drain(), inc=s_v)
                xn_done = s_v.n
                prev_x1[xsl] = p.dma("sync", x1_o[t0 + tc * 128: t0 + (tc + 1) * 128, :],
                                     xb[:], waits=[(s_v, xn_done)], inc=s_x1s[xsl])
                for k in range(16):
                    bi2 = 4 + k % 2
                    tok2 = mm_group(bi2, [partial(
                        lambda t, k_=0, bi_=0: t.transpose(
                            pb[bi_][:, 0:64].bitcast(BF16),
                            xn1_sb[:, k_ * 128:(k_ + 1) * 128], ident[:]),
                        k_=k, bi_=bi2)], waits_first=[(s_v, xn_done), (s_const, CONSTS)])
                    eng, sem = ("vector", s_v) if k % 2 == 0 else ("scalar", s_sc)
                    wv_ = [(s_mm, tok2)]
                    if k < 2:
                        wv_.append((s_trd, prev_tr))
                    fn = (partial(lambda v, k_=0, bi_=0: v.tensor_copy(
                            trst[:, k_ * 128:(k_ + 1) * 128], pb[bi_][:, 0:64].bitcast(BF16)),
                            k_=k, bi_=bi2)
                          if eng == "vector" else
                          partial(lambda s, k_=0, bi_=0: s.copy(
                            trst[:, k_ * 128:(k_ + 1) * 128], pb[bi_][:, 0:64].bitcast(BF16)),
                            k_=k, bi_=bi2))
                    ev2 = p.op(eng, fn, waits=wv_, inc=sem)
                    bank_consumed(bi2, sem, ev2)
                prev_z = s_mm.n
                prev_tr = p.dma("sync",
                                xn1T_d[:, :].rearrange("(k p) t -> p k t", p=128)
                                [:, :, t0 + tc * 128: t0 + (tc + 1) * 128],
                                trst.rearrange("p (k j) -> p k j", k=16),
                                waits=[(s_v, s_v.n), (s_sc, s_sc.n)], inc=s_trd)

        X1_ALL = [0, 0]
        x1pf_tok = 0
        relu_rd = [0, 0]
        down_prev = 0
        for tsl in range(2):
            t0 = tsl * 1024
            X1_ALL = [(s_x1s[0], s_x1s[0].n), (s_x1s[1], s_x1s[1].n)]
            t_xw = p.dma("sync", xnw[:].rearrange("p (k t) -> p k t", k=16),
                         xn1T_d[:, t0:t0 + 1024].rearrange("(k p) t -> p k t", p=128),
                         waits=[(s_mm, s_mm.n), (s_trd, s_trd.n)], inc=s_xw)

            def proj_qk(colbase, out_dram, bias_col0):
                for eh in range(4):
                    w_war = [(s_mm, s_mm.n)]
                    tw0 = p.dma("gpsimd", wbuf[0][:].rearrange("p (k e) -> p k e", k=8),
                                wqkvT[0:1024, colbase + eh * 512: colbase + (eh + 1) * 512]
                                .rearrange("(k p) e -> p k e", p=128), waits=w_war, inc=s_wb[0])
                    tw1 = p.dma("gpsimd", wbuf[1][:].rearrange("p (k e) -> p k e", k=8),
                                wqkvT[1024:2048, colbase + eh * 512: colbase + (eh + 1) * 512]
                                .rearrange("(k p) e -> p k e", p=128), inc=s_wb[1])
                    for ec in range(4):
                        for ts in range(2):
                            bi = 2 + (ec * 2 + ts) % 2
                            mms = []
                            for k in range(16):
                                mms.append(partial(
                                    lambda t, k_=0, ec_=0, ts_=0, bi_=0: t.matmul(
                                        pb[bi_][:, :],
                                        wbuf[k_ // 8][:, (k_ % 8) * 512 + ec_ * 128:
                                                      (k_ % 8) * 512 + (ec_ + 1) * 128],
                                        xnw[:, k_ * 1024 + ts_ * 512: k_ * 1024 + (ts_ + 1) * 512],
                                        start=(k_ == 0), stop=(k_ == 15)),
                                    k_=k, ec_=ec, ts_=ts, bi_=bi))
                            tok = mm_group(bi, mms, waits_first=[(s_wb[0], tw0), (s_wb[1], tw1),
                                                                 (s_xw, t_xw)])
                            si = (ec * 2 + ts) % 4
                            ecg = eh * 4 + ec
                            w_stg = use_tok(stg_tok, si) + [(s_mm, tok), (s_const, CONSTS)]
                            ev = p.op("vector", partial(
                                lambda v, bi_=0, si_=0, bc_=0: v.tensor_scalar(
                                    stg[si_][:], pb[bi_][:, :], bqkv_sb[:, bc_:bc_ + 1], None, ALU.add),
                                bi_=bi, si_=si, bc_=bias_col0 + ecg), waits=w_stg, inc=s_v)
                            bank_consumed(bi, s_v, ev)
                            dt = p.dma("sync",
                                       out_dram[eh * 512 + ec * 128: eh * 512 + (ec + 1) * 128,
                                                t0 + ts * 512: t0 + (ts + 1) * 512],
                                       stg[si][:], waits=[(s_v, ev)], inc=s_stgd[si])
                            stg_tok[si] = (s_stgd[si], dt)

            proj_qk(0, q1T_o, 0)
            proj_qk(2048, k1T_o, 16)

            for dq in range(4):
                w_war = [(s_mm, s_mm.n)]
                tw0 = p.dma("gpsimd", wbuf[0][:].rearrange("p (k e) -> p k e", k=8),
                            wqkvT[0:1024, 4096 + dq * 512: 4096 + (dq + 1) * 512]
                            .rearrange("(k p) e -> p k e", p=128), waits=w_war, inc=s_wb[0])
                tw1 = p.dma("gpsimd", wbuf[1][:].rearrange("p (k e) -> p k e", k=8),
                            wqkvT[1024:2048, 4096 + dq * 512: 4096 + (dq + 1) * 512]
                            .rearrange("(k p) e -> p k e", p=128), inc=s_wb[1])
                for tc in range(8):
                    bi = 2 + tc % 2
                    mms = []
                    for k in range(16):
                        mms.append(partial(
                            lambda t, k_=0, tc_=0, bi_=0: t.matmul(
                                pb[bi_][:, :],
                                xnw[:, k_ * 1024 + tc_ * 128: k_ * 1024 + (tc_ + 1) * 128],
                                wbuf[k_ // 8][:, (k_ % 8) * 512:(k_ % 8 + 1) * 512],
                                start=(k_ == 0), stop=False),
                            k_=k, tc_=tc, bi_=bi))
                    mms.append(partial(
                        lambda t, bi_=0, dq_=0: t.matmul(
                            pb[bi_][:, :], ones1[:], bv_sb[0:1, dq_ * 512:(dq_ + 1) * 512],
                            start=False, stop=True), bi_=bi, dq_=dq))
                    tok = mm_group(bi, mms, waits_first=[(s_wb[0], tw0), (s_wb[1], tw1), (s_xw, t_xw),
                                                         (s_const, CONSTS), (s_constg, CONSTG)])
                    si = tc % 4
                    w_stg = use_tok(stg_tok, si) + [(s_mm, tok)]
                    ev = p.op("vector", partial(
                        lambda v, bi_=0, si_=0: v.tensor_copy(stg[si_][:], pb[bi_][:, :]),
                        bi_=bi, si_=si), waits=w_stg, inc=s_v)
                    bank_consumed(bi, s_v, ev)
                    dt = p.dma("sync", v1_o[t0 + tc * 128: t0 + (tc + 1) * 128, dq * 512:(dq + 1) * 512],
                               stg[si][:], waits=[(s_v, ev)], inc=s_stgd[si])
                    stg_tok[si] = (s_stgd[si], dt)

            for dq in range(4):
                w_war = [(s_mm, s_mm.n)]
                tw0 = p.dma("gpsimd", wbuf[0][:].rearrange("p (k e) -> p k e", k=8),
                            expw[0:1024, dq * 512:(dq + 1) * 512].rearrange("(k p) e -> p k e", p=128),
                            waits=w_war, inc=s_wb[0])
                tw1 = p.dma("gpsimd", wbuf[1][:].rearrange("p (k e) -> p k e", k=8),
                            expw[1024:2048, dq * 512:(dq + 1) * 512].rearrange("(k p) e -> p k e", p=128),
                            inc=s_wb[1])
                for tc in range(8):
                    if dq == 0:
                        mmsl = [partial(
                            lambda t, k_=0, tc_=0: t.matmul(
                                pb[6][:, 0:8],
                                xnw[:, k_ * 1024 + tc_ * 128: k_ * 1024 + (tc_ + 1) * 128],
                                gate1_sb[:, k_ * 8:(k_ + 1) * 8], start=(k_ == 0), stop=(k_ == 15)),
                            k_=k, tc_=tc) for k in range(16)]
                        tokl = mm_group(6, mmsl, waits_first=[(s_xw, t_xw), (s_constg, CONSTG)])
                        ev_mx = p.op("vector", lambda v: v.max(t2_sb[:], pb[6][:, 0:8]),
                                     waits=[(s_mm, tokl)], inc=s_v)
                        bank_consumed(6, s_v, ev_mx)
                        p.op("vector", lambda v: v.drain(), inc=s_v)
                        mx_done = s_v.n
                        p.op("scalar", partial(lambda s, tc_=0: s.activation(
                            sc2_sb[:, 0:2], t2_sb[:, 0:2], AF.Copy, accum_out=stat[:, 8 + tc_:9 + tc_]),
                            tc_=tc), waits=[(s_v, mx_done)], inc=s_sc)
                    scale_done = s_sc.n
                    bi = 2 + tc % 2
                    mms = []
                    for k in range(16):
                        mms.append(partial(
                            lambda t, k_=0, tc_=0, bi_=0: t.matmul(
                                pb[bi_][:, :],
                                xnw[:, k_ * 1024 + tc_ * 128: k_ * 1024 + (tc_ + 1) * 128],
                                wbuf[k_ // 8][:, (k_ % 8) * 512:(k_ % 8 + 1) * 512],
                                start=(k_ == 0), stop=(k_ == 15)),
                            k_=k, tc_=tc, bi_=bi))
                    tok = mm_group(bi, mms, waits_first=[(s_wb[0], tw0), (s_wb[1], tw1), (s_xw, t_xw)])
                    fo = tc * D + dq * 512
                    ev = p.op("vector", partial(
                        lambda v, bi_=0, fo_=0, tc_=0: v.tensor_scalar_mul(
                            facc[:, fo_:fo_ + 512], pb[bi_][:, :], stat[:, 8 + tc_:9 + tc_]),
                        bi_=bi, fo_=fo, tc_=tc),
                        waits=[(s_mm, tok), (s_sc, scale_done), (s_pfw, x1pf_tok)], inc=s_v)
                    bank_consumed(bi, s_v, ev)

            for ig in range(8):
                i0 = ig * 1024
                w_war = [(s_mm, s_mm.n)]
                twu0 = p.dma("gpsimd", wbuf[0][:].rearrange("p (k e) -> p k e", k=8),
                             upw[0:1024, i0:i0 + 512].rearrange("(k p) e -> p k e", p=128),
                             waits=w_war, inc=s_wb[0])
                twu1 = p.dma("gpsimd", wbuf[1][:].rearrange("p (k e) -> p k e", k=8),
                             upw[1024:2048, i0:i0 + 512].rearrange("(k p) e -> p k e", p=128),
                             inc=s_wb[1])
                twu2 = p.dma("gpsimd", wbuf[2][:].rearrange("p (k e) -> p k e", k=8),
                             upw[0:1024, i0 + 512:i0 + 1024].rearrange("(k p) e -> p k e", p=128),
                             inc=s_wb[2])
                twu3 = p.dma("gpsimd", wbuf[3][:].rearrange("p (k e) -> p k e", k=8),
                             upw[1024:2048, i0 + 512:i0 + 1024].rearrange("(k p) e -> p k e", p=128),
                             inc=s_wb[3])
                for ih in range(2):
                    wlo, whi = (0, 1) if ih == 0 else (2, 3)
                    tlo = twu0 if ih == 0 else twu2
                    thi = twu1 if ih == 0 else twu3
                    for ic in range(4):
                        for tw in range(2):
                            bi = ic % 2
                            mms = []
                            for k in range(16):
                                mms.append(partial(
                                    lambda t, k_=0, ic_=0, tw_=0, lo_=0, hi_=0, bi_=0: t.matmul(
                                        pb[bi_][:, :],
                                        wbuf[lo_ if k_ < 8 else hi_]
                                        [:, (k_ % 8) * 512 + ic_ * 128:(k_ % 8) * 512 + (ic_ + 1) * 128],
                                        xnw[:, k_ * 1024 + tw_ * 512: k_ * 1024 + (tw_ + 1) * 512],
                                        start=(k_ == 0), stop=(k_ == 15)),
                                    k_=k, ic_=ic, tw_=tw, lo_=wlo, hi_=whi, bi_=bi))
                            tok = mm_group(bi, mms, waits_first=[(s_wb[wlo], tlo), (s_wb[whi], thi),
                                                                 (s_xw, t_xw)])
                            rsl = tw % 2
                            ev_r = p.op("scalar", partial(
                                lambda s, bi_=0, rsl_=0: s.activation(relu_sb[rsl_][:], pb[bi_][:, :],
                                                                      AF.Relu), bi_=bi, rsl_=rsl),
                                waits=[(s_mm, tok), (s_v, relu_rd[rsl])], inc=s_sc)
                            bank_consumed(bi, s_sc, ev_r)
                            mo = (ih * 4 + ic) * 1024 + tw * 512
                            ev_m = p.op("vector", partial(
                                lambda v, rsl_=0, mo_=0: v.tensor_tensor(
                                    m1g[:, mo_:mo_ + 512], relu_sb[rsl_][:], relu_sb[rsl_][:], ALU.mult),
                                rsl_=rsl, mo_=mo),
                                waits=[(s_sc, ev_r), (s_mm, down_prev)], inc=s_v)
                            relu_rd[rsl] = ev_m
                p.op("vector", lambda v: v.drain(), inc=s_v)
                m_done = s_v.n
                for dq in range(4):
                    wsl = dq % 2
                    twd = p.dma("gpsimd", wbuf[wsl][:].rearrange("p (i e) -> p i e", i=8),
                                downw[i0:i0 + 1024, dq * 512:(dq + 1) * 512]
                                .rearrange("(i p) e -> p i e", p=128),
                                waits=[(s_mm, s_mm.n)], inc=s_wb[wsl])
                    for tc in range(8):
                        bi = 2 + tc % 4
                        mms = []
                        for ic in range(8):
                            mms.append(partial(
                                lambda t, ic_=0, tc_=0, bi_=0, wsl_=0: t.matmul(
                                    pb[bi_][:, :],
                                    m1g[:, ic_ * 1024 + tc_ * 128: ic_ * 1024 + (tc_ + 1) * 128],
                                    wbuf[wsl_][:, ic_ * 512:(ic_ + 1) * 512],
                                    start=(ic_ == 0), stop=(ic_ == 7)),
                                ic_=ic, tc_=tc, bi_=bi, wsl_=wsl))
                        tok = mm_group(bi, mms, waits_first=[(s_wb[wsl], twd), (s_v, m_done)])
                        fo = tc * D + dq * 512
                        ev = p.op("vector", partial(
                            lambda v, bi_=0, fo_=0: v.tensor_tensor(
                                facc[:, fo_:fo_ + 512], facc[:, fo_:fo_ + 512], pb[bi_][:, :], ALU.add),
                            bi_=bi, fo_=fo), waits=[(s_mm, tok)], inc=s_v)
                        bank_consumed(bi, s_v, ev)
                down_prev = s_mm.n

            for tc in range(8):
                xsl = tc % 2
                t_x1 = p.dma("sync", xq[xsl][:], x1_o[t0 + tc * 128: t0 + (tc + 1) * 128, :],
                             waits=[(s_v, s_v.n)] + X1_ALL, inc=s_ld[2 + xsl])
                fb = partial(lambda tc_=0: facc[:, tc_ * D:(tc_ + 1) * D], tc_=tc)
                p.op("vector", partial(lambda v, fb_=None: v.reduce_sum(stat[:, 0:1], fb_(), AX.X),
                                       fb_=fb), inc=s_v)
                p.op("vector", lambda v: v.drain(), inc=s_v)
                sx_done = s_v.n
                p.op("scalar", partial(lambda s, fb_=None: s.activation(
                    xn1_sb, fb_(), AF.Square, accum_out=stat[:, 1:2]), fb_=fb),
                    waits=[(s_v, sx_done)], inc=s_sc)
                sq_done = s_sc.n
                p.op("scalar", lambda s: s.mul(stat[:, 2:3], stat[:, 0:1], 1.0 / D), inc=s_sc)
                mu_done = s_sc.n
                p.op("vector", lambda v: v.tensor_scalar_mul(stat[:, 3:4], stat[:, 1:2], 1.0 / D),
                     waits=[(s_sc, sq_done)], inc=s_v)
                p.op("vector", lambda v: v.tensor_tensor(stat[:, 4:5], stat[:, 2:3], stat[:, 2:3],
                                                         ALU.mult), waits=[(s_sc, mu_done)], inc=s_v)
                p.op("vector", lambda v: v.tensor_tensor(stat[:, 5:6], stat[:, 3:4], stat[:, 4:5],
                                                         ALU.subtract), inc=s_v)
                p.op("vector", lambda v: v.drain(), inc=s_v)
                var_done = s_v.n
                p.op("scalar", lambda s: s.activation(stat[:, 6:7], stat[:, 5:6], AF.Ln,
                                                      bias=constf[:, 0:1]),
                     waits=[(s_v, var_done), (s_const, CONSTS)], inc=s_sc)
                p.op("scalar", lambda s: s.activation(stat[:, 7:8], stat[:, 6:7], AF.Exp, scale=-0.5),
                     inc=s_sc)
                rstd_done = s_sc.n
                p.op("vector", partial(lambda v, fb_=None: v.tensor_scalar(
                    fb_(), fb_(), stat[:, 2:3], stat[:, 7:8], ALU.subtract, ALU.mult), fb_=fb),
                    waits=[(s_sc, rstd_done)], inc=s_v)
                p.op("vector", partial(lambda v, fb_=None: v.tensor_tensor(fb_(), fb_(), gb_sb,
                                                                           ALU.mult), fb_=fb), inc=s_v)
                p.op("vector", partial(lambda v, fb_=None: v.tensor_tensor(fb_(), fb_(), bb_sb,
                                                                           ALU.add), fb_=fb), inc=s_v)
                p.op("vector", partial(lambda v, fb_=None, xsl_=0: v.tensor_tensor(
                    fb_(), fb_(), xq[xsl_][:], ALU.add), fb_=fb, xsl_=xsl),
                    waits=[(s_ld[2 + xsl], t_x1)], inc=s_v)
                p.op("vector", lambda v: v.drain(), inc=s_v)
                p.dma("sync", x1pf_o[t0 + tc * 128: t0 + (tc + 1) * 128, :], fb(),
                      waits=[(s_v, s_v.n)], inc=s_pfw)
            x1pf_tok = s_pfw.n

        fin = ([(s, s.n) for s in s_stgd] + [(s_pfw, s_pfw.n), (s_x1s[0], s_x1s[0].n),
                (s_x1s[1], s_x1s[1].n), (s_x01w, s_x01w.n), (s_trd, s_trd.n)])
        p.op("sync", lambda sy: sy.nop(), waits=fin)

    p.emit()
    return nc, p.n_ops()



def build_phase_d():
    nc = bass.Bass("TRN2", target_bir_lowering=False)
    ctx = ExitStack()
    p = Prog(nc, ctx)

    NU = 8
    SQ = 4096

    qT = nc.dram_tensor("qTu", [NU * 128, SQ], BF16, kind="ExternalInput")
    kT = nc.dram_tensor("kTu", [NU * 128, SQ], BF16, kind="ExternalInput")
    vt = nc.dram_tensor("vtu", [SQ, NU * 128], BF16, kind="ExternalInput")
    ones1_d = nc.dram_tensor("ones1", [1, 128], BF16, kind="ExternalInput")
    onescol_d = nc.dram_tensor("onescol", [128, 1], BF16, kind="ExternalInput")
    dmask_d = nc.dram_tensor("dmask", [128, 4 * 512], BF16, kind="ExternalInput")
    o1T_o = nc.dram_tensor("o1T", [NU * 128, SQ], BF16, kind="ExternalOutput")
    dbgd = nc.dram_tensor("dbgd", [128, 1064], F32, kind="ExternalOutput")

    with ctx:
        e = ctx.enter_context
        ones1 = e(nc.sbuf_tensor("ones1sb", [1, 128], BF16))
        onescol = e(nc.sbuf_tensor("onescolsb", [128, 1], BF16))
        dmask = e(nc.sbuf_tensor("dmasksb", [128, 4 * 512], BF16))
        qsl_ = [e(nc.sbuf_tensor(f"qsl{i}", [128, SQ], BF16)) for i in range(2)]
        ksl_ = [e(nc.sbuf_tensor(f"ksl{i}", [128, SQ], BF16)) for i in range(2)]
        vsl_ = [e(nc.sbuf_tensor(f"vsl{i}", [128, 32 * 128], BF16)) for i in range(2)]
        sq_sb = e(nc.sbuf_tensor("sqsb", [128, SQ], BF16))
        nrm_sb = e(nc.sbuf_tensor("nrmsb", [1, SQ], F32))
        mstat = e(nc.sbuf_tensor("mstat", [1, 8], F32))
        bias_col = e(nc.sbuf_tensor("biascol", [128, 1], F32))
        bias4_col = e(nc.sbuf_tensor("bias4col", [128, 1], F32))
        bias2_col = e(nc.sbuf_tensor("bias2col", [128, 1], F32))
        pT = [e(nc.sbuf_tensor(f"pT{i}", [128, 512], BF16)) for i in range(2)]
        rl_sb = e(nc.sbuf_tensor("rlsb", [1, 512], F32))
        rl_bf = e(nc.sbuf_tensor("rlbf", [1, 512], BF16))
        rlb_sb = e(nc.sbuf_tensor("rlbsb", [128, 512], BF16))
        ost = [e(nc.sbuf_tensor(f"ostg{i}", [128, 512], BF16)) for i in range(2)]

        pb = [e(nc.psum_tensor(f"pb{i}", [128, 512], F32)) for i in range(8)]

        s_const = p.sem("const")
        s_u = [p.sem(f"u{i}") for i in range(2)]
        s_mm = p.sem("mm")
        s_v = p.sem("v")
        s_sc = p.sem("sc")
        s_g = p.sem("g")
        s_od = [p.sem(f"od{i}") for i in range(2)]
        s_dbg = p.sem("dbg")
        p.auto = {"vector": s_v, "scalar": s_sc, "gpsimd": s_g}

        bank_writer = [0] * 8
        bank_reader = [0] * 8
        bank_rsem = [None] * 8

        def mm_one(bank_i, fn, waits=(), close=True, first=True):
            w = list(waits)
            if first and bank_rsem[bank_i] is not None:
                w.append((bank_rsem[bank_i], bank_reader[bank_i]))
            tok = p.op("tensor", fn, waits=w, inc=(s_mm if close else None))
            if close:
                bank_writer[bank_i] = s_mm.n
            return s_mm.n

        def bank_consumed(bank_i, sem, val):
            bank_rsem[bank_i] = sem
            bank_reader[bank_i] = val

        p.dma("sync", ones1[:], ones1_d[:, :], inc=s_const)
        p.dma("sync", onescol[:], onescol_d[:, :], inc=s_const)
        p.dma("sync", dmask[:], dmask_d[:, :], inc=s_const)
        CONSTS = s_const.n

        unit_tok = [0, 0]
        unit_free = [0, 0]
        ost_tok = [0, 0]

        def load_unit(u):
            sl = u % 2
            w = [(s_mm, unit_free[sl])]
            p.dma("sync", qsl_[sl][:], qT[u * 128:(u + 1) * 128, :], waits=w, inc=s_u[sl])
            p.dma("sync", ksl_[sl][:], kT[u * 128:(u + 1) * 128, :], inc=s_u[sl])
            tv = p.dma("sync", vsl_[sl][:].rearrange("p (n d) -> p n d", n=32),
                       vt[:, u * 128:(u + 1) * 128].rearrange("(n p) d -> p n d", p=128),
                       inc=s_u[sl])
            unit_tok[sl] = tv
        load_unit(0)

        for u in range(NU):
            sl = u % 2
            t_u = unit_tok[sl]
            if u + 1 < NU:
                load_unit(u + 1)
            for which in range(2):
                slab = qsl_[sl] if which == 0 else ksl_[sl]
                p.op("scalar", partial(lambda s, sb_=None: s.activation(sq_sb[:], sb_[:], AF.Square),
                                       sb_=slab),
                     waits=[(s_u[sl], t_u), (s_mm, s_mm.n), (s_v, s_v.n)], inc=s_sc)
                sq_done = s_sc.n
                for j in range(8):
                    bi = 2 + j % 2
                    tok = mm_one(bi, partial(
                        lambda t, j_=0, bi_=0: t.matmul(
                            pb[bi_][0:1, :], onescol[:], sq_sb[:, j_ * 512:(j_ + 1) * 512],
                            start=True, stop=True), j_=j, bi_=bi),
                        waits=[(s_sc, sq_done), (s_const, CONSTS)])
                    ev = p.op("vector", partial(
                        lambda v, j_=0, bi_=0: v.tensor_copy(nrm_sb[0:1, j_ * 512:(j_ + 1) * 512],
                                                             pb[bi_][0:1, :]), j_=j, bi_=bi),
                        waits=[(s_mm, tok)], inc=s_v)
                    bank_consumed(bi, s_v, ev)
                p.op("vector", lambda v: v.drain(), inc=s_v)
                p.op("vector", partial(lambda v, w_=0: v.reduce_max(
                    mstat[0:1, w_:w_ + 1], nrm_sb[0:1, :], AX.X), w_=which), inc=s_v)
            p.op("vector", lambda v: v.tensor_tensor(mstat[0:1, 2:3], mstat[0:1, 0:1],
                                                     mstat[0:1, 1:2], ALU.add), inc=s_v)
            p.op("vector", lambda v: v.drain(), inc=s_v)
            m_done = s_v.n
            p.op("scalar", lambda s: s.copy(rl_bf[0:1, 0:1], mstat[0:1, 2:3]),
                 waits=[(s_v, m_done)], inc=s_sc)
            mb_done = s_sc.n
            tokb = mm_one(7, lambda t: t.matmul(pb[7][:, 0:1], ones1[:], rl_bf[0:1, 0:1],
                                                start=True, stop=True),
                          waits=[(s_sc, mb_done), (s_const, CONSTS)])
            ev_b = p.op("vector", lambda v: v.tensor_scalar_mul(
                bias_col[:], pb[7][:, 0:1], -0.5 / math.sqrt(128.0)),
                waits=[(s_mm, tokb)], inc=s_v)
            bank_consumed(7, s_v, ev_b)
            p.op("vector", lambda v: v.drain(), inc=s_v)
            bias_done = s_v.n

            pconsumed = [0, 0]
            dbg_tok = 0
            p.op("vector", lambda v: v.tensor_scalar_mul(mstat[0:1, 3:4], mstat[0:1, 2:3],
                                                         0.5 / math.sqrt(128.0)),
                 waits=[(s_v, m_done)], inc=s_v)
            p.op("vector", lambda v: v.tensor_scalar_mul(bias4_col[:], bias_col[:], 0.25),
                 waits=[(s_v, bias_done)], inc=s_v)
            p.op("vector", lambda v: v.drain(), inc=s_v)
            bias4_done = s_v.n
            for iq in range(8):
                nkc = 4 * (iq + 1)
                bo = 4 + iq % 2
                for kc in range(nkc):
                    bs = kc % 2
                    psl = kc % 2
                    tok_s = mm_one(bs, partial(
                        lambda t, kc_=0, iq_=0, bs_=0, sl_=0: t.matmul(
                            pb[bs_][:, :], ksl_[sl_][:, kc_ * 128:(kc_ + 1) * 128],
                            qsl_[sl_][:, iq_ * 512:(iq_ + 1) * 512],
                            start=True, stop=True), kc_=kc, iq_=iq, bs_=bs, sl_=sl),
                        waits=[(s_u[sl], t_u)])
                    ev_e = p.op("scalar", partial(
                        lambda s, bs_=0, psl_=0: s.activation(
                            pT[psl_][:], pb[bs_][:, :], AF.Exp, bias=bias4_col[:, 0:1],
                            scale=0.25 / math.sqrt(128.0)), bs_=bs, psl_=psl),
                        waits=[(s_mm, tok_s), (s_v, bias4_done), (s_mm, pconsumed[psl])], inc=s_sc)
                    bank_consumed(bs, s_sc, ev_e)
                    rsem, rval = s_sc, ev_e
                    j = kc - 4 * iq
                    if j >= 0:
                        ev_mk = p.op("vector", partial(
                            lambda v, psl_=0, j_=0: v.tensor_tensor(
                                pT[psl_][:], pT[psl_][:], dmask[:, j_ * 512:(j_ + 1) * 512], ALU.mult),
                            psl_=psl, j_=j),
                            waits=[(s_sc, ev_e), (s_const, CONSTS)], inc=s_v)
                        p.op("vector", lambda v: v.drain(), inc=s_v)
                        rsem, rval = s_v, s_v.n
                    wl = [(rsem, rval)]
                    if kc == 0 and bank_rsem[6] is not None:
                        wl.append((bank_rsem[6], bank_reader[6]))
                    p.op("tensor", partial(
                        lambda t, kc_=0, last_=False: t.matmul(
                            pb[6][0:1, :], onescol[:], pT[kc_ % 2][:],
                            start=(kc_ == 0), stop=last_), kc_=kc, last_=(kc == nkc - 1)),
                        waits=wl, inc=s_mm)
                    if kc == nkc - 1:
                        bank_writer[6] = s_mm.n
                    pconsumed[psl] = s_mm.n
                l4tok = s_mm.n
                p.op("scalar", lambda s: s.activation(rl_sb[0:1, :], pb[6][0:1, :], AF.Ln),
                     waits=[(s_mm, l4tok)], inc=s_sc)
                bank_consumed(6, s_sc, s_sc.n)
                ln_done = s_sc.n
                p.op("vector", lambda v: v.tensor_scalar(rl_sb[0:1, :], rl_sb[0:1, :], 4.0,
                                                         mstat[0:1, 3:4], ALU.mult, ALU.add),
                     waits=[(s_sc, ln_done)], inc=s_v)
                p.op("vector", lambda v: v.reduce_max(mstat[0:1, 4:5], rl_sb[0:1, :], AX.X), inc=s_v)
                p.op("vector", lambda v: v.drain(), inc=s_v)
                mn_done = s_v.n
                p.op("scalar", lambda s: s.copy(rl_bf[0:1, 0:1], mstat[0:1, 4:5]),
                     waits=[(s_v, mn_done)], inc=s_sc)
                mn2_done = s_sc.n
                bb2 = 2 + iq % 2
                tokb2 = mm_one(bb2, partial(
                    lambda t, b_=0: t.matmul(pb[b_][:, 0:1], ones1[:], rl_bf[0:1, 0:1],
                                             start=True, stop=True), b_=bb2),
                    waits=[(s_sc, mn2_done), (s_const, CONSTS)])
                ev_b2 = p.op("vector", partial(
                    lambda v, b_=0: v.tensor_scalar_mul(bias2_col[:], pb[b_][:, 0:1], -1.0), b_=bb2),
                    waits=[(s_mm, tokb2)], inc=s_v)
                bank_consumed(bb2, s_v, ev_b2)
                p.op("vector", lambda v: v.drain(), inc=s_v)
                bias2_done = s_v.n
                for kc in range(nkc):
                    bs = kc % 2
                    psl = kc % 2
                    tok_s = mm_one(bs, partial(
                        lambda t, kc_=0, iq_=0, bs_=0, sl_=0: t.matmul(
                            pb[bs_][:, :], ksl_[sl_][:, kc_ * 128:(kc_ + 1) * 128],
                            qsl_[sl_][:, iq_ * 512:(iq_ + 1) * 512],
                            start=True, stop=True), kc_=kc, iq_=iq, bs_=bs, sl_=sl),
                        waits=[(s_u[sl], t_u)])
                    ev_e = p.op("scalar", partial(
                        lambda s, bs_=0, psl_=0: s.activation(
                            pT[psl_][:], pb[bs_][:, :], AF.Exp, bias=bias2_col[:, 0:1],
                            scale=1.0 / math.sqrt(128.0)), bs_=bs, psl_=psl),
                        waits=[(s_mm, tok_s), (s_v, bias2_done), (s_mm, pconsumed[psl])], inc=s_sc)
                    bank_consumed(bs, s_sc, ev_e)
                    rsem, rval = s_sc, ev_e
                    j = kc - 4 * iq
                    if j >= 0:
                        ev_mk = p.op("vector", partial(
                            lambda v, psl_=0, j_=0: v.tensor_tensor(
                                pT[psl_][:], pT[psl_][:], dmask[:, j_ * 512:(j_ + 1) * 512], ALU.mult),
                            psl_=psl, j_=j),
                            waits=[(s_sc, ev_e), (s_const, CONSTS)], inc=s_v)
                        p.op("vector", lambda v: v.drain(), inc=s_v)
                        rsem, rval = s_v, s_v.n
                    wl = [(rsem, rval)]
                    if kc == 0 and bank_rsem[6] is not None:
                        wl.append((bank_rsem[6], bank_reader[6]))
                    p.op("tensor", partial(
                        lambda t, kc_=0, last_=False: t.matmul(
                            pb[6][0:1, :], onescol[:], pT[kc_ % 2][:],
                            start=(kc_ == 0), stop=last_), kc_=kc, last_=(kc == nkc - 1)),
                        waits=wl, inc=(s_mm if kc == nkc - 1 else None))
                    if kc == nkc - 1:
                        bank_writer[6] = s_mm.n
                    wo_ = []
                    if kc == 0 and bank_rsem[bo] is not None:
                        wo_.append((bank_rsem[bo], bank_reader[bo]))
                    tok_pv = p.op("tensor", partial(
                        lambda t, kc_=0, bo_=0, last_=False, sl_=0: t.matmul(
                            pb[bo_][:, :], vsl_[sl_][:, kc_ * 128:(kc_ + 1) * 128], pT[kc_ % 2][:],
                            start=(kc_ == 0), stop=last_), kc_=kc, bo_=bo, last_=(kc == nkc - 1), sl_=sl),
                        waits=wo_, inc=s_mm)
                    if kc == nkc - 1:
                        bank_writer[bo] = s_mm.n
                    pconsumed[psl] = s_mm.n
                otok = s_mm.n
                p.op("vector", lambda v: v.tensor_scalar_add(rl_sb[0:1, :], pb[6][0:1, :], 1e-30),
                     waits=[(s_mm, otok), (s_dbg, dbg_tok)], inc=s_v)
                ev_l = p.op("vector", lambda v: v.reciprocal(rl_sb[0:1, :], rl_sb[0:1, :]), inc=s_v)
                bank_consumed(6, s_v, ev_l)
                p.op("vector", lambda v: v.drain(), inc=s_v)
                l_done = s_v.n
                p.op("scalar", lambda s: s.copy(rl_bf[0:1, :], rl_sb[0:1, :]),
                     waits=[(s_v, l_done)], inc=s_sc)
                rlb_done = s_sc.n
                br = 2 + iq % 2
                tok_br = mm_one(br, partial(
                    lambda t, br_=0: t.matmul(pb[br_][:, :], ones1[:], rl_bf[0:1, :],
                                              start=True, stop=True), br_=br),
                    waits=[(s_sc, rlb_done), (s_const, CONSTS)])
                ev_rb = p.op("scalar", partial(
                    lambda s, br_=0: s.copy(rlb_sb[:], pb[br_][:, :]), br_=br),
                    waits=[(s_mm, tok_br)], inc=s_sc)
                bank_consumed(br, s_sc, ev_rb)
                osl = iq % 2
                ev_on = p.op("vector", partial(
                    lambda v, bo_=0, osl_=0: v.tensor_tensor(
                        ost[osl_][:], pb[bo_][:, :], rlb_sb[:], ALU.mult), bo_=bo, osl_=osl),
                    waits=[(s_mm, otok), (s_sc, ev_rb), (s_od[osl], ost_tok[osl])], inc=s_v)
                bank_consumed(bo, s_v, ev_on)
                p.op("vector", lambda v: v.drain(), inc=s_v)
                ost_tok[osl] = p.dma("sync", o1T_o[u * 128:(u + 1) * 128, iq * 512:(iq + 1) * 512],
                                     ost[osl][:], waits=[(s_v, s_v.n)], inc=s_od[osl])
            unit_free[sl] = s_mm.n

        p.op("sync", lambda sy: sy.nop(), waits=[(s_od[0], s_od[0].n), (s_od[1], s_od[1].n)])

    p.emit()
    return nc, p.n_ops()



def build_phase_e():
    nc = bass.Bass("TRN2", target_bir_lowering=False)
    ctx = ExitStack()
    p = Prog(nc, ctx)

    o1T = nc.dram_tensor("o1T", [D, TC], BF16, kind="ExternalInput")
    woutT = nc.dram_tensor("woutT", [D, D], F32, kind="ExternalInput")
    bout_row = nc.dram_tensor("bout_row", [1, D], F32, kind="ExternalInput")
    x1pf = nc.dram_tensor("x1pf", [TC, D], F32, kind="ExternalInput")
    ones1_d = nc.dram_tensor("ones1", [1, 128], BF16, kind="ExternalInput")
    out_o = nc.dram_tensor("out", [TC, D], F32, kind="ExternalOutput")

    with ctx:
        e = ctx.enter_context
        ones1 = e(nc.sbuf_tensor("ones1sb", [1, 128], BF16))
        bout_sb = e(nc.sbuf_tensor("boutsb", [1, D], BF16))
        ow = e(nc.sbuf_tensor("ow", [128, 16 * 1024], BF16))
        wbuf = [e(nc.sbuf_tensor(f"wbuf{i}", [128, 4096], BF16)) for i in range(2)]
        xp = [e(nc.sbuf_tensor(f"xp{i}", [128, D], F32)) for i in range(2)]
        accf = e(nc.sbuf_tensor("accf", [128, 8 * D], F32))
        pb = [e(nc.psum_tensor(f"pb{i}", [128, 512], F32)) for i in range(8)]

        s_const = p.sem("const")
        s_constg = p.sem("constg")
        s_wb = [p.sem(f"wb{i}") for i in range(2)]
        s_xw = p.sem("xw")
        s_mm = p.sem("mm")
        s_v = p.sem("v")
        s_sc = p.sem("sc")
        s_g = p.sem("g")
        s_ld = [p.sem(f"ld{i}") for i in range(2)]
        s_ow = p.sem("ow")
        s_outd = [p.sem(f"outd{i}") for i in range(2)]
        p.auto = {"vector": s_v, "scalar": s_sc, "gpsimd": s_g}

        bank_writer = [0] * 8
        bank_reader = [0] * 8
        bank_rsem = [None] * 8

        def mm_group(bank_i, mms, waits_first=()):
            w = list(waits_first)
            if bank_rsem[bank_i] is not None:
                w.append((bank_rsem[bank_i], bank_reader[bank_i]))
            for i, fn in enumerate(mms):
                inc = s_mm if i == len(mms) - 1 else None
                p.op("tensor", fn, waits=w if i == 0 else (), inc=inc)
            bank_writer[bank_i] = s_mm.n
            return s_mm.n

        def bank_consumed(bank_i, sem, val):
            bank_rsem[bank_i] = sem
            bank_reader[bank_i] = val

        p.dma("sync", ones1[:], ones1_d[:, :], inc=s_const)
        p.dma("gpsimd", bout_sb[:], bout_row[:, :], inc=s_constg)
        CONSTS = s_const.n
        CONSTG = s_constg.n

        out_tok = [0, 0]
        accf_tok = 0
        for tsl in range(2):
            t0 = tsl * 1024
            t_ow = p.dma("sync", ow[:].rearrange("p (k t) -> p k t", k=16),
                         o1T[:, t0:t0 + 1024].rearrange("(k p) t -> p k t", p=128),
                         waits=[(s_mm, s_mm.n)], inc=s_ow)
            for dq in range(4):
                tw0 = p.dma("gpsimd", wbuf[0][:].rearrange("p (k e) -> p k e", k=8),
                            woutT[0:1024, dq * 512:(dq + 1) * 512].rearrange("(k p) e -> p k e", p=128),
                            waits=[(s_mm, s_mm.n)], inc=s_wb[0])
                tw1 = p.dma("gpsimd", wbuf[1][:].rearrange("p (k e) -> p k e", k=8),
                            woutT[1024:2048, dq * 512:(dq + 1) * 512].rearrange("(k p) e -> p k e", p=128),
                            inc=s_wb[1])
                for tc in range(8):
                    bi = tc % 4
                    mms = []
                    for k in range(16):
                        mms.append(partial(
                            lambda t, k_=0, tc_=0, bi_=0: t.matmul(
                                pb[bi_][:, :],
                                ow[:, k_ * 1024 + tc_ * 128: k_ * 1024 + (tc_ + 1) * 128],
                                wbuf[k_ // 8][:, (k_ % 8) * 512:(k_ % 8 + 1) * 512],
                                start=(k_ == 0), stop=False),
                            k_=k, tc_=tc, bi_=bi))
                    mms.append(partial(
                        lambda t, bi_=0, dq_=0: t.matmul(
                            pb[bi_][:, :], ones1[:], bout_sb[0:1, dq_ * 512:(dq_ + 1) * 512],
                            start=False, stop=True), bi_=bi, dq_=dq))
                    tok = mm_group(bi, mms, waits_first=[(s_wb[0], tw0), (s_wb[1], tw1), (s_ow, t_ow),
                                                         (s_const, CONSTS), (s_constg, CONSTG)])
                    fo = tc * D + dq * 512
                    ev = p.op("vector", partial(
                        lambda v, bi_=0, fo_=0: v.tensor_copy(accf[:, fo_:fo_ + 512], pb[bi_][:, :]),
                        bi_=bi, fo_=fo), waits=[(s_mm, tok), (s_outd[0], accf_tok)], inc=s_v)
                    bank_consumed(bi, s_v, ev)
            for tc in range(8):
                xsl = tc % 2
                t_x = p.dma("sync", xp[xsl][:], x1pf[t0 + tc * 128: t0 + (tc + 1) * 128, :],
                            waits=[(s_v, s_v.n)], inc=s_ld[xsl])
                p.op("vector", partial(
                    lambda v, tc_=0, xsl_=0: v.tensor_tensor(
                        accf[:, tc_ * D:(tc_ + 1) * D], accf[:, tc_ * D:(tc_ + 1) * D],
                        xp[xsl_][:], ALU.add), tc_=tc, xsl_=xsl),
                    waits=[(s_ld[xsl], t_x)], inc=s_v)
                p.op("vector", lambda v: v.drain(), inc=s_v)
                p.dma("sync", out_o[t0 + tc * 128: t0 + (tc + 1) * 128, :],
                      accf[:, tc * D:(tc + 1) * D], waits=[(s_v, s_v.n)], inc=s_outd[0])
            accf_tok = s_outd[0].n

        p.op("sync", lambda sy: sy.nop(),
             waits=[(s_outd[0], s_outd[0].n), (s_outd[1], s_outd[1].n)])

    p.emit()
    return nc, p.n_ops()





def build_fused():
    nc = bass.Bass("TRN2", target_bir_lowering=False)
    ctx = ExitStack()
    p = Prog(nc, ctx)
    U32 = mybir.dt.uint32

    x = nc.dram_tensor("x", [TC, D], F32, kind="ExternalInput")
    wq = nc.dram_tensor("wq", [D, 1024], F32, kind="ExternalInput")
    wk = nc.dram_tensor("wk", [D, 1024], F32, kind="ExternalInput")
    wkg1 = nc.dram_tensor("wkg1", [D, 16], F32, kind="ExternalInput")
    wkg2 = nc.dram_tensor("wkg2", [16, 1024], F32, kind="ExternalInput")
    bkg2c = nc.dram_tensor("bkg2c", [128, 8], F32, kind="ExternalInput")
    wv = nc.dram_tensor("wv", [D, D], F32, kind="ExternalInput")
    wgg = nc.dram_tensor("wgg", [D, D], F32, kind="ExternalInput")
    bg_row = nc.dram_tensor("bg_row", [1, D], F32, kind="ExternalInput")
    gate0 = nc.dram_tensor("gate0", [D, 16], F32, kind="ExternalInput")
    swg = nc.dram_tensor("swg", [D, D], F32, kind="ExternalInput")
    swu = nc.dram_tensor("swu", [D, D], F32, kind="ExternalInput")
    swd = nc.dram_tensor("swd", [D, D], F32, kind="ExternalInput")
    ewg = nc.dram_tensor("ewg", [16, D, 1024], F32, kind="ExternalInput")
    ewu = nc.dram_tensor("ewu", [16, D, 1024], F32, kind="ExternalInput")
    ewd = nc.dram_tensor("ewd", [16, 1024, D], F32, kind="ExternalInput")
    wo = nc.dram_tensor("wo", [D, D], F32, kind="ExternalInput")
    wqkvT = nc.dram_tensor("wqkvT", [D, 3 * D], F32, kind="ExternalInput")
    bqkv_c = nc.dram_tensor("bqkv_c", [128, 32], F32, kind="ExternalInput")
    bv_row = nc.dram_tensor("bv_row", [1, D], F32, kind="ExternalInput")
    gate1 = nc.dram_tensor("gate1", [D, 8], F32, kind="ExternalInput")
    upw = nc.dram_tensor("upw", [D, 8192], F32, kind="ExternalInput")
    downw = nc.dram_tensor("downw", [8192, D], F32, kind="ExternalInput")
    expw = nc.dram_tensor("expw", [D, D], F32, kind="ExternalInput")
    gam_row = nc.dram_tensor("gam_row", [1, D], F32, kind="ExternalInput")
    bet_row = nc.dram_tensor("bet_row", [1, D], F32, kind="ExternalInput")
    woutT = nc.dram_tensor("woutT", [D, D], F32, kind="ExternalInput")
    bout_row = nc.dram_tensor("bout_row", [1, D], F32, kind="ExternalInput")
    ident_d = nc.dram_tensor("ident", [128, 128], BF16, kind="ExternalInput")
    ones1_d = nc.dram_tensor("ones1", [1, 128], BF16, kind="ExternalInput")
    onest_d = nc.dram_tensor("ones_tile", [128, 128], BF16, kind="ExternalInput")
    onescol_d = nc.dram_tensor("onescol", [128, 1], BF16, kind="ExternalInput")
    ut01_d = nc.dram_tensor("ut01", [128, 128], BF16, kind="ExternalInput")
    dmask_d = nc.dram_tensor("dmask", [128, 4 * 512], BF16, kind="ExternalInput")
    constf_d = nc.dram_tensor("constf", [128, 8], F32, kind="ExternalInput")
    selv_d = nc.dram_tensor("selv", [1, 1], U32, kind="ExternalInput")

    qT_o = nc.dram_tensor("qT_i", [1024, TC], BF16)
    kT_o = nc.dram_tensor("kT_i", [1024, TC], BF16)
    gkT_o = nc.dram_tensor("gkT_i", [1024, TC], F32)
    v_o = nc.dram_tensor("v_i", [TC, D], BF16)
    g_o = nc.dram_tensor("g_i", [TC, D], BF16)
    f0_o = nc.dram_tensor("f0_i", [TC, D], F32)
    f0 = f0_o
    g_t = g_o
    xnT_d = nc.dram_tensor("xnT_i", [D, TC], BF16)
    dbg_o = nc.dram_tensor("dbg_i", [128, 512], F32)
    dbgd = nc.dram_tensor("dbgd_i", [128, 1064], F32)
    Gq = nc.dram_tensor("Gq", [2048, TC], BF16)
    Gk = nc.dram_tensor("Gk", [2048, TC], BF16)
    Ggk = nc.dram_tensor("Ggk", [2048, TC], F32)
    Gv = nc.dram_tensor("Gv", [4096, D], BF16)
    o_o = nc.dram_tensor("o_i", [4096, 1024], BF16)
    Go = nc.dram_tensor("Go", [8192, 1024], BF16)
    zT_d = nc.dram_tensor("zT_scr", [D, TC], BF16)
    x01_d = nc.dram_tensor("x01_scr", [TC, D], F32)
    xn1T_d = nc.dram_tensor("xn1T_scr", [D, TC], BF16)
    q1T_o = nc.dram_tensor("q1T_i", [D, TC], BF16)
    k1T_o = nc.dram_tensor("k1T_i", [D, TC], BF16)
    v1_o = nc.dram_tensor("v1_i", [TC, D], BF16)
    x1pf_o = nc.dram_tensor("x1pf_i", [TC, D], F32)
    x1pf = x1pf_o
    x1_o = nc.dram_tensor("x1_i", [TC, D], F32)
    Gq1 = nc.dram_tensor("Gq1", [4096, TC], BF16)
    Gk1 = nc.dram_tensor("Gk1", [4096, TC], BF16)
    Gv1 = nc.dram_tensor("Gv1", [4096, D], BF16)
    o1T_o = nc.dram_tensor("o1T_i", [1024, 4096], BF16)
    Go1 = nc.dram_tensor("Go1", [2048, 4096], BF16)
    out_o = nc.dram_tensor("out", [TC, D], F32, kind="ExternalOutput")

    with ctx:
        e = ctx.enter_context
        IDENT = e(nc.sbuf_tensor("P_ident", [128, 128], BF16))
        ONES1 = e(nc.sbuf_tensor("P_ones1", [1, 128], BF16))
        ONEST = e(nc.sbuf_tensor("P_onest", [128, 128], BF16))
        ONESCOL = e(nc.sbuf_tensor("P_onescol", [128, 1], BF16))
        CONSTF = e(nc.sbuf_tensor("P_constf", [128, 8], F32))
        BKG2 = e(nc.sbuf_tensor("P_bkg2", [128, 8], F32))
        ROWBUF = e(nc.sbuf_tensor("P_rowbuf", [1, D], BF16))
        WKG1 = e(nc.sbuf_tensor("P_wkg1", [128, 256], BF16))
        WKG2 = e(nc.sbuf_tensor("P_wkg2", [16, 1024], BF16))
        GATE0 = e(nc.sbuf_tensor("P_gate0", [128, 256], BF16))
        GATE1 = e(nc.sbuf_tensor("P_gate1", [128, 128], BF16))
        BQKV = e(nc.sbuf_tensor("P_bqkv", [128, 32], F32))
        UT01SB = e(nc.sbuf_tensor("P_ut01", [128, 128], BF16))
        STAT = e(nc.sbuf_tensor("P_stat", [128, 16], F32))
        T2 = e(nc.sbuf_tensor("P_t2", [128, 8], F32))
        SC2 = e(nc.sbuf_tensor("P_sc2", [128, 2], F32))
        SILU2 = [e(nc.sbuf_tensor(f"P_silu{i}", [128, 512], BF16)) for i in range(2)]
        BSC = e(nc.sbuf_tensor("P_bsc", [128, 128], BF16))
        COMB = e(nc.sbuf_tensor("P_comb", [128, 128], F32))
        RW = e(nc.sbuf_tensor("P_rw", [128, 16], F32))
        T8 = e(nc.sbuf_tensor("P_t8", [128, 8], F32))
        MASKW = e(nc.sbuf_tensor("P_maskw", [128, 16], F32))
        STG = [e(nc.sbuf_tensor(f"P_stg{i}", [128, 512], BF16)) for i in range(4)]
        STGF = [e(nc.sbuf_tensor(f"P_stgf{i}", [128, 512], F32)) for i in range(2)]
        SCR = [e(nc.sbuf_tensor(f"P_scr{i}", [128, 512], F32)) for i in range(2)]
        PT = [STG[0][:, :], STG[1][:, :]]
        RLS = SCR[0][0:1, 0:512]
        RLBF = SCR[1][0:1, 0:256].bitcast(BF16)
        RLB = SILU2[0][:, :]
        OSTG = [e(nc.sbuf_tensor(f"P_ostg{i}", [128, 1024], BF16)) for i in range(2)]
        MSTAT = e(nc.sbuf_tensor("P_mstat", [1, 8], F32))
        BIASC = e(nc.sbuf_tensor("P_biasc", [128, 1], F32))
        BIAS4 = e(nc.sbuf_tensor("P_bias4", [128, 1], F32))
        BIAS2 = e(nc.sbuf_tensor("P_bias2", [128, 1], F32))
        SIGC = e(nc.sbuf_tensor("P_sigc", [128, 1], F32))
        GKC = e(nc.sbuf_tensor("P_gkc", [128, 128], F32))
        EQ = e(nc.sbuf_tensor("P_eq", [128, 128], F32))
        EKI = e(nc.sbuf_tensor("P_eki", [128, 128], F32))
        QD = e(nc.sbuf_tensor("P_qd", [128, 128], BF16))
        KEXP = e(nc.sbuf_tensor("P_kexp", [128, 128], BF16))
        AMASK = e(nc.sbuf_tensor("P_amask", [128, 128], BF16))
        KEXPP = e(nc.sbuf_tensor("P_kexpp", [128, 128], BF16))
        SF = e(nc.sbuf_tensor("P_sf", [128, 256], F32))
        SBB = e(nc.sbuf_tensor("P_sb", [128, 256], BF16))
        SELSB = e(nc.sbuf_tensor("P_selsb", [1, 1], U32))
        XQ = [e(nc.sbuf_tensor(f"P_xq{i}", [128, D], F32)) for i in range(2)]
        BIGA = e(nc.sbuf_tensor("P_biga", [128, 16384], BF16))
        XNW = e(nc.sbuf_tensor("P_xnw", [128, 16384], BF16))
        FACC = e(nc.sbuf_tensor("P_facc", [128, 8 * D], F32))
        WBUF = [e(nc.sbuf_tensor(f"P_wbuf{i}", [128, 4096], BF16)) for i in range(4)]
        PB = [e(nc.psum_tensor(f"P_pb{i}", [128, 512], F32)) for i in range(8)]

        dynst = {}
        s_selr = p.sem("selr")
        p.dma("sync", SELSB[:], selv_d[:, :], inc=s_selr)

        def _selload(eng, nm="sync"):
            r0 = eng.alloc_register(f"selreg_{nm}")
            inst = eng.reg_load(r0, SELSB[0:1, 0:1])
            dynst[nm] = bass.RuntimeValue(r0)
            return inst
        for _eng in ("sync", "gpsimd", "scalar"):
            p.op(_eng, partial(lambda ee, nm_="sync": _selload(ee, nm_), nm_=_eng),
                 waits=[(s_selr, 16)])

        def SELRV(nm="sync"):
            return dynst[nm]

        def barrier(tag):
            sems = [s for s in p.all_sems]
            for engx in ENGINES:
                waits = [(s, s.n) for s in sems if s.n > 0]
                p.op(engx, lambda ee: ee.nop(), waits=waits)

        def collective(ins_outs, tag):
            cc = p.sem("cc" + tag)
            for tin, tout in ins_outs:
                p.op("gpsimd", partial(
                    lambda g, a=None, b=None: g.collective_compute(
                        "AllGather", ALU.bypass,
                        replica_groups=[[0, 1], [2, 3], [4, 5], [6, 7]],
                        ins=[a.ap().opt()], outs=[b.ap().opt()]), a=tin, b=tout),
                    inc=cc)
            return cc, len(ins_outs)

        ident = IDENT; ones1 = ONES1; onest = ONEST; constf = CONSTF
        bkg2_sb = BKG2; bg_sb = ROWBUF; wkg1_sb = WKG1; wkg2_sb = WKG2; gate0_sb = GATE0
        stat = STAT
        big = BIGA
        xnw = XNW; wbuf = WBUF; facc = FACC
        comb_sb = COMB; rw_sb = RW; t8_sb = T8; mask_sb = MASKW
        t1_sb = OSTG[0][0:16, 0:1024]
        silu_sb = SILU2; bsc_sb = BSC; be_sb = OSTG[1][:, 0:1024]
        stg = STG; stgf = STGF; scr = SCR; pb = PB


        xc = [big[:, 0:4096].bitcast(F32), big[:, 4096:8192].bitcast(F32)]
        xn_sb = [big[:, 8192:10240], big[:, 10240:12288]]
        xnT_st = [big[:, 12288:14336], big[:, 14336:16384]]
        m_sb = big



        s_const = p.sem("const")
        s_constg = p.sem("constg")
        s_xs = [p.sem(f"x{i}") for i in range(2)]
        s_wb = [p.sem(f"wb{i}") for i in range(4)]
        s_xw = p.sem("xw")
        s_mm = p.sem("mm")
        s_v = p.sem("v")
        s_sc = p.sem("sc")
        s_g = p.sem("g")
        s_stgd = [p.sem(f"stgd{i}") for i in range(4)]
        s_stgfd = [p.sem(f"stgfd{i}") for i in range(2)]
        s_xnt = [p.sem(f"xnt{i}") for i in range(2)]
        s_f0 = p.sem("f0")
        p.auto = {"vector": s_v, "scalar": s_sc, "gpsimd": s_g}

        bank_writer = [0] * 8
        bank_reader = [0] * 8
        bank_rsem = [None] * 8

        def mm_group(bank_i, mms, waits_first=()):
            w = list(waits_first)
            if bank_rsem[bank_i] is not None:
                w.append((bank_rsem[bank_i], bank_reader[bank_i]))
            for i, fn in enumerate(mms):
                inc = s_mm if i == len(mms) - 1 else None
                p.op("tensor", fn, waits=w if i == 0 else (), inc=inc)
            bank_writer[bank_i] = s_mm.n
            return s_mm.n

        def bank_consumed(bank_i, sem, val):
            bank_rsem[bank_i] = sem
            bank_reader[bank_i] = val

        stg_tok = [ (None, 0) ] * 4
        stgf_tok = [ (None, 0) ] * 4

        def use_stg(pool_tok, idx):
            semval = pool_tok[idx]
            return [] if semval[0] is None else [semval]

        p.dma("sync", ident[:], ident_d[:, :], inc=s_const)
        p.dma("sync", ones1[:], ones1_d[:, :], inc=s_const)
        p.dma("sync", onest[:], onest_d[:, :], inc=s_const)
        p.dma("sync", bkg2_sb[:], bkg2c[:, :], inc=s_const)
        p.dma("sync", constf[:], constf_d[:, :], inc=s_const)
        p.dma("gpsimd", bg_sb[:], bg_row[:, :], inc=s_constg)
        p.dma("gpsimd", wkg1_sb[:].rearrange("p (k j) -> p k j", k=16),
              wkg1[:, :].rearrange("(k p) j -> p k j", p=128), inc=s_constg)
        p.dma("gpsimd", wkg2_sb[:], wkg2[:, :], inc=s_constg)
        p.dma("gpsimd", gate0_sb[:].rearrange("p (k j) -> p k j", k=16),
              gate0[:, :].rearrange("(k p) j -> p k j", p=128), inc=s_constg)
        CONSTS = s_const.n
        CONSTG = s_constg.n

        ln_ch_done = []
        tr_last = [0, 0]
        xnT_dma_tok = [0, 0]
        sq_read = 0
        for c in range(NCH):
            sl = c % 2
            xbuf = xc[sl]
            w = []
            if c >= 2:
                w = [(s_v, ln_ch_done[c - 2][0]), (s_sc, ln_ch_done[c - 2][1])]
            t_x = p.dma("sync", xbuf[:], x[c * 128:(c + 1) * 128, :], waits=w, inc=s_xs[sl])
            p.op("scalar", partial(lambda s, xb, xnb_: s.activation(xnb_, xb[:], AF.Square,
                                                                    accum_out=stat[:, 1:2]),
                                   xb=xbuf, xnb_=xn_sb[sl]),
                 waits=[(s_xs[sl], t_x), (s_v, sq_read), (s_mm, tr_last[sl])], inc=s_sc)
            sq_done = s_sc.n
            p.op("vector", partial(lambda v, xb: v.reduce_sum(stat[:, 0:1], xb[:], AX.X), xb=xbuf),
                 waits=[(s_xs[sl], t_x)], inc=s_v)
            p.op("vector", lambda v: v.drain(), inc=s_v)
            sx_done = s_v.n
            p.op("scalar", lambda s: s.mul(stat[:, 2:3], stat[:, 0:1], 1.0 / D),
                 waits=[(s_v, sx_done)], inc=s_sc)
            mu_done = s_sc.n
            p.op("vector", lambda v: v.tensor_scalar_mul(stat[:, 3:4], stat[:, 1:2], 1.0 / D),
                 waits=[(s_sc, sq_done)], inc=s_v)
            sq_read = s_v.n
            p.op("vector", lambda v: v.tensor_tensor(stat[:, 4:5], stat[:, 2:3], stat[:, 2:3], ALU.mult),
                 waits=[(s_sc, mu_done)], inc=s_v)
            p.op("vector", lambda v: v.tensor_tensor(stat[:, 5:6], stat[:, 3:4], stat[:, 4:5], ALU.subtract), inc=s_v)
            p.op("vector", lambda v: v.drain(), inc=s_v)
            var_done = s_v.n
            p.op("scalar", lambda s: s.activation(stat[:, 6:7], stat[:, 5:6], AF.Ln, bias=constf[:, 0:1]),
                 waits=[(s_v, var_done), (s_const, CONSTS)], inc=s_sc)
            p.op("scalar", lambda s: s.activation(stat[:, 7:8], stat[:, 6:7], AF.Exp, scale=-0.5), inc=s_sc)
            rstd_done = s_sc.n
            xnb = xn_sb[sl]
            p.op("vector", partial(lambda v, xb, xnb_: v.tensor_scalar(
                    xnb_[:], xb[:], stat[:, 2:3], stat[:, 7:8], ALU.subtract, ALU.mult),
                    xb=xbuf, xnb_=xnb),
                 waits=[(s_sc, rstd_done), (s_mm, tr_last[sl])], inc=s_v)
            p.op("vector", lambda v: v.drain(), inc=s_v)
            xn_done = s_v.n
            ln_ch_done.append((xn_done, sq_done))
            stbuf = xnT_st[sl]
            for k in range(16):
                bi = k % 2
                w0 = [(s_const, CONSTS), (s_constg, CONSTG), (s_v, xn_done)]
                tok = mm_group(bi, [partial(
                    lambda t, k_, bi_, xnb_: t.transpose(
                        pb[bi_][:, 0:64].bitcast(BF16), xnb_[:, k_ * 128:(k_ + 1) * 128], ident[:]),
                    k_=k, bi_=bi, xnb_=xnb)], waits_first=w0)
                eng = "vector" if k % 2 == 0 else "scalar"
                sem = s_v if eng == "vector" else s_sc
                fn = (partial(lambda v, k_, bi_, sb: v.tensor_copy(
                        sb[:, k_ * 128:(k_ + 1) * 128], pb[bi_][:, 0:64].bitcast(BF16)), k_=k, bi_=bi, sb=stbuf)
                      if eng == "vector" else
                      partial(lambda s, k_, bi_, sb: s.copy(
                        sb[:, k_ * 128:(k_ + 1) * 128], pb[bi_][:, 0:64].bitcast(BF16)), k_=k, bi_=bi, sb=stbuf))
                wv_ = [(s_mm, tok)]
                if k < 2:
                    wv_.append((s_xnt[sl], xnT_dma_tok[sl]))
                ev = p.op(eng, fn, waits=wv_, inc=sem)
                bank_consumed(bi, sem, ev)
            tr_last[sl] = s_mm.n
            xnT_dma_tok[sl] = p.dma(
                "sync", xnT_d[:, :].rearrange("(k p) t -> p k t", p=128)[:, :, c * 128:(c + 1) * 128],
                stbuf[:].rearrange("p (k j) -> p k j", k=16),
                waits=[(s_v, s_v.n), (s_sc, s_sc.n)], inc=s_xnt[sl])

        f0_dma_tok = 0
        for tsl in range(2):
            t0 = tsl * 1024
            t_xw = p.dma("sync", xnw[:].rearrange("p (k t) -> p k t", k=16),
                         xnT_d[:, t0:t0 + 1024].rearrange("(k p) t -> p k t", p=128),
                         waits=[(s_mm, s_mm.n), (s_xnt[0], s_xnt[0].n), (s_xnt[1], s_xnt[1].n)],
                         inc=s_xw)

            comb_read = 0
            dbg_guard = s_f0.n
            for tc in range(8):
                bi = tc % 2
                mms = []
                for k in range(16):
                    mms.append(partial(
                        lambda t, k_, tc_, bi_: t.matmul(
                            pb[bi_][:, 0:16], xnw[:, k_ * 1024 + tc_ * 128: k_ * 1024 + (tc_ + 1) * 128],
                            gate0_sb[:, k_ * 16:(k_ + 1) * 16], start=(k_ == 0), stop=(k_ == 15)),
                        k_=k, tc_=tc, bi_=bi))
                tok = mm_group(bi, mms, waits_first=[(s_xw, t_xw), (s_const, CONSTS), (s_constg, CONSTG)])
                p.op("scalar", partial(lambda s, bi_: s.activation(rw_sb[:], pb[bi_][:, 0:16], AF.Sigmoid), bi_=bi),
                     waits=[(s_mm, tok), (s_v, comb_read), (s_f0, dbg_guard)], inc=s_sc)
                sig_done = s_sc.n
                bank_consumed(bi, s_sc, sig_done)
                p.op("vector", lambda v: v.max(t8_sb[:], rw_sb[:]), waits=[(s_sc, sig_done)], inc=s_v)
                p.op("vector", lambda v: v.drain(), inc=s_v)
                max_done = s_v.n
                p.op("scalar", lambda s: s.activation(stat[:, 0:8], t8_sb[:], AF.Copy,
                                                      accum_out=stat[:, 8:9]),
                     waits=[(s_v, max_done)], inc=s_sc)
                sum_done = s_sc.n
                p.op("scalar", lambda s: s.copy(stat[:, 12:13], t8_sb[:, 7:8]), inc=s_sc)
                thr_done = s_sc.n
                p.op("vector", lambda v: v.tensor_scalar(stat[:, 9:10], stat[:, 8:9], 1e-6, 0.4,
                                                         ALU.add, ALU.mult),
                     waits=[(s_sc, sum_done)], inc=s_v)
                p.op("vector", lambda v: v.reciprocal(stat[:, 10:11], stat[:, 9:10]), inc=s_v)
                p.op("vector", lambda v: v.drain(), inc=s_v)
                recip_done = s_v.n
                p.op("scalar", lambda s: s.copy(stat[:, 13:14], stat[:, 10:11]),
                     waits=[(s_v, recip_done)], inc=s_sc)
                sc2_done = s_sc.n
                p.op("vector", lambda v: v.tensor_scalar(mask_sb[:], rw_sb[:], stat[:, 12:13], None, ALU.is_ge),
                     waits=[(s_sc, thr_done)], inc=s_v)
                p.op("vector", partial(lambda v, tc_: v.scalar_tensor_tensor(
                        comb_sb[:, tc_ * 16:(tc_ + 1) * 16], rw_sb[:], stat[:, 13:14], mask_sb[:],
                        ALU.mult, ALU.mult), tc_=tc),
                     waits=[(s_sc, sc2_done), (s_f0, dbg_guard)], inc=s_v)
                p.op("vector", lambda v: v.drain(), inc=s_v)
                comb_read = s_v.n
            comb_done = s_v.n
            p.dma("sync", dbg_o[:, tsl * 128:(tsl + 1) * 128], comb_sb[:],
                  waits=[(s_v, comb_done)], inc=s_f0)

            def proj_1024(wt_dram, out_dram, name):
                for eh in range(2):
                    w_war = [(s_mm, s_mm.n)]
                    tw0 = p.dma("gpsimd", wbuf[0][:].rearrange("p (k e) -> p k e", k=8),
                          wt_dram[0:1024, eh * 512:(eh + 1) * 512].rearrange("(k p) e -> p k e", p=128),
                          waits=w_war, inc=s_wb[0])
                    tw1 = p.dma("gpsimd", wbuf[1][:].rearrange("p (k e) -> p k e", k=8),
                          wt_dram[1024:2048, eh * 512:(eh + 1) * 512].rearrange("(k p) e -> p k e", p=128),
                          inc=s_wb[1])
                    for ec in range(4):
                        for ts in range(2):
                            bi = 2 + (ec * 2 + ts) % 2
                            mms = []
                            for k in range(16):
                                mms.append(partial(
                                    lambda t, k_, ec_, ts_, bi_: t.matmul(
                                        pb[bi_][:, :],
                                        wbuf[k_ // 8][:, (k_ % 8) * 512 + ec_ * 128:
                                                      (k_ % 8) * 512 + (ec_ + 1) * 128],
                                        xnw[:, k_ * 1024 + ts_ * 512: k_ * 1024 + (ts_ + 1) * 512],
                                        start=(k_ == 0), stop=(k_ == 15)),
                                    k_=k, ec_=ec, ts_=ts, bi_=bi))
                            tok = mm_group(bi, mms, waits_first=[(s_wb[0], tw0), (s_wb[1], tw1),
                                                                  (s_xw, t_xw)])
                            si = (ec * 2 + ts) % 4
                            w_stg = use_stg(stg_tok, si) + [(s_mm, tok)]
                            ev = p.op("vector", partial(
                                lambda v, bi_, si_: v.tensor_copy(stg[si_][:], pb[bi_][:, :]), bi_=bi, si_=si),
                                waits=w_stg, inc=s_v)
                            bank_consumed(bi, s_v, ev)
                            dt = p.dma("sync",
                                       out_dram[eh * 512 + ec * 128: eh * 512 + (ec + 1) * 128,
                                                t0 + ts * 512: t0 + (ts + 1) * 512],
                                       stg[si][:], waits=[(s_v, ev)], inc=s_stgd[si])
                            stg_tok[si] = (s_stgd[si], dt)

            proj_1024(wq, qT_o, "q")
            proj_1024(wk, kT_o, "k")

            for ts in range(2):
                bi = 2 + ts % 2
                mms = []
                for k in range(16):
                    mms.append(partial(
                        lambda t, k_, ts_, bi_: t.matmul(
                            pb[bi_][0:16, :],
                            wkg1_sb[:, k_ * 16:(k_ + 1) * 16],
                            xnw[:, k_ * 1024 + ts_ * 512: k_ * 1024 + (ts_ + 1) * 512],
                            start=(k_ == 0), stop=(k_ == 15)),
                        k_=k, ts_=ts, bi_=bi))
                tok = mm_group(bi, mms, waits_first=[(s_xw, t_xw), (s_const, CONSTS), (s_constg, CONSTG)])
                ev = p.op("vector", partial(
                    lambda v, bi_, ts_: v.tensor_copy(t1_sb[:, ts_ * 512:(ts_ + 1) * 512], pb[bi_][0:16, :]),
                    bi_=bi, ts_=ts), waits=[(s_mm, tok)], inc=s_v)
                bank_consumed(bi, s_v, ev)
            t1_done = s_v.n

            for ec in range(8):
                for ts in range(2):
                    bi = 2 + (ec * 2 + ts) % 2
                    mms = [partial(
                        lambda t, ec_, ts_, bi_: t.matmul(
                            pb[bi_][:, :], wkg2_sb[:, ec_ * 128:(ec_ + 1) * 128],
                            t1_sb[:, ts_ * 512:(ts_ + 1) * 512], start=True, stop=True),
                        ec_=ec, ts_=ts, bi_=bi)]
                    tok = mm_group(bi, mms, waits_first=[(s_v, t1_done)])
                    bq = ec % 8
                    p.op("vector", partial(lambda v, bi_, ec_: v.tensor_scalar(
                            scr[0][:], pb[bi_][:, :], bkg2_sb[:, ec_:ec_ + 1], None, ALU.add),
                            bi_=bi, ec_=ec), waits=[(s_mm, tok), (s_const, CONSTS), (s_constg, CONSTG)], inc=s_v)
                    pre_done = s_v.n
                    bank_consumed(bi, s_v, pre_done)
                    abs_done = p.op("scalar", lambda s: s.activation(scr[1][:], scr[0][:], AF.Abs),
                         waits=[(s_v, pre_done)], inc=s_sc)
                    p.op("scalar", lambda s: s.activation(scr[1][:], scr[1][:], AF.Exp, scale=-1.0), inc=s_sc)
                    p.op("scalar", lambda s: s.activation(scr[1][:], scr[1][:], AF.Ln, bias=constf[:, 1:2]),
                         waits=[(s_const, CONSTS)], inc=s_sc)
                    ln_done = s_sc.n
                    p.op("vector", lambda v: v.tensor_scalar_min(scr[0][:], scr[0][:], 0.0),
                         waits=[(s_sc, abs_done)], inc=s_v)
                    si = (ec * 2 + ts) % 2
                    w_stg = use_stg(stgf_tok, si) + [(s_sc, ln_done)]
                    ev = p.op("vector", partial(
                        lambda v, si_: v.tensor_tensor(stgf[si_][:], scr[0][:], scr[1][:], ALU.subtract), si_=si),
                        waits=w_stg, inc=s_v)
                    dt = p.dma("sync", gkT_o[ec * 128:(ec + 1) * 128, t0 + ts * 512: t0 + (ts + 1) * 512],
                               stgf[si][:], waits=[(s_v, ev)], inc=s_stgfd[si])
                    stgf_tok[si] = (s_stgfd[si], dt)

            def proj_tok(wt_dram, out_dram, bias_row, do_silu):
                for dq in range(4):
                    w_war = [(s_mm, s_mm.n)]
                    tw0 = p.dma("gpsimd", wbuf[0][:].rearrange("p (k e) -> p k e", k=8),
                          wt_dram[0:1024, dq * 512:(dq + 1) * 512].rearrange("(k p) e -> p k e", p=128),
                          waits=w_war, inc=s_wb[0])
                    tw1 = p.dma("gpsimd", wbuf[1][:].rearrange("p (k e) -> p k e", k=8),
                          wt_dram[1024:2048, dq * 512:(dq + 1) * 512].rearrange("(k p) e -> p k e", p=128),
                          inc=s_wb[1])
                    for tc in range(8):
                        bi = 2 + tc % 2
                        last_k = 15 if bias_row is None else None
                        mms = []
                        for k in range(16):
                            mms.append(partial(
                                lambda t, k_, tc_, bi_: t.matmul(
                                    pb[bi_][:, :],
                                    xnw[:, k_ * 1024 + tc_ * 128: k_ * 1024 + (tc_ + 1) * 128],
                                    wbuf[k_ // 8][:, (k_ % 8) * 512:(k_ % 8 + 1) * 512],
                                    start=(k_ == 0), stop=(k_ == last_k)),
                                k_=k, tc_=tc, bi_=bi))
                        if bias_row is not None:
                            mms.append(partial(
                                lambda t, tc_, bi_, dq_: t.matmul(
                                    pb[bi_][:, :], ones1[:],
                                    bg_sb[0:1, dq_ * 512:(dq_ + 1) * 512], start=False, stop=True),
                                tc_=tc, bi_=bi, dq_=dq))
                        tok = mm_group(bi, mms, waits_first=[(s_wb[0], tw0), (s_wb[1], tw1),
                                                              (s_xw, t_xw), (s_const, CONSTS), (s_constg, CONSTG)])
                        si = tc % 4
                        w_stg = use_stg(stg_tok, si) + [(s_mm, tok)]
                        if do_silu:
                            ev = p.op("scalar", partial(
                                lambda s, bi_, si_: s.activation(stg[si_][:], pb[bi_][:, :], AF.Silu),
                                bi_=bi, si_=si), waits=w_stg, inc=s_sc)
                            sem = s_sc
                        else:
                            ev = p.op("vector", partial(
                                lambda v, bi_, si_: v.tensor_copy(stg[si_][:], pb[bi_][:, :]),
                                bi_=bi, si_=si), waits=w_stg, inc=s_v)
                            sem = s_v
                        bank_consumed(bi, sem, ev)
                        dt = p.dma("sync", out_dram[t0 + tc * 128: t0 + (tc + 1) * 128,
                                                    dq * 512:(dq + 1) * 512],
                                   stg[si][:], waits=[(sem, ev)], inc=s_stgd[si])
                        stg_tok[si] = (s_stgd[si], dt)

            proj_tok(wv, v_o, None, False)
            proj_tok(wgg, g_o, bg_sb, True)

            if tsl == 1:
                ccA_t = p.sem("ccAe")
                wA = ([(s, s.n) for s in s_stgd] + [(s, s.n) for s in s_stgfd])
                for a_, b_ in ((qT_o, Gq), (kT_o, Gk), (gkT_o, Ggk), (v_o, Gv)):
                    p.op("gpsimd", partial(
                        lambda g, a=None, b=None: g.collective_compute(
                            "AllGather", ALU.bypass,
                            replica_groups=[[0, 1], [2, 3], [4, 5], [6, 7]],
                            ins=[a.ap().opt()], outs=[b.ap().opt()]), a=a_, b=b_),
                        waits=wA, inc=ccA_t)
                    wA = []

            wd_prev = 0
            silu_tok = [0, 0]
            bsc_tok = 0

            for grp in range(9):
                shared_grp = (grp == 8)
                m_war = wd_prev
                for ei in range(2):
                    eg = grp * 2 + ei
                    for imh in range(2):
                        if shared_grp:
                            col0 = ei * 1024
                            g_src = swg[:, col0 + imh * 512: col0 + (imh + 1) * 512]
                            u_src = swu[:, col0 + imh * 512: col0 + (imh + 1) * 512]
                        else:
                            g_src = ewg[eg, :, imh * 512:(imh + 1) * 512]
                            u_src = ewu[eg, :, imh * 512:(imh + 1) * 512]
                        w_war = [(s_mm, s_mm.n)]
                        twg0 = p.dma("gpsimd", wbuf[0][:].rearrange("p (k e) -> p k e", k=8),
                              g_src[0:1024, :].rearrange("(k p) e -> p k e", p=128),
                              waits=w_war, inc=s_wb[0])
                        twg1 = p.dma("gpsimd", wbuf[1][:].rearrange("p (k e) -> p k e", k=8),
                              g_src[1024:2048, :].rearrange("(k p) e -> p k e", p=128), inc=s_wb[1])
                        twu0 = p.dma("gpsimd", wbuf[2][:].rearrange("p (k e) -> p k e", k=8),
                              u_src[0:1024, :].rearrange("(k p) e -> p k e", p=128), inc=s_wb[2])
                        twu1 = p.dma("gpsimd", wbuf[3][:].rearrange("p (k e) -> p k e", k=8),
                              u_src[1024:2048, :].rearrange("(k p) e -> p k e", p=128), inc=s_wb[3])
                        for imc in range(4):
                            imabs = imh * 4 + imc
                            for tw in range(2):
                                mmsg = [partial(
                                    lambda t, k_, imc_, tw_: t.matmul(
                                        pb[0][:, :],
                                        wbuf[k_ // 8][:, (k_ % 8) * 512 + imc_ * 128:
                                                      (k_ % 8) * 512 + (imc_ + 1) * 128],
                                        xnw[:, k_ * 1024 + tw_ * 512: k_ * 1024 + (tw_ + 1) * 512],
                                        start=(k_ == 0), stop=(k_ == 15)),
                                    k_=k, imc_=imc, tw_=tw) for k in range(16)]
                                tokg = mm_group(0, mmsg, waits_first=[(s_wb[0], twg0), (s_wb[1], twg1),
                                                                      (s_xw, t_xw)])
                                mmsu = [partial(
                                    lambda t, k_, imc_, tw_: t.matmul(
                                        pb[1][:, :],
                                        wbuf[2 + k_ // 8][:, (k_ % 8) * 512 + imc_ * 128:
                                                          (k_ % 8) * 512 + (imc_ + 1) * 128],
                                        xnw[:, k_ * 1024 + tw_ * 512: k_ * 1024 + (tw_ + 1) * 512],
                                        start=(k_ == 0), stop=(k_ == 15)),
                                    k_=k, imc_=imc, tw_=tw) for k in range(16)]
                                toku = mm_group(1, mmsu, waits_first=[(s_wb[2], twu0), (s_wb[3], twu1),
                                                                      (s_xw, t_xw)])
                                sl = tw % 2
                                ev_s = p.op("scalar", partial(
                                    lambda s, sl_: s.activation(silu_sb[sl_][:], pb[0][:, :], AF.Silu), sl_=sl),
                                    waits=[(s_mm, tokg), (s_v, silu_tok[sl])], inc=s_sc)
                                bank_consumed(0, s_sc, ev_s)
                                moff = (ei * 8 + imabs) * 1024 + tw * 512
                                ev_m = p.op("vector", partial(
                                    lambda v, sl_, mo_: v.tensor_tensor(
                                        m_sb[:, mo_:mo_ + 512], silu_sb[sl_][:], pb[1][:, :], ALU.mult),
                                    sl_=sl, mo_=moff),
                                    waits=[(s_mm, toku), (s_sc, ev_s), (s_mm, m_war)], inc=s_v)
                                silu_tok[sl] = ev_m
                                bank_consumed(1, s_v, ev_m)
                p.op("vector", lambda v: v.drain(), inc=s_v)
                m_done = s_v.n

                mscale_done = s_g.n

                for dq in range(4):
                    for ei in range(2):
                        eg = grp * 2 + ei
                        if shared_grp:
                            wd_src = swd[ei * 1024:(ei + 1) * 1024, dq * 512:(dq + 1) * 512]
                        else:
                            wd_src = ewd[eg, :, dq * 512:(dq + 1) * 512]
                        p.dma("gpsimd", wbuf[ei][:].rearrange("p (i e) -> p i e", i=8),
                              wd_src.rearrange("(i p) e -> p i e", p=128),
                              waits=[(s_mm, s_mm.n)], inc=s_wb[ei])
                    t_wd = [(s_wb[0], s_wb[0].n), (s_wb[1], s_wb[1].n)]
                    for tc8 in range(8):
                        fo = tc8 * D + dq * 512
                        if shared_grp:
                            bi = 4 + tc8 % 4
                            mms = []
                            for ei in range(2):
                                for imc in range(8):
                                    mms.append(partial(
                                        lambda t, ei_, imc_, tc_, bi_: t.matmul(
                                            pb[bi_][:, :],
                                            m_sb[:, (ei_ * 8 + imc_) * 1024 + tc_ * 128:
                                                    (ei_ * 8 + imc_) * 1024 + (tc_ + 1) * 128],
                                            wbuf[ei_][:, imc_ * 512:(imc_ + 1) * 512],
                                            start=(ei_ == 0 and imc_ == 0), stop=(ei_ == 1 and imc_ == 7)),
                                        ei_=ei, imc_=imc, tc_=tc8, bi_=bi))
                            tok = mm_group(bi, mms, waits_first=t_wd + [(s_v, m_done)])
                            ev = p.op("vector", partial(
                                lambda v, bi_, fo_: v.tensor_tensor(
                                    facc[:, fo_:fo_ + 512], facc[:, fo_:fo_ + 512], pb[bi_][:, :], ALU.add),
                                bi_=bi, fo_=fo), waits=[(s_mm, tok)], inc=s_v)
                            bank_consumed(bi, s_v, ev)
                        else:
                            for ei in range(2):
                                eg2 = grp * 2 + ei
                                bi = 4 + (tc8 * 2 + ei) % 4
                                mms = []
                                for imc in range(8):
                                    mms.append(partial(
                                        lambda t, ei_, imc_, tc_, bi_: t.matmul(
                                            pb[bi_][:, :],
                                            m_sb[:, (ei_ * 8 + imc_) * 1024 + tc_ * 128:
                                                    (ei_ * 8 + imc_) * 1024 + (tc_ + 1) * 128],
                                            wbuf[ei_][:, imc_ * 512:(imc_ + 1) * 512],
                                            start=(imc_ == 0), stop=(imc_ == 7)),
                                        ei_=ei, imc_=imc, tc_=tc8, bi_=bi))
                                tok = mm_group(bi, mms, waits_first=t_wd + [(s_v, m_done)])
                                ssl = (tc8 * 2 + ei) % 2
                                ev_s = p.op("scalar", partial(
                                    lambda s, bi_, tc_, eg_, ssl_: s.activation(
                                        silu_sb[ssl_][:], pb[bi_][:, :], AF.Copy,
                                        scale=comb_sb[:, tc_ * 16 + eg_: tc_ * 16 + eg_ + 1]),
                                    bi_=bi, tc_=tc8, eg_=eg2, ssl_=ssl),
                                    waits=[(s_mm, tok), (s_v, comb_done), (s_v, silu_tok[ssl])], inc=s_sc)
                                bank_consumed(bi, s_sc, ev_s)
                                if grp == 0 and ei == 0:
                                    ev = p.op("vector", partial(
                                        lambda v, fo_, ssl_: v.tensor_copy(
                                            facc[:, fo_:fo_ + 512], silu_sb[ssl_][:]),
                                        fo_=fo, ssl_=ssl),
                                        waits=[(s_sc, ev_s), (s_f0, f0_dma_tok)], inc=s_v)
                                else:
                                    ev = p.op("vector", partial(
                                        lambda v, fo_, ssl_: v.tensor_tensor(
                                            facc[:, fo_:fo_ + 512], facc[:, fo_:fo_ + 512],
                                            silu_sb[ssl_][:], ALU.add),
                                        fo_=fo, ssl_=ssl),
                                        waits=[(s_sc, ev_s)], inc=s_v)
                                silu_tok[ssl] = ev
                wd_prev = s_mm.n

            p.op("vector", lambda v: v.drain(), inc=s_v)
            f0_dma_tok = p.dma("sync", f0_o[t0:t0 + 1024, :].rearrange("(c p) e -> p c e", p=128),
                               facc[:].rearrange("p (c e) -> p c e", c=8),
                               waits=[(s_v, s_v.n)], inc=s_f0)

        fin = ([(s, s.n) for s in s_stgd] + [(s, s.n) for s in s_stgfd]
               + [(s, s.n) for s in s_xnt] + [(s_f0, s_f0.n)])
        p.op("sync", lambda sy: sy.nop(), waits=fin)

        barrier("A")
        CC_A = (ccA_t, 4)
        NU = 4; NCHU = 32; DV = 256
        ident = IDENT; ut01 = UT01SB; constf = CONSTF
        qs = [FACC[:, 0:2048].bitcast(BF16), FACC[:, 12288:14336].bitcast(BF16)]
        ks = [FACC[:, 2048:4096].bitcast(BF16), FACC[:, 14336:16384].bitcast(BF16)]
        gs = [FACC[:, 4096:8192], XNW[:, 0:8192].bitcast(F32)]
        vs = [FACC[:, 8192:12288].bitcast(BF16), XNW[:, 8192:16384]]
        gkc = GKC; eq = EQ; eki = EKI; qd = QD; kexp = KEXP; amask = AMASK; kexpp = KEXPP
        S_f = SF; S_b = SBB; ostat = STAT; sigc = SIGC
        sqscr = SILU2[0][:, 0:256]
        ost = [OSTG[0][:, 0:1024], OSTG[1][:, 0:1024]]
        pb = PB


        s_const = p.sem("const")
        s_u = [p.sem(f"u{i}") for i in range(2)]
        s_mm = p.sem("mm")
        s_v = p.sem("v")
        s_sc = p.sem("sc")
        s_g = p.sem("g")
        s_od = [p.sem(f"od{i}") for i in range(2)]
        p.auto = {"vector": s_v, "scalar": s_sc, "gpsimd": s_g}

        bank_writer = [0] * 8
        bank_reader = [0] * 8
        bank_rsem = [None] * 8

        def mm_group(bank_i, mms, waits_first=()):
            w = list(waits_first)
            if bank_rsem[bank_i] is not None:
                w.append((bank_rsem[bank_i], bank_reader[bank_i]))
            for i, fn in enumerate(mms):
                inc = s_mm if i == len(mms) - 1 else None
                p.op("tensor", fn, waits=w if i == 0 else (), inc=inc)
            bank_writer[bank_i] = s_mm.n
            return s_mm.n

        def bank_consumed(bank_i, sem, val):
            bank_rsem[bank_i] = sem
            bank_reader[bank_i] = val

        p.dma("sync", ident[:], ident_d[:, :], inc=s_const)
        p.dma("sync", ut01[:], ut01_d[:, :], inc=s_const)
        p.dma("sync", constf[:], constf_d[:, :], inc=s_const)
        CONSTS = s_const.n

        unit_tok = [0, 0]
        unit_free = [0] * 2
        ost_tok = [0, 0]

        def load_unit(u):
            sl = u % 2
            w = [(s_mm, unit_free[sl])]
            p.op("sync", partial(
                lambda sy, sl_=0, u_=0: sy.dma_start(
                    out=qs[sl_][:].rearrange("p (s d) -> p s d", s=2),
                    in_=Gq.rearrange("(s r u p) d -> u p s r d", s=2, r=2, u=4)
                    [u_, :, :, bass.ts(SELRV(), 1), :]),
                sl_=sl, u_=u), waits=w, inc=s_u[sl], inc_amt=16)
            p.op("sync", partial(
                lambda sy, sl_=0, u_=0: sy.dma_start(
                    out=ks[sl_][:].rearrange("p (s d) -> p s d", s=2),
                    in_=Gk.rearrange("(s r u p) d -> u p s r d", s=2, r=2, u=4)
                    [u_, :, :, bass.ts(SELRV(), 1), :]),
                sl_=sl, u_=u), inc=s_u[sl], inc_amt=16)
            p.op("sync", partial(
                lambda sy, sl_=0, u_=0: sy.dma_start(
                    out=gs[sl_][:].rearrange("p (s d) -> p s d", s=2),
                    in_=Ggk.rearrange("(s r u p) d -> u p s r d", s=2, r=2, u=4)
                    [u_, :, :, bass.ts(SELRV(), 1), :]),
                sl_=sl, u_=u), inc=s_u[sl], inc_amt=16)
            tv = p.op("sync", partial(
                lambda sy, sl_=0, u_=0: sy.dma_start(
                    out=vs[sl_][:].rearrange("p (s n d) -> p s n d", s=2, n=16),
                    in_=Gv.rearrange("(s n p) (r u w) -> u p s n r w", s=2, n=16, r=2, u=4)
                    [u_, :, :, :, bass.ts(SELRV(), 1), :]),
                sl_=sl, u_=u), inc=s_u[sl], inc_amt=16)
            unit_tok[sl] = tv
        p.op("sync", lambda sy: sy.nop(), waits=[(CC_A[0], CC_A[1])])
        load_unit(0)

        for u in range(NU):
            sl = u % 2
            tq = tk = tg = tv = unit_tok[sl]
            if u + 1 < NU:
                load_unit(u + 1)
            p.op("vector", lambda v: v.memset(S_f[:], 0.0), waits=[(s_mm, s_mm.n)], inc=s_v)
            p.op("vector", lambda v: v.memset(S_b[:], 0.0), inc=s_v)
            p.op("vector", lambda v: v.drain(), inc=s_v)
            s_ready = s_v.n
            s_ready_mm = 0
            prev_chunk_mm = s_mm.n

            for n in range(NCHU):
                q_sl = partial(lambda s_, n_: s_[:, n_ * 128:(n_ + 1) * 128], qs[sl], n)
                k_sl = partial(lambda s_, n_: s_[:, n_ * 128:(n_ + 1) * 128], ks[sl], n)
                g_sl = partial(lambda s_, n_: s_[:, n_ * 128:(n_ + 1) * 128], gs[sl], n)
                v_sl = partial(lambda s_, n_: s_[:, n_ * DV:(n_ + 1) * DV], vs[sl], n)
                p.op("vector", partial(lambda v, gsl=None: v.tensor_tensor_scan(
                        gkc[:], gsl(), gsl(), 0.0, ALU.add, ALU.bypass), gsl=g_sl),
                     waits=[(s_u[sl], tg)], inc=s_v)
                p.op("vector", lambda v: v.drain(), inc=s_v)
                scan_done = s_v.n
                p.op("scalar", lambda s: s.activation(eq[:], gkc[:], AF.Exp, scale=1.0 / 16),
                     waits=[(s_v, scan_done)], inc=s_sc)
                p.op("scalar", lambda s: s.activation(eki[:], gkc[:], AF.Exp, scale=-1.0 / 16), inc=s_sc)
                p.op("scalar", lambda s: s.activation(sigc[:], gkc[:, 127:128], AF.Exp, scale=1.0 / 16), inc=s_sc)
                exps_done = s_sc.n
                p.op("vector", partial(lambda v, qsl=None: v.tensor_tensor(qd[:], qsl(), eq[:], ALU.mult), qsl=q_sl),
                     waits=[(s_sc, exps_done), (s_u[sl], tq), (s_mm, prev_chunk_mm)], inc=s_v)
                p.op("vector", partial(lambda v, ksl=None: v.scalar_tensor_tensor(
                        kexp[:], ksl(), 1.0 / 16, eki[:], ALU.mult, ALU.mult), ksl=k_sl),
                     waits=[(s_u[sl], tk), (s_mm, prev_chunk_mm)], inc=s_v)
                p.op("vector", lambda v: v.drain(), inc=s_v)
                qdkexp_done = s_v.n
                ba = n % 2
                tok_a = mm_group(ba, [partial(
                    lambda t, ba_=0: t.matmul(pb[ba_][:, 0:128], kexp[:], qd[:], start=True, stop=True), ba_=ba)],
                    waits_first=[(s_v, qdkexp_done)])
                bt = 2 + n % 2
                tok_t = mm_group(bt, [partial(
                    lambda t, bt_=0: t.transpose(pb[bt_][:, 0:64].bitcast(BF16), kexp[:], ident[:]), bt_=bt)],
                    waits_first=[(s_v, qdkexp_done), (s_const, CONSTS)])
                ev_am = p.op("vector", partial(
                    lambda v, ba_=0: v.tensor_tensor(amask[:], pb[ba_][:, 0:128], ut01[:], ALU.mult), ba_=ba),
                    waits=[(s_mm, tok_a), (s_const, CONSTS)], inc=s_v)
                bank_consumed(ba, s_v, ev_am)
                ev_kp = p.op("scalar", partial(
                    lambda s, bt_=0: s.copy(kexpp[:], pb[bt_][:, 0:64].bitcast(BF16)), bt_=bt),
                    waits=[(s_mm, tok_t)], inc=s_sc)
                bank_consumed(bt, s_sc, ev_kp)
                bu = 4 + n % 2
                tok_u = mm_group(bu, [partial(
                    lambda t, bu_=0, vsl=None: t.matmul(pb[bu_][:, 0:DV], kexpp[:], vsl(), start=True, stop=True),
                    bu_=bu, vsl=v_sl)],
                    waits_first=[(s_sc, ev_kp), (s_u[sl], tv)])
                bo = 6 + n % 2
                p.op("vector", lambda v: v.drain(), waits=[(s_v, s_v.n)], inc=s_v)
                am_done = s_v.n
                tok_o = mm_group(bo, [
                    partial(lambda t, bo_=0: t.matmul(pb[bo_][:, 0:DV], qd[:], S_b[:], start=True, stop=False), bo_=bo),
                    partial(lambda t, bo_=0, vsl=None: t.matmul(pb[bo_][:, 0:DV], amask[:], vsl(), start=False, stop=True),
                            bo_=bo, vsl=v_sl),
                ], waits_first=[(s_v, am_done), (s_v, s_ready)])
                s_ready_mm = tok_o
                p.op("vector", partial(lambda v, bu_=0: v.tensor_tensor(
                        S_f[:], S_f[:], pb[bu_][:, 0:DV], ALU.add), bu_=bu),
                     waits=[(s_mm, tok_u), (s_sc, exps_done)], inc=s_v)
                ev_su = p.op("vector", lambda v: v.tensor_scalar_mul(S_f[:], S_f[:], sigc[:, 0:1]), inc=s_v)
                bank_consumed(bu, s_v, ev_su)
                p.op("vector", lambda v: v.tensor_copy(S_b[:], S_f[:]),
                     waits=[(s_mm, s_ready_mm)], inc=s_v)
                p.op("vector", lambda v: v.drain(), inc=s_v)
                s_ready = s_v.n
                p.op("vector", partial(lambda v, bo_=0: v.reduce_sum(ostat[:, 0:1], pb[bo_][:, 0:DV], AX.X), bo_=bo),
                     waits=[(s_mm, tok_o)], inc=s_v)
                p.op("vector", lambda v: v.drain(), inc=s_v)
                osum_done = s_v.n
                p.op("scalar", partial(lambda s, bo_=0: s.activation(
                        sqscr[:], pb[bo_][:, 0:DV], AF.Square, accum_out=ostat[:, 1:2]), bo_=bo),
                     waits=[(s_mm, tok_o)], inc=s_sc)
                p.op("scalar", lambda s: s.mul(ostat[:, 2:3], ostat[:, 0:1], 1.0 / DV),
                     waits=[(s_v, osum_done)], inc=s_sc)
                p.op("scalar", lambda s: s.mul(ostat[:, 3:4], ostat[:, 1:2], 1.0 / DV), inc=s_sc)
                mu_done = s_sc.n
                p.op("vector", lambda v: v.tensor_tensor(ostat[:, 4:5], ostat[:, 2:3], ostat[:, 2:3], ALU.mult),
                     waits=[(s_sc, mu_done)], inc=s_v)
                p.op("vector", lambda v: v.tensor_tensor(ostat[:, 5:6], ostat[:, 3:4], ostat[:, 4:5], ALU.subtract), inc=s_v)
                p.op("vector", lambda v: v.drain(), inc=s_v)
                var_done = s_v.n
                p.op("scalar", lambda s: s.activation(ostat[:, 6:7], ostat[:, 5:6], AF.Ln, bias=constf[:, 0:1]),
                     waits=[(s_v, var_done), (s_const, CONSTS)], inc=s_sc)
                p.op("scalar", lambda s: s.activation(ostat[:, 7:8], ostat[:, 6:7], AF.Exp, scale=-0.5), inc=s_sc)
                rstd_done = s_sc.n
                osl = (n // 4) % 2
                wnorm = [(s_sc, rstd_done)]
                if n % 4 == 0:
                    wnorm.append((s_od[osl], ost_tok[osl]))
                ev_on = p.op("vector", partial(
                    lambda v, bo_=0, osl_=0, n_=0: v.tensor_scalar(
                        ost[osl_][:, (n_ % 4) * DV:(n_ % 4 + 1) * DV], pb[bo_][:, 0:DV],
                        ostat[:, 2:3], ostat[:, 7:8], ALU.subtract, ALU.mult),
                    bo_=bo, osl_=osl, n_=n), waits=wnorm, inc=s_v)
                bank_consumed(bo, s_v, ev_on)
                if n % 4 == 3:
                    p.op("vector", lambda v: v.drain(), inc=s_v)
                    p.dma("sync", o_o[(n - 3) * 128 + 0:(n + 1) * 128, u * DV:(u + 1) * DV]
                          .rearrange("(c p) d -> p c d", p=128),
                          ost[osl][:].rearrange("p (c d) -> p c d", c=4),
                          waits=[(s_v, s_v.n)], inc=s_od[osl])
                    ost_tok[osl] = s_od[osl].n
                prev_chunk_mm = s_mm.n
            unit_free[sl] = s_mm.n

        fin = [(s, s.n) for s in s_od]
        p.op("sync", lambda sy: sy.nop(), waits=fin)

        barrier("B")
        ccB, nB = collective([(o_o, Go)], "B")
        CC_B = (ccB, nB)
        ident = IDENT; ones1 = ONES1; constf = CONSTF
        bqkv_sb = BQKV; bv_sb = ROWBUF; gate1_sb = GATE1
        stat = STAT; t2_sb = T2; sc2_sb = SC2
        big = BIGA
        xq = XQ; xq2 = XQ
        xnw = XNW; wbuf = WBUF; facc = FACC
        stg = STG; relu_sb = SILU2; pb = PB

        oc_sb = big[:, 0:2048]
        gc_sb = big[:, 2048:4096]
        z_sb = big[:, 4096:6144]
        trst = big[:, 6144:8192]
        xn1_sb = big[:, 0:2048]
        m1g = big[:, 0:8192]
        gb_sb = big[:, 8192:12288].bitcast(F32)
        bb_sb = big[:, 12288:16384].bitcast(F32)

        gam_bf = xnw[0:1, 0:D]
        bet_bf = xnw[0:1, D:2 * D]


        s_const = p.sem("const")
        s_constg = p.sem("constg")
        s_ld = [p.sem(f"ld{i}") for i in range(4)]
        s_wb = [p.sem(f"wb{i}") for i in range(4)]
        s_xw = p.sem("xw")
        s_mm = p.sem("mm")
        s_v = p.sem("v")
        s_sc = p.sem("sc")
        s_g = p.sem("g")
        s_stgd = [p.sem(f"stgd{i}") for i in range(4)]
        s_trd = p.sem("trd")
        s_x01w = p.sem("x01w")
        s_x1s = [p.sem(f"x1s{i}") for i in range(2)]
        s_pfw = p.sem("pfw")
        p.auto = {"vector": s_v, "scalar": s_sc, "gpsimd": s_g}

        bank_writer = [0] * 8
        bank_reader = [0] * 8
        bank_rsem = [None] * 8

        def mm_group(bank_i, mms, waits_first=()):
            w = list(waits_first)
            if bank_rsem[bank_i] is not None:
                w.append((bank_rsem[bank_i], bank_reader[bank_i]))
            for i, fn in enumerate(mms):
                inc = s_mm if i == len(mms) - 1 else None
                p.op("tensor", fn, waits=w if i == 0 else (), inc=inc)
            bank_writer[bank_i] = s_mm.n
            return s_mm.n

        def bank_consumed(bank_i, sem, val):
            bank_rsem[bank_i] = sem
            bank_reader[bank_i] = val

        stg_tok = [(None, 0)] * 4

        def use_tok(pool, idx):
            semval = pool[idx]
            return [] if semval[0] is None else [semval]

        p.dma("sync", ident[:], ident_d[:, :], inc=s_const)
        p.dma("sync", constf[:], constf_d[:, :], inc=s_const)
        p.dma("sync", bqkv_sb[:], bqkv_c[:, :], inc=s_const)
        p.dma("gpsimd", ones1[:], ones1_d[:, :], inc=s_constg)
        p.dma("gpsimd", bv_sb[:], bv_row[:, :], inc=s_constg)
        p.dma("gpsimd", gate1_sb[:].rearrange("p (k j) -> p k j", k=16),
              gate1[:, :].rearrange("(k p) j -> p k j", p=128), inc=s_constg)
        p.dma("gpsimd", gam_bf, gam_row[:, :], inc=s_constg)
        p.dma("gpsimd", bet_bf, bet_row[:, :], inc=s_constg)
        CONSTS = s_const.n
        CONSTG = s_constg.n

        for half, (row, dst) in enumerate([(gam_bf, gb_sb), (bet_bf, bb_sb)]):
            for i in range(4):
                bi = half * 4 + i
                tok = mm_group(bi, [partial(
                    lambda t, bi_=0, i_=0, row_=None: t.matmul(
                        pb[bi_][:, :], ones1[:], row_[0:1, i_ * 512:(i_ + 1) * 512],
                        start=True, stop=True), bi_=bi, i_=i, row_=row)],
                    waits_first=[(s_constg, CONSTG), (s_const, CONSTS)])
                ev = p.op("vector", partial(lambda v, bi_=0, i_=0, dst_=None: v.tensor_copy(
                    dst_[:, i_ * 512:(i_ + 1) * 512], pb[bi_][:, :]), bi_=bi, i_=i, dst_=dst),
                    waits=[(s_mm, tok)], inc=s_v)
                bank_consumed(bi, s_v, ev)

        p.op("sync", lambda sy: sy.nop(), waits=[(CC_B[0], CC_B[1])])
        prev_z = 0
        prev_x01 = 0
        prev_tr = 0
        for c in range(NCH):
            ldw = [(s_v, s_v.n)]
            if c == 0:
                ldw.append((CC_B[0], CC_B[1]))
            t_o = p.op("gpsimd", partial(
                lambda g_, c_=0: g_.dma_start(
                    out=oc_sb[:].rearrange("p (g d) -> p g d", g=2),
                    in_=Go.rearrange("(g r c p) d -> c p g r d", g=2, r=2, c=16)
                    [c_, :, :, bass.ts(SELRV("gpsimd"), 1), :]), c_=c),
                waits=ldw, inc=s_ld[0], inc_amt=16)
            t_g = p.dma("sync", gc_sb, g_t[c * 128:(c + 1) * 128, :], inc=s_ld[1])
            t_x = p.dma("sync", xq[0][:], x[c * 128:(c + 1) * 128, :],
                        waits=[(s_x01w, prev_x01)], inc=s_ld[2])
            t_f = p.dma("sync", xq[1][:], f0[c * 128:(c + 1) * 128, :], inc=s_ld[3])
            p.op("vector", lambda v: v.tensor_tensor(z_sb, gc_sb, oc_sb, ALU.mult),
                 waits=[(s_ld[0], t_o), (s_ld[1], t_g), (s_mm, prev_z)], inc=s_v)
            p.op("vector", lambda v: v.drain(), inc=s_v)
            z_done = s_v.n
            p.op("vector", lambda v: v.tensor_tensor(xq2[0][:], xq[0][:], xq[1][:], ALU.add),
                 waits=[(s_ld[2], t_x), (s_ld[3], t_f), (s_x01w, prev_x01)], inc=s_v)
            p.op("vector", lambda v: v.drain(), inc=s_v)
            prev_x01 = p.dma("sync", x01_d[c * 128:(c + 1) * 128, :], xq2[0][:],
                             waits=[(s_v, s_v.n)], inc=s_x01w)
            for k in range(16):
                bi = k % 2
                tok = mm_group(bi, [partial(
                    lambda t, k_=0, bi_=0: t.transpose(
                        pb[bi_][:, 0:64].bitcast(BF16), z_sb[:, k_ * 128:(k_ + 1) * 128], ident[:]),
                    k_=k, bi_=bi)], waits_first=[(s_v, z_done), (s_const, CONSTS)])
                eng, sem = ("vector", s_v) if k % 2 == 0 else ("scalar", s_sc)
                wv_ = [(s_mm, tok)]
                if k < 2:
                    wv_.append((s_trd, prev_tr))
                fn = (partial(lambda v, k_=0, bi_=0: v.tensor_copy(
                        trst[:, k_ * 128:(k_ + 1) * 128], pb[bi_][:, 0:64].bitcast(BF16)), k_=k, bi_=bi)
                      if eng == "vector" else
                      partial(lambda s, k_=0, bi_=0: s.copy(
                        trst[:, k_ * 128:(k_ + 1) * 128], pb[bi_][:, 0:64].bitcast(BF16)), k_=k, bi_=bi))
                ev = p.op(eng, fn, waits=wv_, inc=sem)
                bank_consumed(bi, sem, ev)
            prev_z = s_mm.n
            prev_tr = p.dma("sync",
                            zT_d[:, :].rearrange("(k p) t -> p k t", p=128)[:, :, c * 128:(c + 1) * 128],
                            trst.rearrange("p (k j) -> p k j", k=16),
                            waits=[(s_v, s_v.n), (s_sc, s_sc.n)], inc=s_trd)

        X01_ALL = s_x01w.n
        prev_x1 = [0, 0]
        x01_tok = [0, 0]
        for tsl in range(2):
            t0 = tsl * 1024
            t_zw = p.dma("sync", xnw[:].rearrange("p (k t) -> p k t", k=16),
                         zT_d[:, t0:t0 + 1024].rearrange("(k p) t -> p k t", p=128),
                         waits=[(s_mm, s_mm.n), (s_trd, s_trd.n)], inc=s_xw)
            for dq in range(4):
                w_war = [(s_mm, s_mm.n)]
                tw0 = p.dma("gpsimd", wbuf[0][:].rearrange("p (k e) -> p k e", k=8),
                            wo[0:1024, dq * 512:(dq + 1) * 512].rearrange("(k p) e -> p k e", p=128),
                            waits=w_war, inc=s_wb[0])
                tw1 = p.dma("gpsimd", wbuf[1][:].rearrange("p (k e) -> p k e", k=8),
                            wo[1024:2048, dq * 512:(dq + 1) * 512].rearrange("(k p) e -> p k e", p=128),
                            inc=s_wb[1])
                for tc in range(8):
                    bi = 2 + tc % 2
                    mms = []
                    for k in range(16):
                        mms.append(partial(
                            lambda t, k_=0, tc_=0, bi_=0: t.matmul(
                                pb[bi_][:, :],
                                xnw[:, k_ * 1024 + tc_ * 128: k_ * 1024 + (tc_ + 1) * 128],
                                wbuf[k_ // 8][:, (k_ % 8) * 512:(k_ % 8 + 1) * 512],
                                start=(k_ == 0), stop=(k_ == 15)),
                            k_=k, tc_=tc, bi_=bi))
                    tok = mm_group(bi, mms, waits_first=[(s_wb[0], tw0), (s_wb[1], tw1), (s_xw, t_zw)])
                    fo = tc * D + dq * 512
                    ev = p.op("vector", partial(
                        lambda v, bi_=0, fo_=0: v.tensor_copy(facc[:, fo_:fo_ + 512], pb[bi_][:, :]),
                        bi_=bi, fo_=fo), waits=[(s_mm, tok)], inc=s_v)
                    bank_consumed(bi, s_v, ev)
            for tc in range(8):
                xsl = tc % 2
                x01_tok[xsl] = p.dma("sync", xq[xsl][:],
                                     x01_d[t0 + tc * 128: t0 + (tc + 1) * 128, :],
                                     waits=[(s_v, s_v.n), (s_x01w, X01_ALL),
                                            (s_x1s[xsl], prev_x1[xsl])], inc=s_ld[2 + xsl])
                xb = xq2[xsl]
                p.op("vector", partial(lambda v, xb_=None, xsl_=0, tc_=0: v.tensor_tensor(
                    xb_[:], xq[xsl_][:], facc[:, tc_ * D:(tc_ + 1) * D], ALU.add),
                    xb_=xb, xsl_=xsl, tc_=tc),
                    waits=[(s_ld[2 + xsl], x01_tok[xsl]), (s_x1s[xsl], prev_x1[xsl])], inc=s_v)
                p.op("vector", lambda v: v.drain(), inc=s_v)
                p.op("vector", partial(lambda v, xb_=None: v.reduce_sum(
                    stat[:, 0:1], xb_[:], AX.X), xb_=xb), inc=s_v)
                p.op("vector", lambda v: v.drain(), inc=s_v)
                sx_done = s_v.n
                p.op("scalar", partial(lambda s, xb_=None: s.activation(
                    xn1_sb, xb_[:], AF.Square, accum_out=stat[:, 1:2]), xb_=xb),
                    waits=[(s_v, sx_done), (s_mm, prev_z)], inc=s_sc)
                sq_done = s_sc.n
                p.op("scalar", lambda s: s.mul(stat[:, 2:3], stat[:, 0:1], 1.0 / D), inc=s_sc)
                mu_done = s_sc.n
                p.op("vector", lambda v: v.tensor_scalar_mul(stat[:, 3:4], stat[:, 1:2], 1.0 / D),
                     waits=[(s_sc, sq_done)], inc=s_v)
                p.op("vector", lambda v: v.tensor_tensor(stat[:, 4:5], stat[:, 2:3], stat[:, 2:3],
                                                         ALU.mult), waits=[(s_sc, mu_done)], inc=s_v)
                p.op("vector", lambda v: v.tensor_tensor(stat[:, 5:6], stat[:, 3:4], stat[:, 4:5],
                                                         ALU.subtract), inc=s_v)
                p.op("vector", lambda v: v.drain(), inc=s_v)
                var_done = s_v.n
                p.op("scalar", lambda s: s.activation(stat[:, 6:7], stat[:, 5:6], AF.Ln,
                                                      bias=constf[:, 0:1]),
                     waits=[(s_v, var_done), (s_const, CONSTS)], inc=s_sc)
                p.op("scalar", lambda s: s.activation(stat[:, 7:8], stat[:, 6:7], AF.Exp,
                                                      scale=-0.5), inc=s_sc)
                rstd_done = s_sc.n
                p.op("vector", partial(lambda v, xb_=None: v.tensor_scalar(
                    xn1_sb, xb_[:], stat[:, 2:3], stat[:, 7:8], ALU.subtract, ALU.mult), xb_=xb),
                    waits=[(s_sc, rstd_done)], inc=s_v)
                p.op("vector", lambda v: v.drain(), inc=s_v)
                xn_done = s_v.n
                prev_x1[xsl] = p.dma("sync", x1_o[t0 + tc * 128: t0 + (tc + 1) * 128, :],
                                     xb[:], waits=[(s_v, xn_done)], inc=s_x1s[xsl])
                for k in range(16):
                    bi2 = 4 + k % 2
                    tok2 = mm_group(bi2, [partial(
                        lambda t, k_=0, bi_=0: t.transpose(
                            pb[bi_][:, 0:64].bitcast(BF16),
                            xn1_sb[:, k_ * 128:(k_ + 1) * 128], ident[:]),
                        k_=k, bi_=bi2)], waits_first=[(s_v, xn_done), (s_const, CONSTS)])
                    eng, sem = ("vector", s_v) if k % 2 == 0 else ("scalar", s_sc)
                    wv_ = [(s_mm, tok2)]
                    if k < 2:
                        wv_.append((s_trd, prev_tr))
                    fn = (partial(lambda v, k_=0, bi_=0: v.tensor_copy(
                            trst[:, k_ * 128:(k_ + 1) * 128], pb[bi_][:, 0:64].bitcast(BF16)),
                            k_=k, bi_=bi2)
                          if eng == "vector" else
                          partial(lambda s, k_=0, bi_=0: s.copy(
                            trst[:, k_ * 128:(k_ + 1) * 128], pb[bi_][:, 0:64].bitcast(BF16)),
                            k_=k, bi_=bi2))
                    ev2 = p.op(eng, fn, waits=wv_, inc=sem)
                    bank_consumed(bi2, sem, ev2)
                prev_z = s_mm.n
                prev_tr = p.dma("sync",
                                xn1T_d[:, :].rearrange("(k p) t -> p k t", p=128)
                                [:, :, t0 + tc * 128: t0 + (tc + 1) * 128],
                                trst.rearrange("p (k j) -> p k j", k=16),
                                waits=[(s_v, s_v.n), (s_sc, s_sc.n)], inc=s_trd)

        X1_ALL = [0, 0]
        x1pf_tok = 0
        relu_rd = [0, 0]
        down_prev = 0
        for tsl in range(2):
            t0 = tsl * 1024
            X1_ALL = [(s_x1s[0], s_x1s[0].n), (s_x1s[1], s_x1s[1].n)]
            t_xw = p.dma("sync", xnw[:].rearrange("p (k t) -> p k t", k=16),
                         xn1T_d[:, t0:t0 + 1024].rearrange("(k p) t -> p k t", p=128),
                         waits=[(s_mm, s_mm.n), (s_trd, s_trd.n)], inc=s_xw)

            def proj_qk(colbase, out_dram, bias_col0):
                for eh in range(4):
                    w_war = [(s_mm, s_mm.n)]
                    tw0 = p.dma("gpsimd", wbuf[0][:].rearrange("p (k e) -> p k e", k=8),
                                wqkvT[0:1024, colbase + eh * 512: colbase + (eh + 1) * 512]
                                .rearrange("(k p) e -> p k e", p=128), waits=w_war, inc=s_wb[0])
                    tw1 = p.dma("gpsimd", wbuf[1][:].rearrange("p (k e) -> p k e", k=8),
                                wqkvT[1024:2048, colbase + eh * 512: colbase + (eh + 1) * 512]
                                .rearrange("(k p) e -> p k e", p=128), inc=s_wb[1])
                    for ec in range(4):
                        for ts in range(2):
                            bi = 2 + (ec * 2 + ts) % 2
                            mms = []
                            for k in range(16):
                                mms.append(partial(
                                    lambda t, k_=0, ec_=0, ts_=0, bi_=0: t.matmul(
                                        pb[bi_][:, :],
                                        wbuf[k_ // 8][:, (k_ % 8) * 512 + ec_ * 128:
                                                      (k_ % 8) * 512 + (ec_ + 1) * 128],
                                        xnw[:, k_ * 1024 + ts_ * 512: k_ * 1024 + (ts_ + 1) * 512],
                                        start=(k_ == 0), stop=(k_ == 15)),
                                    k_=k, ec_=ec, ts_=ts, bi_=bi))
                            tok = mm_group(bi, mms, waits_first=[(s_wb[0], tw0), (s_wb[1], tw1),
                                                                 (s_xw, t_xw)])
                            si = (ec * 2 + ts) % 4
                            ecg = eh * 4 + ec
                            w_stg = use_tok(stg_tok, si) + [(s_mm, tok), (s_const, CONSTS)]
                            ev = p.op("vector", partial(
                                lambda v, bi_=0, si_=0, bc_=0: v.tensor_scalar(
                                    stg[si_][:], pb[bi_][:, :], bqkv_sb[:, bc_:bc_ + 1], None, ALU.add),
                                bi_=bi, si_=si, bc_=bias_col0 + ecg), waits=w_stg, inc=s_v)
                            bank_consumed(bi, s_v, ev)
                            dt = p.dma("sync",
                                       out_dram[eh * 512 + ec * 128: eh * 512 + (ec + 1) * 128,
                                                t0 + ts * 512: t0 + (ts + 1) * 512],
                                       stg[si][:], waits=[(s_v, ev)], inc=s_stgd[si])
                            stg_tok[si] = (s_stgd[si], dt)

            proj_qk(0, q1T_o, 0)
            proj_qk(2048, k1T_o, 16)

            for dq in range(4):
                w_war = [(s_mm, s_mm.n)]
                tw0 = p.dma("gpsimd", wbuf[0][:].rearrange("p (k e) -> p k e", k=8),
                            wqkvT[0:1024, 4096 + dq * 512: 4096 + (dq + 1) * 512]
                            .rearrange("(k p) e -> p k e", p=128), waits=w_war, inc=s_wb[0])
                tw1 = p.dma("gpsimd", wbuf[1][:].rearrange("p (k e) -> p k e", k=8),
                            wqkvT[1024:2048, 4096 + dq * 512: 4096 + (dq + 1) * 512]
                            .rearrange("(k p) e -> p k e", p=128), inc=s_wb[1])
                for tc in range(8):
                    bi = 2 + tc % 2
                    mms = []
                    for k in range(16):
                        mms.append(partial(
                            lambda t, k_=0, tc_=0, bi_=0: t.matmul(
                                pb[bi_][:, :],
                                xnw[:, k_ * 1024 + tc_ * 128: k_ * 1024 + (tc_ + 1) * 128],
                                wbuf[k_ // 8][:, (k_ % 8) * 512:(k_ % 8 + 1) * 512],
                                start=(k_ == 0), stop=False),
                            k_=k, tc_=tc, bi_=bi))
                    mms.append(partial(
                        lambda t, bi_=0, dq_=0: t.matmul(
                            pb[bi_][:, :], ones1[:], bv_sb[0:1, dq_ * 512:(dq_ + 1) * 512],
                            start=False, stop=True), bi_=bi, dq_=dq))
                    tok = mm_group(bi, mms, waits_first=[(s_wb[0], tw0), (s_wb[1], tw1), (s_xw, t_xw),
                                                         (s_const, CONSTS), (s_constg, CONSTG)])
                    si = tc % 4
                    w_stg = use_tok(stg_tok, si) + [(s_mm, tok)]
                    ev = p.op("vector", partial(
                        lambda v, bi_=0, si_=0: v.tensor_copy(stg[si_][:], pb[bi_][:, :]),
                        bi_=bi, si_=si), waits=w_stg, inc=s_v)
                    bank_consumed(bi, s_v, ev)
                    dt = p.dma("sync", v1_o[t0 + tc * 128: t0 + (tc + 1) * 128, dq * 512:(dq + 1) * 512],
                               stg[si][:], waits=[(s_v, ev)], inc=s_stgd[si])
                    stg_tok[si] = (s_stgd[si], dt)

            if tsl == 1:
                ccC_t = p.sem("ccCe")
                wC = [(s, s.n) for s in s_stgd]
                for a_, b_ in ((q1T_o, Gq1), (k1T_o, Gk1), (v1_o, Gv1)):
                    p.op("gpsimd", partial(
                        lambda g, a=None, b=None: g.collective_compute(
                            "AllGather", ALU.bypass,
                            replica_groups=[[0, 1], [2, 3], [4, 5], [6, 7]],
                            ins=[a.ap().opt()], outs=[b.ap().opt()]), a=a_, b=b_),
                        waits=wC, inc=ccC_t)
                    wC = []

            for dq in range(4):
                w_war = [(s_mm, s_mm.n)]
                tw0 = p.dma("gpsimd", wbuf[0][:].rearrange("p (k e) -> p k e", k=8),
                            expw[0:1024, dq * 512:(dq + 1) * 512].rearrange("(k p) e -> p k e", p=128),
                            waits=w_war, inc=s_wb[0])
                tw1 = p.dma("gpsimd", wbuf[1][:].rearrange("p (k e) -> p k e", k=8),
                            expw[1024:2048, dq * 512:(dq + 1) * 512].rearrange("(k p) e -> p k e", p=128),
                            inc=s_wb[1])
                for tc in range(8):
                    if dq == 0:
                        mmsl = [partial(
                            lambda t, k_=0, tc_=0: t.matmul(
                                pb[6][:, 0:8],
                                xnw[:, k_ * 1024 + tc_ * 128: k_ * 1024 + (tc_ + 1) * 128],
                                gate1_sb[:, k_ * 8:(k_ + 1) * 8], start=(k_ == 0), stop=(k_ == 15)),
                            k_=k, tc_=tc) for k in range(16)]
                        tokl = mm_group(6, mmsl, waits_first=[(s_xw, t_xw), (s_constg, CONSTG)])
                        ev_mx = p.op("vector", lambda v: v.max(t2_sb[:], pb[6][:, 0:8]),
                                     waits=[(s_mm, tokl)], inc=s_v)
                        bank_consumed(6, s_v, ev_mx)
                        p.op("vector", lambda v: v.drain(), inc=s_v)
                        mx_done = s_v.n
                        p.op("scalar", partial(lambda s, tc_=0: s.activation(
                            sc2_sb[:, 0:2], t2_sb[:, 0:2], AF.Copy, accum_out=stat[:, 8 + tc_:9 + tc_]),
                            tc_=tc), waits=[(s_v, mx_done)], inc=s_sc)
                    scale_done = s_sc.n
                    bi = 2 + tc % 2
                    mms = []
                    for k in range(16):
                        mms.append(partial(
                            lambda t, k_=0, tc_=0, bi_=0: t.matmul(
                                pb[bi_][:, :],
                                xnw[:, k_ * 1024 + tc_ * 128: k_ * 1024 + (tc_ + 1) * 128],
                                wbuf[k_ // 8][:, (k_ % 8) * 512:(k_ % 8 + 1) * 512],
                                start=(k_ == 0), stop=(k_ == 15)),
                            k_=k, tc_=tc, bi_=bi))
                    tok = mm_group(bi, mms, waits_first=[(s_wb[0], tw0), (s_wb[1], tw1), (s_xw, t_xw)])
                    fo = tc * D + dq * 512
                    ev = p.op("vector", partial(
                        lambda v, bi_=0, fo_=0, tc_=0: v.tensor_scalar_mul(
                            facc[:, fo_:fo_ + 512], pb[bi_][:, :], stat[:, 8 + tc_:9 + tc_]),
                        bi_=bi, fo_=fo, tc_=tc),
                        waits=[(s_mm, tok), (s_sc, scale_done), (s_pfw, x1pf_tok)], inc=s_v)
                    bank_consumed(bi, s_v, ev)

            for ig in range(8):
                i0 = ig * 1024
                w_war = [(s_mm, s_mm.n)]
                twu0 = p.dma("gpsimd", wbuf[0][:].rearrange("p (k e) -> p k e", k=8),
                             upw[0:1024, i0:i0 + 512].rearrange("(k p) e -> p k e", p=128),
                             waits=w_war, inc=s_wb[0])
                twu1 = p.dma("gpsimd", wbuf[1][:].rearrange("p (k e) -> p k e", k=8),
                             upw[1024:2048, i0:i0 + 512].rearrange("(k p) e -> p k e", p=128),
                             inc=s_wb[1])
                twu2 = p.dma("gpsimd", wbuf[2][:].rearrange("p (k e) -> p k e", k=8),
                             upw[0:1024, i0 + 512:i0 + 1024].rearrange("(k p) e -> p k e", p=128),
                             inc=s_wb[2])
                twu3 = p.dma("gpsimd", wbuf[3][:].rearrange("p (k e) -> p k e", k=8),
                             upw[1024:2048, i0 + 512:i0 + 1024].rearrange("(k p) e -> p k e", p=128),
                             inc=s_wb[3])
                for ih in range(2):
                    wlo, whi = (0, 1) if ih == 0 else (2, 3)
                    tlo = twu0 if ih == 0 else twu2
                    thi = twu1 if ih == 0 else twu3
                    for ic in range(4):
                        for tw in range(2):
                            bi = ic % 2
                            mms = []
                            for k in range(16):
                                mms.append(partial(
                                    lambda t, k_=0, ic_=0, tw_=0, lo_=0, hi_=0, bi_=0: t.matmul(
                                        pb[bi_][:, :],
                                        wbuf[lo_ if k_ < 8 else hi_]
                                        [:, (k_ % 8) * 512 + ic_ * 128:(k_ % 8) * 512 + (ic_ + 1) * 128],
                                        xnw[:, k_ * 1024 + tw_ * 512: k_ * 1024 + (tw_ + 1) * 512],
                                        start=(k_ == 0), stop=(k_ == 15)),
                                    k_=k, ic_=ic, tw_=tw, lo_=wlo, hi_=whi, bi_=bi))
                            tok = mm_group(bi, mms, waits_first=[(s_wb[wlo], tlo), (s_wb[whi], thi),
                                                                 (s_xw, t_xw)])
                            rsl = tw % 2
                            ev_r = p.op("scalar", partial(
                                lambda s, bi_=0, rsl_=0: s.activation(relu_sb[rsl_][:], pb[bi_][:, :],
                                                                      AF.Relu), bi_=bi, rsl_=rsl),
                                waits=[(s_mm, tok), (s_v, relu_rd[rsl])], inc=s_sc)
                            bank_consumed(bi, s_sc, ev_r)
                            mo = (ih * 4 + ic) * 1024 + tw * 512
                            ev_m = p.op("vector", partial(
                                lambda v, rsl_=0, mo_=0: v.tensor_tensor(
                                    m1g[:, mo_:mo_ + 512], relu_sb[rsl_][:], relu_sb[rsl_][:], ALU.mult),
                                rsl_=rsl, mo_=mo),
                                waits=[(s_sc, ev_r), (s_mm, down_prev)], inc=s_v)
                            relu_rd[rsl] = ev_m
                p.op("vector", lambda v: v.drain(), inc=s_v)
                m_done = s_v.n
                for dq in range(4):
                    wsl = dq % 2
                    twd = p.dma("gpsimd", wbuf[wsl][:].rearrange("p (i e) -> p i e", i=8),
                                downw[i0:i0 + 1024, dq * 512:(dq + 1) * 512]
                                .rearrange("(i p) e -> p i e", p=128),
                                waits=[(s_mm, s_mm.n)], inc=s_wb[wsl])
                    for tc in range(8):
                        bi = 2 + tc % 4
                        mms = []
                        for ic in range(8):
                            mms.append(partial(
                                lambda t, ic_=0, tc_=0, bi_=0, wsl_=0: t.matmul(
                                    pb[bi_][:, :],
                                    m1g[:, ic_ * 1024 + tc_ * 128: ic_ * 1024 + (tc_ + 1) * 128],
                                    wbuf[wsl_][:, ic_ * 512:(ic_ + 1) * 512],
                                    start=(ic_ == 0), stop=(ic_ == 7)),
                                ic_=ic, tc_=tc, bi_=bi, wsl_=wsl))
                        tok = mm_group(bi, mms, waits_first=[(s_wb[wsl], twd), (s_v, m_done)])
                        fo = tc * D + dq * 512
                        ev = p.op("vector", partial(
                            lambda v, bi_=0, fo_=0: v.tensor_tensor(
                                facc[:, fo_:fo_ + 512], facc[:, fo_:fo_ + 512], pb[bi_][:, :], ALU.add),
                            bi_=bi, fo_=fo), waits=[(s_mm, tok)], inc=s_v)
                        bank_consumed(bi, s_v, ev)
                down_prev = s_mm.n

            for tc in range(8):
                xsl = tc % 2
                t_x1 = p.dma("sync", xq[xsl][:], x1_o[t0 + tc * 128: t0 + (tc + 1) * 128, :],
                             waits=[(s_v, s_v.n)] + X1_ALL, inc=s_ld[2 + xsl])
                fb = partial(lambda tc_=0: facc[:, tc_ * D:(tc_ + 1) * D], tc_=tc)
                p.op("vector", partial(lambda v, fb_=None: v.reduce_sum(stat[:, 0:1], fb_(), AX.X),
                                       fb_=fb), inc=s_v)
                p.op("vector", lambda v: v.drain(), inc=s_v)
                sx_done = s_v.n
                p.op("scalar", partial(lambda s, fb_=None: s.activation(
                    xn1_sb, fb_(), AF.Square, accum_out=stat[:, 1:2]), fb_=fb),
                    waits=[(s_v, sx_done)], inc=s_sc)
                sq_done = s_sc.n
                p.op("scalar", lambda s: s.mul(stat[:, 2:3], stat[:, 0:1], 1.0 / D), inc=s_sc)
                mu_done = s_sc.n
                p.op("vector", lambda v: v.tensor_scalar_mul(stat[:, 3:4], stat[:, 1:2], 1.0 / D),
                     waits=[(s_sc, sq_done)], inc=s_v)
                p.op("vector", lambda v: v.tensor_tensor(stat[:, 4:5], stat[:, 2:3], stat[:, 2:3],
                                                         ALU.mult), waits=[(s_sc, mu_done)], inc=s_v)
                p.op("vector", lambda v: v.tensor_tensor(stat[:, 5:6], stat[:, 3:4], stat[:, 4:5],
                                                         ALU.subtract), inc=s_v)
                p.op("vector", lambda v: v.drain(), inc=s_v)
                var_done = s_v.n
                p.op("scalar", lambda s: s.activation(stat[:, 6:7], stat[:, 5:6], AF.Ln,
                                                      bias=constf[:, 0:1]),
                     waits=[(s_v, var_done), (s_const, CONSTS)], inc=s_sc)
                p.op("scalar", lambda s: s.activation(stat[:, 7:8], stat[:, 6:7], AF.Exp, scale=-0.5),
                     inc=s_sc)
                rstd_done = s_sc.n
                p.op("vector", partial(lambda v, fb_=None: v.tensor_scalar(
                    fb_(), fb_(), stat[:, 2:3], stat[:, 7:8], ALU.subtract, ALU.mult), fb_=fb),
                    waits=[(s_sc, rstd_done)], inc=s_v)
                p.op("vector", partial(lambda v, fb_=None: v.tensor_tensor(fb_(), fb_(), gb_sb,
                                                                           ALU.mult), fb_=fb), inc=s_v)
                p.op("vector", partial(lambda v, fb_=None: v.tensor_tensor(fb_(), fb_(), bb_sb,
                                                                           ALU.add), fb_=fb), inc=s_v)
                p.op("vector", partial(lambda v, fb_=None, xsl_=0: v.tensor_tensor(
                    fb_(), fb_(), xq[xsl_][:], ALU.add), fb_=fb, xsl_=xsl),
                    waits=[(s_ld[2 + xsl], t_x1)], inc=s_v)
                p.op("vector", lambda v: v.drain(), inc=s_v)
                p.dma("sync", x1pf_o[t0 + tc * 128: t0 + (tc + 1) * 128, :], fb(),
                      waits=[(s_v, s_v.n)], inc=s_pfw)
            x1pf_tok = s_pfw.n

        fin = ([(s, s.n) for s in s_stgd] + [(s_pfw, s_pfw.n), (s_x1s[0], s_x1s[0].n),
                (s_x1s[1], s_x1s[1].n), (s_x01w, s_x01w.n), (s_trd, s_trd.n)])
        p.op("sync", lambda sy: sy.nop(), waits=fin)

        barrier("C")
        CC_C = (ccC_t, 3)
        NU = 8; SQ = 4096
        ones1 = ONES1; onescol = ONESCOL
        dmask = XNW[:, 0:2048]
        qsl_ = [FACC[:, 0:2048].bitcast(BF16), FACC[:, 6144:8192].bitcast(BF16)]
        ksl_ = [FACC[:, 2048:4096].bitcast(BF16), FACC[:, 8192:10240].bitcast(BF16)]
        vsl_ = [FACC[:, 4096:6144].bitcast(BF16), FACC[:, 10240:12288].bitcast(BF16)]
        sq_sb = XNW[:, 2048:6144]
        nrm_sb = XNW[:, 6144:14336].bitcast(F32)[0:1, 0:4096]
        mstat = MSTAT; bias_col = BIASC; bias4_col = BIAS4; bias2_col = BIAS2
        pT = [STG[0][:, :], STG[1][:, :], STG[2][:, :], STG[3][:, :]]; rl_sb = RLS; rl_bf = RLBF; rlb_sb = RLB
        ostD = [OSTG[0][:, 0:512], OSTG[1][:, 0:512]]
        pb = PB


        s_const = p.sem("const")
        s_u = [p.sem(f"u{i}") for i in range(2)]
        s_mm = p.sem("mm")
        s_v = p.sem("v")
        s_sc = p.sem("sc")
        s_g = p.sem("g")
        s_od = [p.sem(f"od{i}") for i in range(2)]
        s_dbg = p.sem("dbg")
        p.auto = {"vector": s_v, "scalar": s_sc, "gpsimd": s_g}

        bank_writer = [0] * 8
        bank_reader = [0] * 8
        bank_rsem = [None] * 8

        def mm_one(bank_i, fn, waits=(), close=True, first=True):
            w = list(waits)
            if first and bank_rsem[bank_i] is not None:
                w.append((bank_rsem[bank_i], bank_reader[bank_i]))
            tok = p.op("tensor", fn, waits=w, inc=(s_mm if close else None))
            if close:
                bank_writer[bank_i] = s_mm.n
            return s_mm.n

        def bank_consumed(bank_i, sem, val):
            bank_rsem[bank_i] = sem
            bank_reader[bank_i] = val

        p.dma("sync", ones1[:], ones1_d[:, :], inc=s_const)
        p.dma("sync", onescol[:], onescol_d[:, :], inc=s_const)
        p.dma("sync", dmask[:], dmask_d[:, :], inc=s_const)
        CONSTS = s_const.n

        unit_tok = [0, 0]
        unit_free = [0, 0]
        ostD_tok = [0, 0]

        def load_unit(u):
            sl = u % 2
            w = [(s_mm, unit_free[sl])]
            if u == 0:
                w.append((CC_C[0], CC_C[1]))
            p.op("scalar", partial(
                lambda te, sl_=0, u_=0: te.dma_start(
                    out=qsl_[sl_][:].rearrange("p (s d) -> p s d", s=2),
                    in_=Gq1.rearrange("(s r u p) d -> u p s r d", s=2, r=2, u=8)
                    [u_, :, :, bass.ts(SELRV("scalar"), 1), :]),
                sl_=sl, u_=u), waits=w, inc=s_u[sl], inc_amt=16)
            p.op("scalar", partial(
                lambda te, sl_=0, u_=0: te.dma_start(
                    out=ksl_[sl_][:].rearrange("p (s d) -> p s d", s=2),
                    in_=Gk1.rearrange("(s r u p) d -> u p s r d", s=2, r=2, u=8)
                    [u_, :, :, bass.ts(SELRV("scalar"), 1), :]),
                sl_=sl, u_=u), inc=s_u[sl], inc_amt=16)
            tv = p.op("scalar", partial(
                lambda te, sl_=0, u_=0: te.dma_start(
                    out=vsl_[sl_][:].rearrange("p (s n d) -> p s n d", s=2, n=16),
                    in_=Gv1.rearrange("(s n p) (r u w) -> u p s n r w", s=2, n=16, r=2, u=8)
                    [u_, :, :, :, bass.ts(SELRV("scalar"), 1), :]),
                sl_=sl, u_=u), inc=s_u[sl], inc_amt=16)
            unit_tok[sl] = tv
        load_unit(0)

        for u in range(NU):
            sl = u % 2
            t_u = unit_tok[sl]
            if u + 1 < NU:
                load_unit(u + 1)
            for which in range(2):
                slab = qsl_[sl] if which == 0 else ksl_[sl]
                p.op("scalar", partial(lambda s, sb_=None: s.activation(sq_sb[:], sb_[:], AF.Square),
                                       sb_=slab),
                     waits=[(s_u[sl], t_u), (s_mm, s_mm.n), (s_v, s_v.n)], inc=s_sc)
                sq_done = s_sc.n
                for j in range(8):
                    bi = 2 + j % 2
                    tok = mm_one(bi, partial(
                        lambda t, j_=0, bi_=0: t.matmul(
                            pb[bi_][0:1, :], onescol[:], sq_sb[:, j_ * 512:(j_ + 1) * 512],
                            start=True, stop=True), j_=j, bi_=bi),
                        waits=[(s_sc, sq_done), (s_const, CONSTS)])
                    ev = p.op("vector", partial(
                        lambda v, j_=0, bi_=0: v.tensor_copy(nrm_sb[0:1, j_ * 512:(j_ + 1) * 512],
                                                             pb[bi_][0:1, :]), j_=j, bi_=bi),
                        waits=[(s_mm, tok)], inc=s_v)
                    bank_consumed(bi, s_v, ev)
                p.op("vector", lambda v: v.drain(), inc=s_v)
                p.op("vector", partial(lambda v, w_=0: v.reduce_max(
                    mstat[0:1, w_:w_ + 1], nrm_sb[0:1, :], AX.X), w_=which), inc=s_v)
            p.op("vector", lambda v: v.tensor_tensor(mstat[0:1, 2:3], mstat[0:1, 0:1],
                                                     mstat[0:1, 1:2], ALU.add), inc=s_v)
            p.op("vector", lambda v: v.drain(), inc=s_v)
            m_done = s_v.n
            p.op("scalar", lambda s: s.copy(rl_bf[0:1, 0:1], mstat[0:1, 2:3]),
                 waits=[(s_v, m_done)], inc=s_sc)
            mb_done = s_sc.n
            tokb = mm_one(7, lambda t: t.matmul(pb[7][:, 0:1], ones1[:], rl_bf[0:1, 0:1],
                                                start=True, stop=True),
                          waits=[(s_sc, mb_done), (s_const, CONSTS)])
            ev_b = p.op("vector", lambda v: v.tensor_scalar_mul(
                bias_col[:], pb[7][:, 0:1], -0.5 / math.sqrt(128.0)),
                waits=[(s_mm, tokb)], inc=s_v)
            bank_consumed(7, s_v, ev_b)
            p.op("vector", lambda v: v.drain(), inc=s_v)
            bias_done = s_v.n

            pconsumed = [0, 0, 0, 0]
            dbg_tok = 0
            p.op("vector", lambda v: v.tensor_scalar_mul(mstat[0:1, 3:4], mstat[0:1, 2:3],
                                                         0.5 / math.sqrt(128.0)),
                 waits=[(s_v, m_done)], inc=s_v)
            p.op("vector", lambda v: v.tensor_scalar_mul(bias4_col[:], bias_col[:], 0.25),
                 waits=[(s_v, bias_done)], inc=s_v)
            p.op("vector", lambda v: v.drain(), inc=s_v)
            bias4_done = s_v.n
            for iq in range(8):
                nkc = 4 * (iq + 1)
                bo = 4 + iq % 2
                p1_state = {}

                def p1_head(kc):
                    bs = kc % 2
                    psl = kc % 4
                    tok_s = mm_one(bs, partial(
                        lambda t, kc_=0, iq_=0, bs_=0, sl_=0: t.matmul(
                            pb[bs_][:, :], ksl_[sl_][:, kc_ * 128:(kc_ + 1) * 128],
                            qsl_[sl_][:, iq_ * 512:(iq_ + 1) * 512],
                            start=True, stop=True), kc_=kc, iq_=iq, bs_=bs, sl_=sl),
                        waits=[(s_u[sl], t_u)])
                    ev_e = p.op("scalar", partial(
                        lambda s, bs_=0, psl_=0: s.activation(
                            pT[psl_][:], pb[bs_][:, :], AF.Exp, bias=bias4_col[:, 0:1],
                            scale=0.25 / math.sqrt(128.0)), bs_=bs, psl_=psl),
                        waits=[(s_mm, tok_s), (s_v, bias4_done), (s_mm, pconsumed[psl])], inc=s_sc)
                    bank_consumed(bs, s_sc, ev_e)
                    rsem, rval = s_sc, ev_e
                    j = kc - 4 * iq
                    if j >= 0:
                        p.op("vector", partial(
                            lambda v, psl_=0, j_=0: v.tensor_tensor(
                                pT[psl_][:], pT[psl_][:], dmask[:, j_ * 512:(j_ + 1) * 512], ALU.mult),
                            psl_=psl, j_=j),
                            waits=[(s_sc, ev_e), (s_const, CONSTS)], inc=s_v)
                        p.op("vector", lambda v: v.drain(), inc=s_v)
                        rsem, rval = s_v, s_v.n
                    p1_state[kc] = (rsem, rval)

                def p1_tail(kc):
                    psl = kc % 4
                    wl = [p1_state.pop(kc)]
                    if kc == 0 and bank_rsem[6] is not None:
                        wl.append((bank_rsem[6], bank_reader[6]))
                    p.op("tensor", partial(
                        lambda t, kc_=0, last_=False: t.matmul(
                            pb[6][0:1, :], onescol[:], pT[kc_ % 4][:],
                            start=(kc_ == 0), stop=last_), kc_=kc, last_=(kc == nkc - 1)),
                        waits=wl, inc=s_mm)
                    if kc == nkc - 1:
                        bank_writer[6] = s_mm.n
                    pconsumed[psl] = s_mm.n
                for kc in range(nkc):
                    p1_head(kc)
                    if kc >= 1:
                        p1_tail(kc - 1)
                p1_tail(nkc - 1)
                l4tok = s_mm.n
                p.op("scalar", lambda s: s.activation(rl_sb[0:1, :], pb[6][0:1, :], AF.Ln),
                     waits=[(s_mm, l4tok)], inc=s_sc)
                bank_consumed(6, s_sc, s_sc.n)
                ln_done = s_sc.n
                p.op("vector", lambda v: v.tensor_scalar(rl_sb[0:1, :], rl_sb[0:1, :], 4.0,
                                                         mstat[0:1, 3:4], ALU.mult, ALU.add),
                     waits=[(s_sc, ln_done)], inc=s_v)
                p.op("vector", lambda v: v.reduce_max(mstat[0:1, 4:5], rl_sb[0:1, :], AX.X), inc=s_v)
                p.op("vector", lambda v: v.drain(), inc=s_v)
                mn_done = s_v.n
                p.op("scalar", lambda s: s.copy(rl_bf[0:1, 0:1], mstat[0:1, 4:5]),
                     waits=[(s_v, mn_done)], inc=s_sc)
                mn2_done = s_sc.n
                bb2 = 2 + iq % 2
                tokb2 = mm_one(bb2, partial(
                    lambda t, b_=0: t.matmul(pb[b_][:, 0:1], ones1[:], rl_bf[0:1, 0:1],
                                             start=True, stop=True), b_=bb2),
                    waits=[(s_sc, mn2_done), (s_const, CONSTS)])
                ev_b2 = p.op("vector", partial(
                    lambda v, b_=0: v.tensor_scalar_mul(bias2_col[:], pb[b_][:, 0:1], -1.0), b_=bb2),
                    waits=[(s_mm, tokb2)], inc=s_v)
                bank_consumed(bb2, s_v, ev_b2)
                p.op("vector", lambda v: v.drain(), inc=s_v)
                bias2_done = s_v.n
                p2_state = {}

                def p2_head(kc):
                    bs = kc % 2
                    psl = kc % 4
                    tok_s = mm_one(bs, partial(
                        lambda t, kc_=0, iq_=0, bs_=0, sl_=0: t.matmul(
                            pb[bs_][:, :], ksl_[sl_][:, kc_ * 128:(kc_ + 1) * 128],
                            qsl_[sl_][:, iq_ * 512:(iq_ + 1) * 512],
                            start=True, stop=True), kc_=kc, iq_=iq, bs_=bs, sl_=sl),
                        waits=[(s_u[sl], t_u)])
                    ev_e = p.op("scalar", partial(
                        lambda s, bs_=0, psl_=0: s.activation(
                            pT[psl_][:], pb[bs_][:, :], AF.Exp, bias=bias2_col[:, 0:1],
                            scale=1.0 / math.sqrt(128.0)), bs_=bs, psl_=psl),
                        waits=[(s_mm, tok_s), (s_v, bias2_done), (s_mm, pconsumed[psl])], inc=s_sc)
                    bank_consumed(bs, s_sc, ev_e)
                    rsem, rval = s_sc, ev_e
                    j = kc - 4 * iq
                    if j >= 0:
                        p.op("vector", partial(
                            lambda v, psl_=0, j_=0: v.tensor_tensor(
                                pT[psl_][:], pT[psl_][:], dmask[:, j_ * 512:(j_ + 1) * 512], ALU.mult),
                            psl_=psl, j_=j),
                            waits=[(s_sc, ev_e), (s_const, CONSTS)], inc=s_v)
                        p.op("vector", lambda v: v.drain(), inc=s_v)
                        rsem, rval = s_v, s_v.n
                    p2_state[kc] = (rsem, rval)

                def p2_tail(kc):
                    psl = kc % 4
                    wl = [p2_state.pop(kc)]
                    if kc == 0 and bank_rsem[6] is not None:
                        wl.append((bank_rsem[6], bank_reader[6]))
                    p.op("tensor", partial(
                        lambda t, kc_=0, last_=False: t.matmul(
                            pb[6][0:1, :], onescol[:], pT[kc_ % 4][:],
                            start=(kc_ == 0), stop=last_), kc_=kc, last_=(kc == nkc - 1)),
                        waits=wl, inc=(s_mm if kc == nkc - 1 else None))
                    if kc == nkc - 1:
                        bank_writer[6] = s_mm.n
                    wo_ = []
                    if kc == 0 and bank_rsem[bo] is not None:
                        wo_.append((bank_rsem[bo], bank_reader[bo]))
                    p.op("tensor", partial(
                        lambda t, kc_=0, bo_=0, last_=False, sl_=0: t.matmul(
                            pb[bo_][:, :], vsl_[sl_][:, kc_ * 128:(kc_ + 1) * 128], pT[kc_ % 4][:],
                            start=(kc_ == 0), stop=last_), kc_=kc, bo_=bo, last_=(kc == nkc - 1), sl_=sl),
                        waits=wo_, inc=s_mm)
                    if kc == nkc - 1:
                        bank_writer[bo] = s_mm.n
                    pconsumed[psl] = s_mm.n
                for kc in range(nkc):
                    p2_head(kc)
                    if kc >= 1:
                        p2_tail(kc - 1)
                p2_tail(nkc - 1)
                otok = s_mm.n
                p.op("vector", lambda v: v.tensor_scalar_add(rl_sb[0:1, :], pb[6][0:1, :], 1e-30),
                     waits=[(s_mm, otok), (s_dbg, dbg_tok)], inc=s_v)
                ev_l = p.op("vector", lambda v: v.reciprocal(rl_sb[0:1, :], rl_sb[0:1, :]), inc=s_v)
                bank_consumed(6, s_v, ev_l)
                p.op("vector", lambda v: v.drain(), inc=s_v)
                l_done = s_v.n
                p.op("scalar", lambda s: s.copy(rl_bf[0:1, :], rl_sb[0:1, :]),
                     waits=[(s_v, l_done)], inc=s_sc)
                rlb_done = s_sc.n
                br = 2 + iq % 2
                tok_br = mm_one(br, partial(
                    lambda t, br_=0: t.matmul(pb[br_][:, :], ones1[:], rl_bf[0:1, :],
                                              start=True, stop=True), br_=br),
                    waits=[(s_sc, rlb_done), (s_const, CONSTS)])
                ev_rb = p.op("scalar", partial(
                    lambda s, br_=0: s.copy(rlb_sb[:], pb[br_][:, :]), br_=br),
                    waits=[(s_mm, tok_br)], inc=s_sc)
                bank_consumed(br, s_sc, ev_rb)
                osl = iq % 2
                ev_on = p.op("vector", partial(
                    lambda v, bo_=0, osl_=0: v.tensor_tensor(
                        ostD[osl_][:], pb[bo_][:, :], rlb_sb[:], ALU.mult), bo_=bo, osl_=osl),
                    waits=[(s_mm, otok), (s_sc, ev_rb), (s_od[osl], ostD_tok[osl])], inc=s_v)
                bank_consumed(bo, s_v, ev_on)
                p.op("vector", lambda v: v.drain(), inc=s_v)
                ostD_tok[osl] = p.dma("sync", o1T_o[u * 128:(u + 1) * 128, iq * 512:(iq + 1) * 512],
                                     ostD[osl][:], waits=[(s_v, s_v.n)], inc=s_od[osl])
            unit_free[sl] = s_mm.n

        p.op("sync", lambda sy: sy.nop(), waits=[(s_od[0], s_od[0].n), (s_od[1], s_od[1].n)])

        barrier("D")
        ccD, nD = collective([(o1T_o, Go1)], "D")
        CC_D = (ccD, nD)
        ones1 = ONES1; bout_sb = ROWBUF
        ow = XNW
        wbuf = WBUF
        xp = XQ
        accf = FACC
        pb = PB

        s_const = p.sem("const")
        s_constg = p.sem("constg")
        s_wb = [p.sem(f"wb{i}") for i in range(2)]
        s_xw = p.sem("xw")
        s_mm = p.sem("mm")
        s_v = p.sem("v")
        s_sc = p.sem("sc")
        s_g = p.sem("g")
        s_ld = [p.sem(f"ld{i}") for i in range(2)]
        s_ow = p.sem("ow")
        s_outd = [p.sem(f"outd{i}") for i in range(2)]
        p.auto = {"vector": s_v, "scalar": s_sc, "gpsimd": s_g}

        bank_writer = [0] * 8
        bank_reader = [0] * 8
        bank_rsem = [None] * 8

        def mm_group(bank_i, mms, waits_first=()):
            w = list(waits_first)
            if bank_rsem[bank_i] is not None:
                w.append((bank_rsem[bank_i], bank_reader[bank_i]))
            for i, fn in enumerate(mms):
                inc = s_mm if i == len(mms) - 1 else None
                p.op("tensor", fn, waits=w if i == 0 else (), inc=inc)
            bank_writer[bank_i] = s_mm.n
            return s_mm.n

        def bank_consumed(bank_i, sem, val):
            bank_rsem[bank_i] = sem
            bank_reader[bank_i] = val

        p.dma("sync", ones1[:], ones1_d[:, :], inc=s_const)
        p.dma("gpsimd", bout_sb[:], bout_row[:, :], inc=s_constg)
        CONSTS = s_const.n
        CONSTG = s_constg.n

        p.op("sync", lambda sy: sy.nop(), waits=[(CC_D[0], CC_D[1])])
        out_tok = [0, 0]
        accf_tok = 0
        for tsl in range(2):
            t0 = tsl * 1024
            t_ow = p.op("sync", partial(
                lambda sy, tsl_=0: sy.dma_start(
                    out=ow[:].rearrange("p (k t) -> p k t", k=16),
                    in_=Go1.rearrange("(k p) (r t w) -> t p k r w", p=128, r=2, t=2)
                    [tsl_, :, :, bass.ts(SELRV(), 1), :]
                    .rearrange("p k 1 w -> p k w")), tsl_=tsl),
                waits=[(s_mm, s_mm.n)], inc=s_ow, inc_amt=16)
            for dq in range(4):
                tw0 = p.dma("gpsimd", wbuf[0][:].rearrange("p (k e) -> p k e", k=8),
                            woutT[0:1024, dq * 512:(dq + 1) * 512].rearrange("(k p) e -> p k e", p=128),
                            waits=[(s_mm, s_mm.n)], inc=s_wb[0])
                tw1 = p.dma("gpsimd", wbuf[1][:].rearrange("p (k e) -> p k e", k=8),
                            woutT[1024:2048, dq * 512:(dq + 1) * 512].rearrange("(k p) e -> p k e", p=128),
                            inc=s_wb[1])
                for tc in range(8):
                    bi = tc % 4
                    mms = []
                    for k in range(16):
                        mms.append(partial(
                            lambda t, k_=0, tc_=0, bi_=0: t.matmul(
                                pb[bi_][:, :],
                                ow[:, k_ * 1024 + tc_ * 128: k_ * 1024 + (tc_ + 1) * 128],
                                wbuf[k_ // 8][:, (k_ % 8) * 512:(k_ % 8 + 1) * 512],
                                start=(k_ == 0), stop=False),
                            k_=k, tc_=tc, bi_=bi))
                    mms.append(partial(
                        lambda t, bi_=0, dq_=0: t.matmul(
                            pb[bi_][:, :], ones1[:], bout_sb[0:1, dq_ * 512:(dq_ + 1) * 512],
                            start=False, stop=True), bi_=bi, dq_=dq))
                    tok = mm_group(bi, mms, waits_first=[(s_wb[0], tw0), (s_wb[1], tw1), (s_ow, t_ow),
                                                         (s_const, CONSTS), (s_constg, CONSTG)])
                    fo = tc * D + dq * 512
                    ev = p.op("vector", partial(
                        lambda v, bi_=0, fo_=0: v.tensor_copy(accf[:, fo_:fo_ + 512], pb[bi_][:, :]),
                        bi_=bi, fo_=fo), waits=[(s_mm, tok), (s_outd[0], accf_tok)], inc=s_v)
                    bank_consumed(bi, s_v, ev)
            for tc in range(8):
                xsl = tc % 2
                t_x = p.dma("sync", xp[xsl][:], x1pf[t0 + tc * 128: t0 + (tc + 1) * 128, :],
                            waits=[(s_v, s_v.n)], inc=s_ld[xsl])
                p.op("vector", partial(
                    lambda v, tc_=0, xsl_=0: v.tensor_tensor(
                        accf[:, tc_ * D:(tc_ + 1) * D], accf[:, tc_ * D:(tc_ + 1) * D],
                        xp[xsl_][:], ALU.add), tc_=tc, xsl_=xsl),
                    waits=[(s_ld[xsl], t_x)], inc=s_v)
                p.op("vector", lambda v: v.drain(), inc=s_v)
                p.dma("sync", out_o[t0 + tc * 128: t0 + (tc + 1) * 128, :],
                      accf[:, tc * D:(tc + 1) * D], waits=[(s_v, s_v.n)], inc=s_outd[0])
            accf_tok = s_outd[0].n

        p.op("sync", lambda sy: sy.nop(),
             waits=[(s_outd[0], s_outd[0].n), (s_outd[1], s_outd[1].n)])


        barrier("E")

    p.emit()
    return nc, p.n_ops()


def prep_a_inputs(inp, shard):
    c = _consts_np()
    f32 = np.float32
    return dict(
        x=np.ascontiguousarray(shard).astype(f32, copy=False),
        wq=inp["L0_attn_Wq"], wk=inp["L0_attn_Wk"],
        wkg1=inp["L0_attn_Wkg1"], wkg2=inp["L0_attn_Wkg2"],
        bkg2c=np.ascontiguousarray(inp["L0_attn_bkg2"].reshape(8, 128).T),
        wv=inp["L0_attn_Wv"], wgg=inp["L0_attn_Wg"],
        bg_row=np.ascontiguousarray(inp["L0_attn_bg"].reshape(1, D)),
        gate0=np.ascontiguousarray(inp["L0_ffn_gate_w"].T),
        swg=inp["L0_ffn_sWg"], swu=inp["L0_ffn_sWu"], swd=inp["L0_ffn_sWd"],
        ewg=inp["L0_ffn_Wg"], ewu=inp["L0_ffn_Wu"], ewd=inp["L0_ffn_Wd"],
        ident=c["ident"], ones1=c["ones1"], ones_tile=c["ones_tile"],
        constf=c["constf"],
    )


def prep_b_inputs(inp, couts, core):
    c = _consts_np()
    b = core // 2
    heads = [(core % 2) * 4 + i for i in range(4)]
    cA, cB2 = 2 * b, 2 * b + 1
    qT = [np.concatenate([couts[cA]["qT"][h * 128:(h + 1) * 128],
                          couts[cB2]["qT"][h * 128:(h + 1) * 128]], axis=1) for h in heads]
    kT = [np.concatenate([couts[cA]["kT"][h * 128:(h + 1) * 128],
                          couts[cB2]["kT"][h * 128:(h + 1) * 128]], axis=1) for h in heads]
    gkT = [np.concatenate([couts[cA]["gkT"][h * 128:(h + 1) * 128],
                           couts[cB2]["gkT"][h * 128:(h + 1) * 128]], axis=1) for h in heads]
    vt = [np.concatenate([couts[cA]["v_tok"][:, h * 256:(h + 1) * 256],
                          couts[cB2]["v_tok"][:, h * 256:(h + 1) * 256]], axis=0) for h in heads]
    return dict(
        qTu=np.ascontiguousarray(np.concatenate(qT, axis=0)),
        kTu=np.ascontiguousarray(np.concatenate(kT, axis=0)),
        gkTu=np.ascontiguousarray(np.concatenate(gkT, axis=0)),
        vtu=np.ascontiguousarray(np.concatenate(vt, axis=1)),
        ident=c["ident"], ut01=c["ut01"], constf=c["constf"],
    )


def prep_c_inputs(inp, shard, aouts, bouts, core):
    c = _consts_np()
    b = core // 2
    hh = core % 2
    tsl = slice(hh * 2048, hh * 2048 + 2048)
    o = np.empty((2048, 2048), dtype=BF)
    for h in range(8):
        bcore = 2 * b + h // 4
        i = h % 4
        o[:, h * 256:(h + 1) * 256] = bouts[bcore]["o_tok"][tsl, i * 256:(i + 1) * 256]
    bqkv = inp["L1_attn_b_qkv"]
    return dict(
        x=np.ascontiguousarray(shard),
        f0=aouts[core]["f0"],
        o_tok=o,
        g_tok=aouts[core]["g_tok"],
        wo=inp["L0_attn_Wo"],
        wqkvT=np.ascontiguousarray(inp["L1_attn_w_qkv"].T),
        bqkv_c=np.ascontiguousarray(np.concatenate(
            [bqkv[0:2048].reshape(16, 128).T, bqkv[2048:4096].reshape(16, 128).T], axis=1)),
        bv_row=np.ascontiguousarray(bqkv[4096:6144].reshape(1, D)),
        gate1=inp["L1_ffn_gate_w"],
        upw=inp["L1_ffn_up_w"],
        downw=inp["L1_ffn_down_w"],
        expw=inp["L1_ffn_expert_weight"],
        gam_row=np.ascontiguousarray(inp["L1_ffn_ln_gamma"].reshape(1, D)),
        bet_row=np.ascontiguousarray(inp["L1_ffn_ln_beta"].reshape(1, D)),
        ident=c["ident"], ones1=c["ones1"], constf=c["constf"],
    )


def _dmask_np():
    dmask = np.zeros((128, 4 * 512), dtype=BF)
    for j in range(4):
        for kt in range(128):
            dmask[kt, j * 512 + 128 * j + kt:(j + 1) * 512] = 1.0
    return dmask


def prep_d_inputs(couts, core):
    c = _consts_np()
    b = core // 2
    heads = [(core % 2) * 8 + i for i in range(8)]
    cA, cB2 = 2 * b, 2 * b + 1
    qT = [np.concatenate([couts[cA]["q1T"][h * 128:(h + 1) * 128],
                          couts[cB2]["q1T"][h * 128:(h + 1) * 128]], axis=1) for h in heads]
    kT = [np.concatenate([couts[cA]["k1T"][h * 128:(h + 1) * 128],
                          couts[cB2]["k1T"][h * 128:(h + 1) * 128]], axis=1) for h in heads]
    vt = [np.concatenate([couts[cA]["v1_tok"][:, h * 128:(h + 1) * 128],
                          couts[cB2]["v1_tok"][:, h * 128:(h + 1) * 128]], axis=0) for h in heads]
    return dict(
        qTu=np.ascontiguousarray(np.concatenate(qT, axis=0)),
        kTu=np.ascontiguousarray(np.concatenate(kT, axis=0)),
        vtu=np.ascontiguousarray(np.concatenate(vt, axis=1)),
        ones1=c["ones1"],
        onescol=np.ones((128, 1), dtype=BF),
        dmask=_dmask_np(),
    )


def prep_e_inputs(inp, couts, douts, core):
    c = _consts_np()
    b = core // 2
    hh = core % 2
    tsl = slice(hh * 2048, hh * 2048 + 2048)
    o1T = np.empty((2048, 2048), dtype=BF)
    for h in range(16):
        dcore = 2 * b + h // 8
        i = h % 8
        o1T[h * 128:(h + 1) * 128, :] = douts[dcore]["o1T"][i * 128:(i + 1) * 128, tsl]
    return dict(
        o1T=o1T,
        woutT=np.ascontiguousarray(inp["L1_attn_w_out"].T),
        bout_row=np.ascontiguousarray(inp["L1_attn_b_out"].reshape(1, D)),
        x1pf=couts[core]["x1pf"],
        ones1=c["ones1"],
    )


def prep_fused_inputs(inp, shard, core):
    c = _consts_np()
    bqkv = inp["L1_attn_b_qkv"]
    return dict(
        x=np.ascontiguousarray(shard).astype(np.float32, copy=False),
        wq=inp["L0_attn_Wq"], wk=inp["L0_attn_Wk"],
        wkg1=inp["L0_attn_Wkg1"], wkg2=inp["L0_attn_Wkg2"],
        bkg2c=np.ascontiguousarray(inp["L0_attn_bkg2"].reshape(8, 128).T),
        wv=inp["L0_attn_Wv"], wgg=inp["L0_attn_Wg"],
        bg_row=np.ascontiguousarray(inp["L0_attn_bg"].reshape(1, D)),
        gate0=np.ascontiguousarray(inp["L0_ffn_gate_w"].T),
        swg=inp["L0_ffn_sWg"], swu=inp["L0_ffn_sWu"], swd=inp["L0_ffn_sWd"],
        ewg=inp["L0_ffn_Wg"], ewu=inp["L0_ffn_Wu"], ewd=inp["L0_ffn_Wd"],
        wo=inp["L0_attn_Wo"],
        wqkvT=np.ascontiguousarray(inp["L1_attn_w_qkv"].T),
        bqkv_c=np.ascontiguousarray(np.concatenate(
            [bqkv[0:2048].reshape(16, 128).T, bqkv[2048:4096].reshape(16, 128).T], axis=1)),
        bv_row=np.ascontiguousarray(bqkv[4096:6144].reshape(1, D)),
        gate1=inp["L1_ffn_gate_w"],
        upw=inp["L1_ffn_up_w"],
        downw=inp["L1_ffn_down_w"],
        expw=inp["L1_ffn_expert_weight"],
        gam_row=np.ascontiguousarray(inp["L1_ffn_ln_gamma"].reshape(1, D)),
        bet_row=np.ascontiguousarray(inp["L1_ffn_ln_beta"].reshape(1, D)),
        woutT=np.ascontiguousarray(inp["L1_attn_w_out"].T),
        bout_row=np.ascontiguousarray(inp["L1_attn_b_out"].reshape(1, D)),
        ident=c["ident"], ones1=c["ones1"], ones_tile=c["ones_tile"],
        onescol=np.ones((128, 1), dtype=BF), ut01=c["ut01"],
        dmask=_dmask_np(), constf=c["constf"],
        selv=np.array([[core % 2]], dtype=np.uint32),
    )


def kernel(**inputs):
    inp = {k: np.asarray(v) for k, v in inputs.items()}
    x_full = np.ascontiguousarray(inp["L0_attn_x"].reshape(16384, D).astype(np.float32, copy=False))
    cores = list(range(NCORES))
    ncF, _ = build_fused()
    res = run_bass_kernel_spmd(
        ncF, [prep_fused_inputs(inp, x_full[c * TC:(c + 1) * TC], c) for c in cores],
        core_ids=cores)
    out = np.concatenate([np.asarray(res.results[c]["out"]) for c in cores], axis=0)
    return out.reshape(B, S, D).astype(np.float32, copy=False)


def kernel_5launch(**inputs):
    inp = {k: np.asarray(v) for k, v in inputs.items()}
    x_full = np.ascontiguousarray(inp["L0_attn_x"].reshape(16384, D).astype(np.float32, copy=False))
    cores = list(range(NCORES))

    ncA, _ = build_phase_a()
    resA = run_bass_kernel_spmd(
        ncA, [prep_a_inputs(inp, x_full[c * TC:(c + 1) * TC]) for c in cores], core_ids=cores)
    aouts = resA.results

    ncB, _ = build_phase_b()
    resB = run_bass_kernel_spmd(
        ncB, [prep_b_inputs(inp, aouts, c) for c in cores], core_ids=cores)
    bouts = resB.results

    ncC, _ = build_phase_c()
    resC = run_bass_kernel_spmd(
        ncC, [prep_c_inputs(inp, x_full[c * TC:(c + 1) * TC], aouts, bouts, c) for c in cores],
        core_ids=cores)
    couts = resC.results

    ncD, _ = build_phase_d()
    resD = run_bass_kernel_spmd(
        ncD, [prep_d_inputs(couts, c) for c in cores], core_ids=cores)
    douts = resD.results

    ncE, _ = build_phase_e()
    resE = run_bass_kernel_spmd(
        ncE, [prep_e_inputs(inp, couts, douts, c) for c in cores], core_ids=cores)

    out = np.concatenate([np.asarray(resE.results[c]["out"]) for c in cores], axis=0)
    return out.reshape(B, S, D).astype(np.float32, copy=False)
